# version 11
# baseline (speedup 1.0000x reference)
import numpy as np
import ml_dtypes
from contextlib import ExitStack
import concourse.bass as bass
import concourse.mybir as mybir
from concourse.bass_utils import run_bass_kernel_spmd
from concourse.aot_env import allow_ici_all_to_all

F32 = mybir.dt.float32
BF16 = mybir.dt.bfloat16
I8 = mybir.dt.int8
AF = mybir.ActivationFunctionType
ALU = mybir.AluOpType
AX = mybir.AxisListType
BFNP = ml_dtypes.bfloat16

NCORE = 8
TPC = 1024
NTT = TPC // 128
D = 2048
NEC = D // 128
NEG = 4
INTER = 8192
NIC = INTER // 128
NHEAD = 16
HD = 128
SEQ = 4096
NRANK = 4
INV_SQRT_HD = 1.0 / float(np.sqrt(np.float32(HD)))
L1_DENOM = float(np.float32(np.sqrt(np.float32(D))) / np.float32(SEQ))
LN_EPS = 1e-5



def _tile_b(w):
    K, M = w.shape
    return np.ascontiguousarray(w.reshape(K // 128, 128, M // 128, 128).transpose(2, 0, 1, 3))


def _tile_a(w):
    K, M = w.shape
    return np.ascontiguousarray(w.reshape(K // 128, 128, M // 512, 512).transpose(0, 2, 1, 3))


def _pp(vec):
    n = vec.shape[0] // 128
    return np.ascontiguousarray(vec.reshape(n, 128).T)


def _rope_tables(sl, scale):
    inv = (1.0 / (10000.0 ** (np.arange(0, HD, 2, dtype=np.float32) / np.float32(HD)))).astype(np.float32)
    pos = np.arange(sl * TPC, (sl + 1) * TPC, dtype=np.float32)
    fr = np.outer(pos, inv).astype(np.float32)
    cos = (np.cos(fr) * scale).astype(np.float32)
    sin = (np.sin(fr) * scale).astype(np.float32)
    rep = lambda t: np.ascontiguousarray(np.tile(t, (1, NHEAD)))
    return rep(cos), rep(sin)


def _stage(inputs):
    sh = {}
    sh["ident"] = np.eye(128, dtype=np.float32)
    T = lambda a: np.ascontiguousarray(np.asarray(a).T)
    sh["l0_wq_a"] = _tile_a(T(inputs["L0_attn_Wq"]))
    sh["l0_wk_a"] = _tile_a(T(inputs["L0_attn_Wk"]))
    sh["l0_wv_a"] = _tile_a(T(inputs["L0_attn_Wv"]))
    sh["l0_wo_b"] = _tile_b(T(inputs["L0_attn_Wo"]))
    sh["l0_bq_r"] = np.ascontiguousarray(np.asarray(inputs["L0_attn_bq"]).reshape(1, D))
    sh["l0_bk_r"] = np.ascontiguousarray(np.asarray(inputs["L0_attn_bk"]).reshape(1, D))
    sh["l0_bv_r"] = np.ascontiguousarray(np.asarray(inputs["L0_attn_bv"]).reshape(1, D))
    sh["l0_bo_pp"] = _pp(np.asarray(inputs["L0_attn_bo"]))
    for l in (0, 1):
        sh[f"f{l}_wg"] = np.ascontiguousarray(
            _tile_b(np.asarray(inputs[f"L{l}_ffn_w_gate"])).reshape(NIC // 2, 2 * NEC, 128, 128))
        sh[f"f{l}_wu"] = np.ascontiguousarray(
            _tile_b(np.asarray(inputs[f"L{l}_ffn_w_up"])).reshape(NIC // 2, 2 * NEC, 128, 128))
        sh[f"f{l}_wd"] = _tile_b(np.asarray(inputs[f"L{l}_ffn_w_down"]))
        sh[f"f{l}_sg"] = _pp(np.asarray(inputs[f"L{l}_ffn_s_gate"]))
        sh[f"f{l}_su"] = _pp(np.asarray(inputs[f"L{l}_ffn_s_up"]))
        sh[f"f{l}_sd"] = _pp(np.asarray(inputs[f"L{l}_ffn_s_down"]))
    sh["l1_wq_b"] = _tile_b(T(inputs["L1_attn_Wq"]))
    sh["l1_wk_a"] = _tile_a(T(inputs["L1_attn_Wk"]))
    sh["l1_wv_a"] = _tile_a(T(inputs["L1_attn_Wv"]))
    sh["l1_wo_b"] = _tile_b(T(inputs["L1_attn_Wo"]))

    x = np.asarray(inputs["L0_attn_x"])
    percore = []
    for c in range(NCORE):
        b, sl = divmod(c, NRANK)
        pc = {"x": np.ascontiguousarray(x[b, sl * TPC:(sl + 1) * TPC, :])}
        pc["ropecq"], pc["ropesq"] = _rope_tables(sl, INV_SQRT_HD)
        pc["ropeck"], pc["ropesk"] = _rope_tables(sl, 1.0)
        pc["mask_pp"] = np.ascontiguousarray(
            np.broadcast_to((np.arange(NRANK) < sl).astype(np.float32)[None, :], (128, NRANK)).copy())
        sel = np.array([1.0, 0.0] if b == 0 else [0.0, 1.0], dtype=np.float32)
        pc["sel_pp"] = np.ascontiguousarray(np.broadcast_to(sel[None, :], (128, 2)).copy())
        percore.append(pc)
    return sh, percore



class P:

    def __init__(self):
        self.nc = bass.Bass("TRN2", target_bir_lowering=False, debug=False)
        self.ctx = ExitStack()
        self.h = {}
        self.c = {}
        for n in ("pe", "vc", "sc", "pl", "ld", "lwa", "lwb", "lwc", "lwd", "sd",
                  "cc1", "cc2", "fsa", "fsb", "fsc", "fsd"):
            self.h[n] = self.ctx.enter_context(self.nc.semaphore(n))
            self.c[n] = 0

    _n = 0

    def uname(self, base):
        P._n += 1
        return f"{base}_{P._n}"

    def sb(self, name, shape, dtype):
        return self.ctx.enter_context(self.nc.sbuf_tensor(name, shape, dtype))

    def dram_in(self, name, shape, dtype):
        return self.nc.dram_tensor(name, list(shape), dtype, kind="ExternalInput").ap()

    def dram_out(self, name, shape, dtype):
        return self.nc.dram_tensor(name, list(shape), dtype, kind="ExternalOutput").ap()


def _consts(p):
    k = {
        "ident_b": p.sb("ident_b", [128, 128], BF16),
        "ident_f": p.sb("ident_f", [128, 128], F32),
        "ones_row_b": p.sb("ones_row_b", [1, 128], BF16),
        "ones_row_f": p.sb("ones_row_f", [1, 128], F32),
        "ones_col_b": p.sb("ones_col_b", [128, 1], BF16),
        "eps11": p.sb("eps11", [1, 1], F32),
    }
    return k


def _emit_init(p, k, io):
    ld0 = p.c["ld"]
    with p.nc.Block() as blk:
        @blk.gpsimd
        def _(g):
            g.dma_start(out=k["ident_b"][:, :], in_=io["ident"]).then_inc(p.h["ld"], 16)
            g.dma_start(out=k["ident_f"][:, :], in_=io["ident"]).then_inc(p.h["ld"], 16)
            g.memset(k["ones_row_b"][:, :], 1.0)
            g.memset(k["ones_row_f"][:, :], 1.0)
            g.memset(k["eps11"][:, :], LN_EPS)
            g.memset(k["ones_col_b"][:, :], 1.0).then_inc(p.h["pl"], 1)
            g.wait_ge(p.h["ld"], ld0 + 32)
    p.c["ld"] = ld0 + 32
    p.c["pl"] += 1


def _emit_load_stream(p, k, x_ap, stream):
    nc = p.nc
    with ExitStack() as ctx:
        E = ctx.enter_context
        xr = E(nc.sbuf_tensor(p.uname("ls_xr"), [128, D], BF16))
        trp = [E(nc.psum_tensor(p.uname(f"ls_tr{i}"), [128, 1024], BF16)) for i in range(4)]
        for tc in range(NTT):
            ld0, pe0, vc0 = p.c["ld"], p.c["pe"], p.c["vc"]
            with nc.Block() as blk:
                @blk.gpsimd
                def _(g, tc=tc):
                    g.dma_start(out=xr[:, :], in_=x_ap[tc * 128:(tc + 1) * 128, :]).then_inc(p.h["ld"], 16)

                @blk.tensor
                def _(t, tc=tc, ld0=ld0, vc0=vc0):
                    t.wait_ge(p.h["ld"], ld0 + 16)
                    t.wait_ge(p.h["pl"], 1)
                    for ec in range(NEC):
                        if ec >= 4:
                            t.wait_ge(p.h["vc"], vc0 + ec - 3)
                        t.transpose(trp[ec % 4][:, 0:128], xr[:, ec * 128:(ec + 1) * 128],
                                    k["ident_b"][:, :]).then_inc(p.h["pe"], 1)

                @blk.vector
                def _(v, tc=tc, pe0=pe0):
                    for ec in range(NEC):
                        v.wait_ge(p.h["pe"], pe0 + ec + 1)
                        v.tensor_copy(stream[:, ec * TPC + tc * 128: ec * TPC + (tc + 1) * 128],
                                      trp[ec % 4][:, 0:128]).then_inc(p.h["vc"], 1)
            p.c["ld"] = ld0 + 16
            p.c["pe"] = pe0 + NEC
            p.c["vc"] = vc0 + NEC


def _emit_qkv(p, k, io, stream, outs, wnames, biases, rope, l1=False):
    nc = p.nc
    names = list(outs.keys())
    with ExitStack() as ctx:
        E = ctx.enter_context
        wrhs = [E(nc.sbuf_tensor(p.uname(f"qk_w{i}"), [128, NEC * 512], BF16)) for i in range(2)]
        bias_r = E(nc.sbuf_tensor(p.uname("qk_bias"), [1, D], BF16))
        if rope:
            tab_c = E(nc.sbuf_tensor(p.uname("qk_tc"), [128, NTT * 256], F32))
            tab_s = E(nc.sbuf_tensor(p.uname("qk_ts"), [128, NTT * 256], F32))
            qtmp = [E(nc.sbuf_tensor(p.uname(f"qk_t{i}"), [128, 512], F32)) for i in range(4)]
            m1 = E(nc.sbuf_tensor(p.uname("qk_m1"), [128, 256], F32))
            m2 = E(nc.sbuf_tensor(p.uname("qk_m2"), [128, 256], F32))
            m3 = E(nc.sbuf_tensor(p.uname("qk_m3"), [128, 256], F32))
            m4 = E(nc.sbuf_tensor(p.uname("qk_m4"), [128, 256], F32))
        acc = [E(nc.psum_tensor(p.uname(f"qk_ps{i}"), [128, 512], F32)) for i in range(4)]

        lwsem = ("lwa", "lwb")
        with nc.Block() as blk:
            @blk.gpsimd
            def _(g):
                g.dma_start(out=wrhs[0][:, :].rearrange("p (d e) -> p d e", d=NEC),
                            in_=io[wnames[names[0]]][:, 0, :, :].rearrange("d p e -> p d e")).then_inc(p.h["lwa"], 16)
        p.c["lwa"] += 16

        for ti, tname in enumerate(names):
            do_rope = tname in rope
            has_bias = biases.get(tname) is not None
            dst = outs[tname]
            if do_rope:
                ctab = io["ropecq"] if tname == "q" else io["ropeck"]
                stab = io["ropesq"] if tname == "q" else io["ropesk"]
            for eg in range(NEG):
                wslot = (ti * NEG + eg) % 2
                ld0, pe0, vc0, pl0 = (p.c["ld"], p.c["pe"], p.c["vc"], p.c["pl"])
                wsem = lwsem[wslot]
                lw_need = p.c[wsem]
                n_tab = 2 if do_rope else 0
                has_pref = not (ti == len(names) - 1 and eg == NEG - 1)
                sd0 = p.c["sd"]
                load_bias = has_bias and eg == 0
                with nc.Block() as blk:
                    @blk.gpsimd
                    def _(g, ti=ti, eg=eg, wslot=wslot, has_bias=has_bias,
                          tname=tname, has_pref=has_pref, load_bias=load_bias):
                        if load_bias:
                            g.dma_start(out=bias_r[:, :], in_=io[biases[tname]]).then_inc(p.h["ld"], 16)
                        if has_pref:
                            nt, ne = (ti, eg + 1) if eg + 1 < NEG else (ti + 1, 0)
                            g.dma_start(
                                out=wrhs[1 - wslot][:, :].rearrange("p (d e) -> p d e", d=NEC),
                                in_=io[wnames[names[nt]]][:, ne, :, :].rearrange("d p e -> p d e")).then_inc(p.h[lwsem[1 - wslot]], 16)

                    if do_rope:
                        @blk.sync
                        def _(sy, ti=ti, eg=eg, tname=tname):
                            sy.dma_start(
                                out=tab_c[:, :].rearrange("p (t e) -> p t e", t=NTT),
                                in_=(io["ropecq"] if tname == "q" else io["ropeck"])
                                .rearrange("(t p) e -> p t e", p=128)[:, :, eg * 256:(eg + 1) * 256],
                            ).then_inc(p.h["sd"], 16)
                            sy.dma_start(
                                out=tab_s[:, :].rearrange("p (t e) -> p t e", t=NTT),
                                in_=(io["ropesq"] if tname == "q" else io["ropesk"])
                                .rearrange("(t p) e -> p t e", p=128)[:, :, eg * 256:(eg + 1) * 256],
                            ).then_inc(p.h["sd"], 16)
                    n_dma = 1 if load_bias else 0
                    ld_data = ld0 + 16 * n_dma
                    sd_data = sd0 + (32 if do_rope else 0)

                    v_rope = 3 if do_rope else 0
                    v_per_sweep = 4 + 4 * v_rope if do_rope else 4
                    v_total = 2 * v_per_sweep

                    @blk.tensor
                    def _(t, eg=eg, wslot=wslot, vc0=vc0, ld_data=ld_data, lw_need=lw_need,
                          has_bias=has_bias, wsem=wsem):
                        t.wait_ge(p.h["ld"], ld_data)
                        t.wait_ge(p.h[wsem], lw_need)
                        for sw in range(2):
                            if sw == 1:
                                t.wait_ge(p.h["vc"], vc0 + 4)
                            for tt in range(4):
                                tc = sw * 4 + tt
                                for dc in range(NEC):
                                    mm = t.matmul(acc[tt][:, :],
                                                  stream[:, dc * TPC + tc * 128:dc * TPC + (tc + 1) * 128],
                                                  wrhs[wslot][:, dc * 512:(dc + 1) * 512],
                                                  start=(dc == 0),
                                                  stop=(not has_bias) and dc == NEC - 1)
                                    if (not has_bias) and dc == NEC - 1:
                                        mm.then_inc(p.h["pe"], 1)
                                if has_bias:
                                    t.matmul(acc[tt][:, :], k["ones_row_b"][:, :],
                                             bias_r[:, eg * 512:(eg + 1) * 512],
                                             start=False, stop=True).then_inc(p.h["pe"], 1)

                    if do_rope:
                        @blk.vector
                        def _(v, eg=eg, pe0=pe0, pl0=pl0, vc0=vc0, ld_data=ld_data, dst=dst,
                              sd_data=sd_data):
                            v.wait_ge(p.h["ld"], ld_data)
                            v.wait_ge(p.h["sd"], sd_data)
                            vcc = vc0
                            for sw in range(2):
                                for tt in range(4):
                                    v.wait_ge(p.h["pe"], pe0 + sw * 4 + tt + 1)
                                    if sw == 1:
                                        v.wait_ge(p.h["pl"], pl0 + tt * 3 + 2)
                                        v.wait_ge(p.h["vc"], vcc)
                                    v.tensor_copy(qtmp[tt][:, :], acc[tt][:, :]).then_inc(p.h["vc"], 1)
                                    vcc += 1
                                for tt in range(4):
                                    tc = sw * 4 + tt
                                    ts = qtmp[tt]
                                    x0 = ts[:, 0:512:2]
                                    x1 = ts[:, 1:512:2]
                                    tcs = tab_c[:, tc * 256:(tc + 1) * 256]
                                    plb = pl0 + sw * 12 + tt * 3
                                    dste = dst[:, tc * D + eg * 512 + 0: tc * D + (eg + 1) * 512: 2]
                                    v.wait_ge(p.h["vc"], vcc)
                                    v.tensor_tensor(m1[:, :], x0, tcs, op=ALU.mult).then_inc(p.h["vc"], 1)
                                    vcc += 1
                                    v.wait_ge(p.h["pl"], plb + 1)
                                    v.wait_ge(p.h["vc"], vcc)
                                    v.tensor_tensor(dste, m1[:, :], m2[:, :],
                                                    op=ALU.subtract).then_inc(p.h["vc"], 1)
                                    vcc += 1
                                    v.tensor_tensor(m4[:, :], x1, tcs, op=ALU.mult).then_inc(p.h["vc"], 1)
                                    vcc += 1

                        @blk.gpsimd
                        def _(g, eg=eg, pl0=pl0, vc0=vc0, ld_data=ld_data, dst=dst,
                              v_per_sweep=v_per_sweep, sd_data=sd_data):
                            g.wait_ge(p.h["ld"], ld_data)
                            g.wait_ge(p.h["sd"], sd_data)
                            plc = pl0
                            for sw in range(2):
                                for tt in range(4):
                                    tc = sw * 4 + tt
                                    ts = qtmp[tt]
                                    x0 = ts[:, 0:512:2]
                                    x1 = ts[:, 1:512:2]
                                    tss = tab_s[:, tc * 256:(tc + 1) * 256]
                                    vb = vc0 + sw * v_per_sweep + 4 + tt * 3
                                    evb = vc0 + sw * v_per_sweep + tt + 1
                                    dsto = dst[:, tc * D + eg * 512 + 1: tc * D + (eg + 1) * 512: 2]
                                    g.wait_ge(p.h["vc"], evb)
                                    g.wait_ge(p.h["pl"], plc)
                                    g.tensor_tensor(m2[:, :], x1, tss, op=ALU.mult).then_inc(p.h["pl"], 1)
                                    plc += 1
                                    g.tensor_tensor(m3[:, :], x0, tss, op=ALU.mult).then_inc(p.h["pl"], 1)
                                    plc += 1
                                    g.wait_ge(p.h["vc"], vb + 3)
                                    g.wait_ge(p.h["pl"], plc)
                                    g.tensor_tensor(dsto, m3[:, :], m4[:, :],
                                                    op=ALU.add).then_inc(p.h["pl"], 1)
                                    plc += 1
                    else:
                        @blk.vector
                        def _(v, eg=eg, pe0=pe0, dst=dst):
                            for sw in range(2):
                                for tt in range(4):
                                    tc = sw * 4 + tt
                                    v.wait_ge(p.h["pe"], pe0 + sw * 4 + tt + 1)
                                    v.tensor_copy(dst[:, tc * D + eg * 512:tc * D + (eg + 1) * 512],
                                                  acc[tt][:, :]).then_inc(p.h["vc"], 1)

                p.c["ld"] = ld0 + 16 * n_dma
                p.c["sd"] = sd0 + (32 if do_rope else 0)
                if has_pref:
                    p.c[lwsem[1 - wslot]] += 16
                p.c["pe"] = pe0 + 8
                p.c["vc"] = vc0 + v_total
                p.c["pl"] = pl0 + (24 if do_rope else 0)


def _emit_l0_attn(p, k, io, bufs, cc=None):
    nc = p.nc
    q_tok, k_tok, v_tok, attn_fm = bufs["q_tok"], bufs["k_tok"], bufs["v_tok"], bufs["attn_fm"]
    NPV = 12
    with ExitStack() as ctx:
        E = ctx.enter_context
        prodS = [E(nc.sbuf_tensor(p.uname(f"at_ps{i}"), [128, 2048], BF16)) for i in range(2)]
        prodP = [E(nc.sbuf_tensor(p.uname(f"at_pp{i}"), [128, 2048], BF16)) for i in range(2)]
        prodV = E(nc.sbuf_tensor(p.uname("at_pv"), [128, 4 * 2048], BF16))
        sc = [E(nc.sbuf_tensor(p.uname(f"at_sc{i}"), [128, 256], F32)) for i in range(2)]
        ex = [E(nc.sbuf_tensor(p.uname(f"at_ex{i}"), [128, 256], F32)) for i in range(2)]
        ssum = [E(nc.sbuf_tensor(p.uname(f"at_ss{i}"), [128, 16], F32)) for i in range(2)]
        rcp = [E(nc.sbuf_tensor(p.uname(f"at_rc{i}"), [128, 16], F32)) for i in range(2)]
        pn = [E(nc.sbuf_tensor(p.uname(f"at_pn{i}"), [128, 256], F32)) for i in range(2)]
        otok = [E(nc.sbuf_tensor(p.uname(f"at_ot{i}"), [128, 2048], BF16)) for i in range(2)]
        trp = [E(nc.psum_tensor(p.uname(f"at_tr{i}"), [128, 1024], BF16)) for i in range(2)]

        pe0, vc0, pl0, sc0, ld0, cc0 = (p.c["pe"], p.c["vc"], p.c["pl"], p.c["sc"],
                                        p.c["ld"], p.c["cc1"])
        vbase = [vc0]
        for tc in range(NTT):
            vbase.append(vbase[-1] + 35 + (16 if tc >= 1 else 0) + 4)
        vfinal = vbase[NTT] + 16
        sbase = [sc0]
        for tc in range(NTT):
            sbase.append(sbase[-1] + 1 + (16 if tc >= 1 else 0))
        sfinal = sbase[NTT] + 16

        def pncnt(tc):
            return vbase[tc] + 35

        def redcnt(tcp, h):
            if tcp < NTT - 1:
                return vbase[tcp + 1] + 35 + h + 1
            return vbase[NTT] + h + 1

        def expcnt(tc):
            return sbase[tc] + 1

        def cpycnt(tcp, h):
            if tcp < NTT - 1:
                return sbase[tcp + 1] + 1 + h + 1
            return sbase[NTT] + h + 1

        with nc.Block() as blk:
            @blk.vector
            def _(v):
                vcc = vc0

                def emit_pvreds(tcp):
                    nonlocal vcc
                    ot = otok[tcp % 2]
                    if tcp >= 2:
                        v.wait_ge(p.h["pe"], pe0 + (tcp - 2) * 16 + 16)
                    for h in range(NHEAD):
                        if h < NPV:
                            v.wait_ge(p.h["pl"], pl0 + tcp * NPV + h + 1)
                            pr = prodP[(tcp * NPV + h) % 2][:, :]
                        else:
                            v.wait_ge(p.h["vc"], vcc)
                            pr = prodV[:, (h - NPV) * 2048:(h - NPV + 1) * 2048]
                        with p.nc.allow_low_precision(reason="DVE reduce accumulates fp32; bf16 store"):
                            v.tensor_reduce(ot[:, h * 128:(h + 1) * 128],
                                            pr.rearrange("p (d h) -> p d h", d=128),
                                            axis=AX.X, op=ALU.add).then_inc(p.h["vc"], 1)
                        vcc += 1

                for tc in range(NTT):
                    s_ = sc[tc % 2]
                    e_ = ex[tc % 2]
                    if tc >= 2:
                        v.wait_ge(p.h["sc"], expcnt(tc - 2))
                    for h in range(NHEAD):
                        qv = bass.AP(q_tok, tc * D + h * 128, [[NTT * D, 128], [0, 16], [1, 128]])
                        kv = bass.AP(k_tok, tc * D, [[NTT * D, 128], [128, 16], [1, 128]])
                        v.wait_ge(p.h["vc"], vcc)
                        v.tensor_tensor(prodS[h % 2][:, :].rearrange("p (h d) -> p h d", h=16),
                                        qv, kv, op=ALU.mult).then_inc(p.h["vc"], 1)
                        vcc += 1
                        v.wait_ge(p.h["vc"], vcc)
                        v.tensor_reduce(s_[:, h * 16:(h + 1) * 16],
                                        prodS[h % 2][:, :].rearrange("p (h d) -> p h d", h=16),
                                        axis=AX.X, op=ALU.add).then_inc(p.h["vc"], 1)
                        vcc += 1
                    v.wait_ge(p.h["sc"], expcnt(tc))
                    v.tensor_reduce(ssum[tc % 2][:, :], e_[:, :].rearrange("p (h j) -> p h j", h=16),
                                    axis=AX.X, op=ALU.add).then_inc(p.h["vc"], 1)
                    vcc += 1
                    v.wait_ge(p.h["vc"], vcc)
                    v.reciprocal(rcp[tc % 2][:, :], ssum[tc % 2][:, :]).then_inc(p.h["vc"], 1)
                    vcc += 1
                    rcv = bass.AP(rcp[tc % 2], 0, [[16, 128], [1, 16], [0, 16]])
                    v.wait_ge(p.h["vc"], vcc)
                    if tc >= 2:
                        v.wait_ge(p.h["pl"], pl0 + (tc - 1) * NPV)
                    v.tensor_tensor(pn[tc % 2][:, :].rearrange("p (h j) -> p h j", h=16),
                                    e_[:, :].rearrange("p (h j) -> p h j", h=16),
                                    rcv, op=ALU.mult).then_inc(p.h["vc"], 1)
                    vcc += 1
                    if tc >= 1:
                        emit_pvreds(tc - 1)
                    for i in range(4):
                        h = NPV + i
                        pv = bass.AP(pn[tc % 2], h * 16, [[256, 128], [0, 128], [1, 16]])
                        vv = bass.AP(v_tok, tc * D, [[NTT * D, 128], [1, 128], [128, 16]])
                        v.wait_ge(p.h["vc"], vcc)
                        v.tensor_tensor(prodV[:, i * 2048:(i + 1) * 2048]
                                        .rearrange("p (d h) -> p d h", d=128),
                                        pv, vv, op=ALU.mult).then_inc(p.h["vc"], 1)
                        vcc += 1
                emit_pvreds(NTT - 1)

            @blk.scalar
            def _(s):
                def emit_copies(tcp):
                    for h in range(NHEAD):
                        s.wait_ge(p.h["pe"], pe0 + tcp * 16 + h + 1)
                        s.copy(attn_fm[:, h * TPC + tcp * 128:h * TPC + (tcp + 1) * 128],
                               trp[(tcp * 16 + h) % 2][:, 0:128]).then_inc(p.h["sc"], 1)
                for tc in range(NTT):
                    s.wait_ge(p.h["vc"], vbase[tc] + 32)
                    s.activation(ex[tc % 2][:, :], sc[tc % 2][:, :], AF.Exp).then_inc(p.h["sc"], 1)
                    if tc >= 1:
                        emit_copies(tc - 1)
                emit_copies(NTT - 1)

            @blk.gpsimd
            def _(g):
                plc = pl0
                ldc = ld0

                def emit_cc(tcp):
                    nonlocal ldc
                    cc1i, cc1o, groups = cc
                    g.wait_ge(p.h["sc"], cpycnt(tcp, 15))
                    for j in range(2 * NRANK):
                        g.dma_start(
                            out=cc1i[tcp].ap()[j * 128:(j + 1) * 128, :]
                            .rearrange("p (h s) -> p h s", h=4),
                            in_=bass.AP(attn_fm, 4 * (j % NRANK) * TPC + tcp * 128,
                                        [[NHEAD * TPC, 128], [TPC, 4], [1, 128]]),
                        ).then_inc(p.h["ld"], 16)
                    ldc += 16 * 2 * NRANK
                    g.wait_ge(p.h["ld"], ldc)
                    with allow_ici_all_to_all():
                        g.collective_compute(
                            "AllToAll", ALU.bypass, replica_groups=groups,
                            ins=[cc1i[tcp].ap()], outs=[cc1o[tcp].ap()]).then_inc(p.h["cc1"])

                for tc in range(NTT):
                    if cc is not None and tc >= 1:
                        emit_cc(tc - 1)
                    for h in range(NPV):
                        j = tc * NPV + h
                        w = pncnt(tc)
                        if j >= 2:
                            tcp, hp = divmod(j - 2, NPV)
                            w = max(w, redcnt(tcp, hp))
                        g.wait_ge(p.h["vc"], w)
                        g.wait_ge(p.h["pl"], plc)
                        pv = bass.AP(pn[tc % 2], h * 16, [[256, 128], [0, 128], [1, 16]])
                        vv = bass.AP(v_tok, tc * D, [[NTT * D, 128], [1, 128], [128, 16]])
                        g.tensor_tensor(prodP[j % 2][:, :].rearrange("p (d h) -> p d h", d=128),
                                        pv, vv, op=ALU.mult).then_inc(p.h["pl"], 1)
                        plc += 1
                if cc is not None:
                    emit_cc(NTT - 1)

            @blk.tensor
            def _(t):
                for tc in range(NTT):
                    ot = otok[tc % 2]
                    for h in range(NHEAD):
                        i = tc * 16 + h
                        t.wait_ge(p.h["vc"], redcnt(tc, h))
                        if i >= 2:
                            tcp, hp = divmod(i - 2, 16)
                            t.wait_ge(p.h["sc"], cpycnt(tcp, hp))
                        t.transpose(trp[i % 2][:, 0:128], ot[:, h * 128:(h + 1) * 128],
                                    k["ident_b"][:, :]).then_inc(p.h["pe"], 1)

        p.c["pe"] = pe0 + 16 * NTT
        p.c["vc"] = vfinal
        p.c["pl"] = pl0 + NPV * NTT
        p.c["sc"] = sfinal
        if cc is not None:
            p.c["ld"] = ld0 + 16 * 2 * NRANK * NTT
            p.c["cc1"] = cc0 + NTT


def _emit_store(p, aps):
    sd0 = p.c["sd"]
    n = len(aps)
    with p.nc.Block() as blk:
        @blk.sync
        def _(sy):
            for dram, sb in aps:
                sy.dma_start(out=dram, in_=sb).then_inc(p.h["sd"], 16)
            sy.wait_ge(p.h["sd"], sd0 + 16 * n)
    p.c["sd"] = sd0 + 16 * n



def _emit_scramble8(p, io, a2abuf8, out_scrT, sel_sb, cc1o=None):
    nc = p.nc
    ld0, vc0, sd0 = p.c["ld"], p.c["vc"], p.c["sd"]
    per_tc = cc1o is not None
    with nc.Block() as blk:
        if per_tc:
            @blk.sync
            def _(sy):
                for tc in range(NTT):
                    sy.wait_ge(p.h["cc1"], tc + 1)
                    for gg in range(2 * NRANK):
                        sy.dma_start(
                            out=bass.AP(a2abuf8, gg * 4096 + tc * 128,
                                        [[32 * TPC, 128], [TPC, 4], [1, 128]]),
                            in_=cc1o[tc].ap()[gg * 128:(gg + 1) * 128, :]
                            .rearrange("p (h c) -> p h c", h=4),
                        ).then_inc(p.h["sd"], 16)
        else:
            @blk.gpsimd
            def _(g):
                g.dma_start(
                    out=a2abuf8[:, :].rearrange("p (r f) -> p r f", r=2 * NRANK),
                    in_=io["a2a"].rearrange("(r p) f -> p r f", p=128),
                ).then_inc(p.h["ld"], 16)

        @blk.vector
        def _(v):
            vcc = vc0
            if per_tc:
                v.wait_ge(p.h["sd"], sd0 + 16 * NTT * 2 * NRANK)
            else:
                v.wait_ge(p.h["ld"], ld0 + 16)
            for r in range(NRANK):
                for hl in range(4):
                    va = bass.AP(a2abuf8, (r * 4 + hl) * TPC,
                                 [[32 * TPC, 128], [1, 16], [16, 64]])
                    vb = bass.AP(a2abuf8, ((r + NRANK) * 4 + hl) * TPC,
                                 [[32 * TPC, 128], [1, 16], [16, 64]])
                    dstv = bass.AP(out_scrT, hl * 256 + r * 64,
                                   [[16 * TPC, 128], [1024, 16], [1, 64]])
                    v.wait_ge(p.h["vc"], vcc)
                    v.tensor_scalar(dstv, va, sel_sb[:, 0:1], None, op0=ALU.mult).then_inc(p.h["vc"], 1)
                    vcc += 1
                    v.wait_ge(p.h["vc"], vcc)
                    v.scalar_tensor_tensor(dstv, vb, sel_sb[:, 1:2], dstv,
                                           op0=ALU.mult, op1=ALU.add).then_inc(p.h["vc"], 1)
                    vcc += 1
    if per_tc:
        p.c["sd"] = sd0 + 16 * NTT * 2 * NRANK
    else:
        p.c["ld"] = ld0 + 16
    p.c["vc"] = vc0 + 32


def _ln_scratch(p, ctx, mbc=None, rbc=None, pool_split=True):
    E = ctx.enter_context
    nc = p.nc
    return {
        "mean": E(nc.sbuf_tensor(p.uname("ln_mean"), [1, 512], F32)),
        "msq": E(nc.sbuf_tensor(p.uname("ln_msq"), [1, 512], F32)),
        "var": E(nc.sbuf_tensor(p.uname("ln_var"), [1, 512], F32)),
        "sqt": E(nc.sbuf_tensor(p.uname("ln_sqt"), [1, 512], F32)),
        "rstd": E(nc.sbuf_tensor(p.uname("ln_rstd"), [1, 512], F32)),
        "tmp": E(nc.sbuf_tensor(p.uname("ln_tmp"), [128, 512], F32)),
        "tmp2": E(nc.sbuf_tensor(p.uname("ln_tmp2"), [128, 512], F32)) if pool_split else None,
        "mbs": E(nc.sbuf_tensor(p.uname("ln_mbs"), [128, 512], F32)) if pool_split else None,
        "rbs": E(nc.sbuf_tensor(p.uname("ln_rbs"), [128, 512], F32)) if pool_split else None,
        "mbc": mbc if mbc is not None else E(nc.psum_tensor(p.uname("ln_mbc"), [128, 512], F32)),
        "rbc": rbc if rbc is not None else E(nc.psum_tensor(p.uname("ln_rbc"), [128, 512], F32)),
    }


def _ln_finalize(p, k, stream, half, ps_s, ps_sq, sn, out_f32=None):
    pool_split = sn["mbs"] is not None
    """Finalize post-LN for a 512-token half from stats psums [1,512].

    Chunk normalization split between VectorE (ec 0-7, psum operands) and
    GpSimd (ec 8-15, from sbuf copies of the broadcasts).
    """
    nc = p.nc
    pe0, vc0, sc0, pl0 = p.c["pe"], p.c["vc"], p.c["sc"], p.c["pl"]
    with nc.Block() as blk:
        @blk.vector
        def _(v, half=half):
            vcc = vc0
            v.tensor_scalar(sn["mean"][:, :], ps_s[0:1, :], 1.0 / D, None, op0=ALU.mult).then_inc(p.h["vc"], 1)
            vcc += 1
            v.wait_ge(p.h["vc"], vcc)
            v.tensor_tensor(sn["msq"][:, :], sn["mean"][:, :], sn["mean"][:, :], op=ALU.mult).then_inc(p.h["vc"], 1)
            vcc += 1
            v.wait_ge(p.h["vc"], vcc)
            v.scalar_tensor_tensor(sn["var"][:, :], ps_sq[0:1, :], 1.0 / D, sn["msq"][:, :],
                                   op0=ALU.mult, op1=ALU.subtract).then_inc(p.h["vc"], 1)
            vcc += 1
            v.wait_ge(p.h["sc"], sc0 + 1)
            v.reciprocal(sn["rstd"][:, :], sn["sqt"][:, :]).then_inc(p.h["vc"], 1)
            vcc += 1
            v.wait_ge(p.h["pe"], pe0 + 2)
            if pool_split:
                v.wait_ge(p.h["sc"], sc0 + 3)
            for ec in range(8 if pool_split else NEC):
                sl_ = stream[:, ec * TPC + half * 512: ec * TPC + half * 512 + 512]
                v.wait_ge(p.h["vc"], vcc)
                v.tensor_tensor(sn["tmp"][:, :], sl_, sn["mbc"][:, :], op=ALU.subtract).then_inc(p.h["vc"], 1)
                vcc += 1
                v.wait_ge(p.h["vc"], vcc)
                if out_f32 is not None:
                    v.tensor_tensor(out_f32[:, ec * 512:(ec + 1) * 512],
                                    sn["tmp"][:, :], sn["rbc"][:, :], op=ALU.mult).then_inc(p.h["vc"], 1)
                    vcc += 1
                    v.wait_ge(p.h["vc"], vcc)
                v.tensor_tensor(sl_, sn["tmp"][:, :], sn["rbc"][:, :], op=ALU.mult).then_inc(p.h["vc"], 1)
                vcc += 1

        @blk.scalar
        def _(s):
            s.wait_ge(p.h["vc"], vc0 + 3)
            s.activation(sn["sqt"][:, :], sn["var"][:, :], AF.Sqrt, bias=k["eps11"][:, :]).then_inc(p.h["sc"], 1)
            if pool_split:
                s.wait_ge(p.h["pe"], pe0 + 2)
                s.copy(sn["mbs"][:, :], sn["mbc"][:, :]).then_inc(p.h["sc"], 1)
                s.copy(sn["rbs"][:, :], sn["rbc"][:, :]).then_inc(p.h["sc"], 1)

        @blk.gpsimd
        def _(g, half=half):
            if not pool_split:
                return
            plc = pl0
            g.wait_ge(p.h["sc"], sc0 + 3)
            for ec in range(8, NEC):
                sl_ = stream[:, ec * TPC + half * 512: ec * TPC + half * 512 + 512]
                g.wait_ge(p.h["pl"], plc)
                g.tensor_tensor(sn["tmp2"][:, :], sl_, sn["mbs"][:, :], op=ALU.subtract).then_inc(p.h["pl"], 1)
                plc += 1
                g.wait_ge(p.h["pl"], plc)
                if out_f32 is not None:
                    g.tensor_tensor(out_f32[:, ec * 512:(ec + 1) * 512],
                                    sn["tmp2"][:, :], sn["rbs"][:, :], op=ALU.mult).then_inc(p.h["pl"], 1)
                    plc += 1
                    g.wait_ge(p.h["pl"], plc)
                g.tensor_tensor(sl_, sn["tmp2"][:, :], sn["rbs"][:, :], op=ALU.mult).then_inc(p.h["pl"], 1)
                plc += 1

        @blk.tensor
        def _(t):
            t.wait_ge(p.h["vc"], vc0 + 1)
            t.matmul(sn["mbc"][:, :], k["ones_row_f"][:, :], sn["mean"][:, :],
                     start=True, stop=True).then_inc(p.h["pe"], 1)
            t.wait_ge(p.h["vc"], vc0 + 4)
            t.matmul(sn["rbc"][:, :], k["ones_row_f"][:, :], sn["rstd"][:, :],
                     start=True, stop=True).then_inc(p.h["pe"], 1)
    nf = 3 if out_f32 is not None else 2
    p.c["pe"] = pe0 + 2
    if pool_split:
        p.c["vc"] = vc0 + 4 + 8 * nf
        p.c["sc"] = sc0 + 3
        p.c["pl"] = pl0 + 8 * nf
    else:
        p.c["vc"] = vc0 + 4 + NEC * nf
        p.c["sc"] = sc0 + 1


def _emit_proj_b(p, k, io, wkey, rhs_buf, stream, mode, bias_pp=None, dst=None,
                 sn=None, out_f32=None, n_out=NEC, n_k=NEC):
    nc = p.nc
    with ExitStack() as ctx:
        E = ctx.enter_context
        wb = [E(nc.sbuf_tensor(p.uname(f"pb_w{i}"), [128, n_k * 128], BF16)) for i in range(2)]
        acc = [E(nc.psum_tensor(p.uname(f"pb_ps{i}"), [128, 512], F32)) for i in range(2)]
        wsems = ("lwa", "lwb")
        if mode == "resid_ln":
            ps_s = [E(nc.psum_tensor(p.uname(f"pb_s{i}"), [128, 512], F32)) for i in range(2)]
            ps_sq = [E(nc.psum_tensor(p.uname(f"pb_q{i}"), [128, 512], F32)) for i in range(2)]
            ysq = [E(nc.sbuf_tensor(p.uname(f"pb_y{i}"), [128, 512], BF16)) for i in range(2)]
            bias_sb = E(nc.sbuf_tensor(p.uname("pb_bias"), [128, NEC], F32)) if bias_pp else None
            if bias_pp:
                ld0 = p.c["ld"]
                with nc.Block() as blk:
                    @blk.gpsimd
                    def _(g):
                        g.dma_start(out=bias_sb[:, :], in_=io[bias_pp]).then_inc(p.h["ld"], 16)
                        g.wait_ge(p.h["ld"], ld0 + 16)
                p.c["ld"] = ld0 + 16
        with nc.Block() as blk:
            @blk.gpsimd
            def _(g):
                g.dma_start(out=wb[0][:, :].rearrange("p (c m) -> p c m", c=n_k),
                            in_=io[wkey][0].rearrange("c p m -> p c m")).then_inc(p.h["lwa"], 16)
        p.c["lwa"] += 16

        for oc in range(n_out):
            wslot = oc % 2
            wsem = wsems[wslot]
            lw_need = p.c[wsem]
            pe0, vc0 = p.c["pe"], p.c["vc"]
            with nc.Block() as blk:
                @blk.gpsimd
                def _(g, oc=oc, wslot=wslot):
                    if oc + 1 < n_out:
                        g.dma_start(out=wb[1 - wslot][:, :].rearrange("p (c m) -> p c m", c=n_k),
                                    in_=io[wkey][oc + 1].rearrange("c p m -> p c m")
                                    ).then_inc(p.h[wsems[1 - wslot]], 16)

                @blk.tensor
                def _(t, oc=oc, wslot=wslot, lw_need=lw_need, vc0=vc0, wsem=wsem):
                    t.wait_ge(p.h[wsem], lw_need)
                    for half in range(2):
                        for kc in range(n_k):
                            mm = t.matmul(acc[half % 2][:, :],
                                          wb[wslot][:, kc * 128:(kc + 1) * 128],
                                          rhs_buf[:, kc * TPC + half * 512:kc * TPC + half * 512 + 512],
                                          start=(kc == 0), stop=(kc == n_k - 1))
                            if kc == n_k - 1:
                                mm.then_inc(p.h["pe"], 1)
                    if mode == "resid_ln":
                        for half in range(2):
                            t.wait_ge(p.h["vc"], vc0 + 2 * (half + 1))
                            t.matmul(ps_s[half][0:1, :], k["ones_col_b"][:, :],
                                     stream[:, oc * TPC + half * 512: oc * TPC + half * 512 + 512],
                                     start=(oc == 0), stop=(oc == n_out - 1)).then_inc(p.h["pe"], 1)
                            t.matmul(ps_sq[half][0:1, :], k["ones_col_b"][:, :],
                                     ysq[half][:, :],
                                     start=(oc == 0), stop=(oc == n_out - 1)).then_inc(p.h["pe"], 1)

                @blk.vector
                def _(v, oc=oc, pe0=pe0, vc0=vc0):
                    vcc = vc0
                    for half in range(2):
                        v.wait_ge(p.h["pe"], pe0 + half + 1)
                        sl_ = stream[:, oc * TPC + half * 512: oc * TPC + half * 512 + 512] \
                            if mode == "resid_ln" else None
                        if mode == "resid_ln":
                            if bias_pp:
                                v.scalar_tensor_tensor(sl_, acc[half % 2][:, :],
                                                       bias_sb[:, oc:oc + 1], sl_,
                                                       op0=ALU.add, op1=ALU.add).then_inc(p.h["vc"], 1)
                            else:
                                v.tensor_tensor(sl_, acc[half % 2][:, :], sl_,
                                                op=ALU.add).then_inc(p.h["vc"], 1)
                            vcc += 1
                            v.wait_ge(p.h["vc"], vcc)
                            v.tensor_tensor(ysq[half][:, :], sl_, sl_, op=ALU.mult).then_inc(p.h["vc"], 1)
                            vcc += 1
                        else:
                            v.tensor_copy(dst[:, oc * TPC + half * 512: oc * TPC + half * 512 + 512],
                                          acc[half % 2][:, :]).then_inc(p.h["vc"], 1)
                            vcc += 1
            p.c["pe"] = pe0 + 2 + (4 if mode == "resid_ln" else 0)
            p.c["vc"] = vc0 + (4 if mode == "resid_ln" else 2)
            if oc + 1 < n_out:
                p.c[wsems[1 - wslot]] += 16
        if mode == "resid_ln":
            for half in range(2):
                _ln_finalize(p, k, stream, half, ps_s[half], ps_sq[half], sn,
                             out_f32=out_f32)


def _emit_ffn(p, k, io, stream, wg_key, wu_key, wd_key, sg_key, su_key, sd_key,
              fuseT, out_f32=None, o_y=None):
    nc = p.nc
    with ExitStack() as ctx:
        E = ctx.enter_context
        wgb = [E(nc.sbuf_tensor(p.uname(f"ff_wg{i}"), [128, 2 * NEC * 128], BF16)) for i in range(2)]
        wub = [E(nc.sbuf_tensor(p.uname(f"ff_wu{i}"), [128, 2 * NEC * 128], BF16)) for i in range(2)]
        sg_sb = E(nc.sbuf_tensor(p.uname("ff_sg"), [128, NIC], F32))
        su_sb = E(nc.sbuf_tensor(p.uname("ff_su"), [128, NIC], F32))
        sd_sb = E(nc.sbuf_tensor(p.uname("ff_sd"), [128, NEC], F32))
        silu_sb = [E(nc.sbuf_tensor(p.uname(f"ff_si{i}"), [128, 512], BF16)) for i in range(2)]
        ysq = [E(nc.sbuf_tensor(p.uname(f"ff_ysq{i}"), [128, 512], BF16)) for i in range(2)]
        pg = [E(nc.psum_tensor(p.uname(f"ff_pg{i}"), [128, 512], F32)) for i in range(2)]
        pu = [E(nc.psum_tensor(p.uname(f"ff_pu{i}"), [128, 512], F32)) for i in range(2)]
        py = [E(nc.psum_tensor(p.uname(f"ff_py{i}"), [128, 512], F32)) for i in range(2)]
        ps_s = E(nc.psum_tensor(p.uname("ff_s"), [128, 512], F32))
        ps_sq = E(nc.psum_tensor(p.uname("ff_q"), [128, 512], F32))

        ld0 = p.c["ld"]
        with nc.Block() as blk:
            @blk.gpsimd
            def _(g):
                g.dma_start(out=sg_sb[:, :], in_=io[sg_key]).then_inc(p.h["ld"], 16)
                g.dma_start(out=su_sb[:, :], in_=io[su_key]).then_inc(p.h["ld"], 16)
                g.dma_start(out=sd_sb[:, :], in_=io[sd_key]).then_inc(p.h["ld"], 16)
                g.wait_ge(p.h["ld"], ld0 + 48)
        p.c["ld"] = ld0 + 48

        for half in range(2):
            with nc.Block() as blk:
                @blk.gpsimd
                def _(g):
                    g.dma_start(out=wgb[0][:, :].rearrange("p (c m) -> p c m", c=2 * NEC),
                                in_=io[wg_key][0].rearrange("c p m -> p c m")).then_inc(p.h["lwa"], 16)
                    g.dma_start(out=wub[0][:, :].rearrange("p (c m) -> p c m", c=2 * NEC),
                                in_=io[wu_key][0].rearrange("c p m -> p c m")).then_inc(p.h["lwc"], 16)
            p.c["lwa"] += 16
            p.c["lwc"] += 16
            pe0, vc0, sc0 = p.c["pe"], p.c["vc"], p.c["sc"]
            lwg = [p.c["lwa"], p.c["lwb"]]
            lwu = [p.c["lwc"], p.c["lwd"]]
            need_g = []
            need_u = []
            for q in range(NIC // 2):
                need_g.append(lwg[q % 2])
                need_u.append(lwu[q % 2])
                if q + 1 < NIC // 2:
                    lwg[(q + 1) % 2] += 16
                    lwu[(q + 1) % 2] += 16
            gsem = ("lwa", "lwb")
            usem = ("lwc", "lwd")
            with nc.Block() as blk:
                @blk.gpsimd
                def _(g, half=half):
                    for q in range(NIC // 2):
                        if q + 1 < NIC // 2:
                            if q >= 1:
                                g.wait_ge(p.h["pe"], pe0 + 2 * (2 * q - 1) + 2)
                            g.dma_start(out=wgb[(q + 1) % 2][:, :].rearrange("p (c m) -> p c m", c=2 * NEC),
                                        in_=io[wg_key][q + 1].rearrange("c p m -> p c m")
                                        ).then_inc(p.h[gsem[(q + 1) % 2]], 16)
                            g.dma_start(out=wub[(q + 1) % 2][:, :].rearrange("p (c m) -> p c m", c=2 * NEC),
                                        in_=io[wu_key][q + 1].rearrange("c p m -> p c m")
                                        ).then_inc(p.h[usem[(q + 1) % 2]], 16)

                @blk.tensor
                def _(t, half=half):
                    for ic in range(NIC):
                        q, qi = divmod(ic, 2)
                        if qi == 0:
                            t.wait_ge(p.h[gsem[q % 2]], need_g[q])
                            t.wait_ge(p.h[usem[q % 2]], need_u[q])
                        if ic >= 2:
                            t.wait_ge(p.h["sc"], sc0 + (ic - 2) + 1)
                            t.wait_ge(p.h["vc"], vc0 + (ic - 2) + 1)
                        for dc in range(NEC):
                            t.matmul(pg[ic % 2][:, :], wgb[q % 2][:, (qi * NEC + dc) * 128:(qi * NEC + dc + 1) * 128],
                                     stream[:, dc * TPC + half * 512:dc * TPC + half * 512 + 512],
                                     start=(dc == 0), stop=(dc == NEC - 1)
                                     ).then_inc(p.h["pe"], 1) if dc == NEC - 1 else t.matmul(
                                pg[ic % 2][:, :], wgb[q % 2][:, (qi * NEC + dc) * 128:(qi * NEC + dc + 1) * 128],
                                stream[:, dc * TPC + half * 512:dc * TPC + half * 512 + 512],
                                start=(dc == 0), stop=False)
                        for dc in range(NEC):
                            t.matmul(pu[ic % 2][:, :], wub[q % 2][:, (qi * NEC + dc) * 128:(qi * NEC + dc + 1) * 128],
                                     stream[:, dc * TPC + half * 512:dc * TPC + half * 512 + 512],
                                     start=(dc == 0), stop=(dc == NEC - 1)
                                     ).then_inc(p.h["pe"], 1) if dc == NEC - 1 else t.matmul(
                                pu[ic % 2][:, :], wub[q % 2][:, (qi * NEC + dc) * 128:(qi * NEC + dc + 1) * 128],
                                stream[:, dc * TPC + half * 512:dc * TPC + half * 512 + 512],
                                start=(dc == 0), stop=False)

                @blk.scalar
                def _(s, half=half):
                    s.wait_ge(p.h["ld"], ld0 + 48)
                    for ic in range(NIC):
                        s.wait_ge(p.h["pe"], pe0 + 2 * ic + 1)
                        if ic >= 2:
                            s.wait_ge(p.h["vc"], vc0 + (ic - 2) + 1)
                        s.activation(silu_sb[ic % 2][:, :], pg[ic % 2][:, :], AF.Silu,
                                     scale=sg_sb[:, ic:ic + 1]).then_inc(p.h["sc"], 1)

                @blk.vector
                def _(v, half=half):
                    for ic in range(NIC):
                        v.wait_ge(p.h["pe"], pe0 + 2 * ic + 2)
                        v.wait_ge(p.h["sc"], sc0 + ic + 1)
                        v.scalar_tensor_tensor(fuseT[:, ic * 512:(ic + 1) * 512],
                                               pu[ic % 2][:, :], su_sb[:, ic:ic + 1],
                                               silu_sb[ic % 2][:, :],
                                               op0=ALU.mult, op1=ALU.mult).then_inc(p.h["vc"], 1)
            p.c["pe"] = pe0 + 2 * NIC
            p.c["sc"] = sc0 + NIC
            p.c["vc"] = vc0 + NIC
            p.c["lwa"], p.c["lwb"] = lwg[0], lwg[1]
            p.c["lwc"], p.c["lwd"] = lwu[0], lwu[1]

            ctxd = ExitStack()
            wdb = [ctxd.enter_context(nc.sbuf_tensor(p.uname(f"ff_wd{i}"), [128, 64 * 128], BF16))
                   for i in range(2)]
            with nc.Block() as blk:
                @blk.gpsimd
                def _(g):
                    g.dma_start(out=wdb[0][:, :].rearrange("p (c m) -> p c m", c=64),
                                in_=io[wd_key][0].rearrange("c p m -> p c m")).then_inc(p.h["lwa"], 16)
            p.c["lwa"] += 16
            pe0, vc0 = p.c["pe"], p.c["vc"]
            lwd_ = [p.c["lwa"], p.c["lwb"]]
            need_d = []
            for dc in range(NEC):
                need_d.append(lwd_[dc % 2])
                if dc + 1 < NEC:
                    lwd_[(dc + 1) % 2] += 16
            dsem = ("lwa", "lwb")
            with nc.Block() as blk:
                @blk.gpsimd
                def _(g):
                    for dc in range(1, NEC):
                        if dc >= 2:
                            dcp = dc - 2
                            g.wait_ge(p.h["pe"], pe0 + (1 if dcp == 0 else 3 * dcp - 1))
                        g.dma_start(out=wdb[dc % 2][:, :].rearrange("p (c m) -> p c m", c=64),
                                    in_=io[wd_key][dc].rearrange("c p m -> p c m")
                                    ).then_inc(p.h[dsem[dc % 2]], 16)

                @blk.tensor
                def _(t, half=half):
                    def chain_inc(dc):
                        return pe0 + (1 if dc == 0 else 3 * dc - 1)
                    def emit_stats(j):
                        t.wait_ge(p.h["vc"], vc0 + 2 * j + 2)
                        t.matmul(ps_s[0:1, :], k["ones_col_b"][:, :],
                                 stream[:, j * TPC + half * 512:j * TPC + half * 512 + 512],
                                 start=(j == 0), stop=(j == NEC - 1)).then_inc(p.h["pe"], 1)
                        t.matmul(ps_sq[0:1, :], k["ones_col_b"][:, :], ysq[j % 2][:, :],
                                 start=(j == 0), stop=(j == NEC - 1)).then_inc(p.h["pe"], 1)
                    for dc in range(NEC):
                        if dc >= 2:
                            t.wait_ge(p.h["vc"], vc0 + 2 * (dc - 2) + 2)
                        t.wait_ge(p.h[dsem[dc % 2]], need_d[dc])
                        for ic in range(NIC):
                            mm = t.matmul(py[dc % 2][:, :],
                                          wdb[dc % 2][:, ic * 128:(ic + 1) * 128],
                                          fuseT[:, ic * 512:(ic + 1) * 512],
                                          start=(ic == 0), stop=(ic == NIC - 1))
                            if ic == NIC - 1:
                                mm.then_inc(p.h["pe"], 1)
                        if dc >= 1:
                            emit_stats(dc - 1)
                    emit_stats(NEC - 1)

                @blk.vector
                def _(v, half=half):
                    vcc = vc0
                    for dc in range(NEC):
                        v.wait_ge(p.h["pe"], pe0 + (1 if dc == 0 else 3 * dc - 1))
                        if dc >= 2:
                            v.wait_ge(p.h["pe"], pe0 + 3 * (dc - 2) + 4)
                        sl_ = stream[:, dc * TPC + half * 512:dc * TPC + half * 512 + 512]
                        v.scalar_tensor_tensor(sl_, py[dc % 2][:, :], sd_sb[:, dc:dc + 1], sl_,
                                               op0=ALU.mult, op1=ALU.add).then_inc(p.h["vc"], 1)
                        vcc += 1
                        v.wait_ge(p.h["vc"], vcc)
                        v.tensor_tensor(ysq[dc % 2][:, :], sl_, sl_, op=ALU.mult).then_inc(p.h["vc"], 1)
                        vcc += 1
            p.c["pe"] = pe0 + 3 * NEC
            p.c["vc"] = vc0 + 2 * NEC
            p.c["lwa"], p.c["lwb"] = lwd_[0], lwd_[1]
            ctxd.close()

            with ExitStack() as ctx2:
                sn = _ln_scratch(p, ctx2, mbc=pg[0], rbc=pg[1],
                                 pool_split=(out_f32 is None))
                _ln_finalize(p, k, stream, half, ps_s, ps_sq, sn, out_f32=out_f32)
            if out_f32 is not None and o_y is not None:
                _emit_final_store_half(p, k, out_f32, o_y, half, pg)



def _emit_kv_transpose(p, k, k_tok, v_tok, kvT):
    nc = p.nc
    with ExitStack() as ctx:
        E = ctx.enter_context
        kvpc = [E(nc.sbuf_tensor(p.uname(f"kv_pc{i}"), [128, D], BF16)) for i in range(1)]
        trp = [E(nc.psum_tensor(p.uname(f"kv_tr{i}"), [128, 1024], BF16)) for i in range(2)]
        for tc in range(NTT):
            pe0, vc0 = p.c["pe"], p.c["vc"]
            with nc.Block() as blk:
                @blk.vector
                def _(v, tc=tc, pe0=pe0, vc0=vc0):
                    v.tensor_tensor(kvpc[0][:, :], k_tok[:, tc * D:(tc + 1) * D],
                                    v_tok[:, tc * D:(tc + 1) * D], op=ALU.mult).then_inc(p.h["vc"], 1)
                    for ec in range(NEC):
                        v.wait_ge(p.h["pe"], pe0 + ec + 1)
                        v.tensor_copy(kvT[:, ec * TPC + tc * 128:ec * TPC + (tc + 1) * 128],
                                      trp[ec % 2][:, 0:128]).then_inc(p.h["vc"], 1)

                @blk.tensor
                def _(t, tc=tc, pe0=pe0, vc0=vc0):
                    t.wait_ge(p.h["vc"], vc0 + 1)
                    for ec in range(NEC):
                        if ec >= 2:
                            t.wait_ge(p.h["vc"], vc0 + 1 + ec - 1)
                        t.transpose(trp[ec % 2][:, 0:128], kvpc[0][:, ec * 128:(ec + 1) * 128],
                                    k["ident_b"][:, :]).then_inc(p.h["pe"], 1)
            p.c["pe"] = pe0 + NEC
            p.c["vc"] = vc0 + 1 + NEC


def _emit_scan(p, k, kvT, kvt_loc, blocksums):
    nc = p.nc
    with ExitStack() as ctx:
        E = ctx.enter_context
        zeros = E(nc.sbuf_tensor(p.uname("sc_zero"), [128, TPC], F32))
        pl0 = p.c["pl"]
        with nc.Block() as blk:
            @blk.gpsimd
            def _(g):
                g.memset(zeros[:, :], 0.0).then_inc(p.h["pl"], 1)
        p.c["pl"] = pl0 + 1
        vc0 = p.c["vc"]
        with nc.Block() as blk:
            @blk.vector
            def _(v):
                vcc = vc0
                v.wait_ge(p.h["pl"], pl0 + 1)
                for ec in range(NEC):
                    with p.nc.allow_low_precision(reason="scan state fp32; bf16 store"):
                        v.tensor_tensor_scan(kvt_loc[:, ec * TPC:(ec + 1) * TPC],
                                             kvT[:, ec * TPC:(ec + 1) * TPC], zeros[:, :],
                                             0.0, op0=ALU.add, op1=ALU.add).then_inc(p.h["vc"], 1)
                    vcc += 1
                    v.wait_ge(p.h["vc"], vcc)
                    v.tensor_copy(blocksums[:, ec:ec + 1],
                                  kvt_loc[:, (ec + 1) * TPC - 1:(ec + 1) * TPC]).then_inc(p.h["vc"], 1)
                    vcc += 1
        p.c["vc"] = vc0 + 2 * NEC


def _emit_l1b_vhat(p, k, io, qT1, kvt, vhat):
    nc = p.nc
    with ExitStack() as ctx:
        E = ctx.enter_context
        bs4 = E(nc.sbuf_tensor(p.uname("vb_bs4"), [128, 4 * NEC], F32))
        mask = E(nc.sbuf_tensor(p.uname("vb_mask"), [128, NRANK], F32))
        offs = E(nc.sbuf_tensor(p.uname("vb_offs"), [128, NEC], F32))
        kvf = E(nc.sbuf_tensor(p.uname("vb_kvf"), [128, TPC], F32))
        ld0, vc0, pl0 = p.c["ld"], p.c["vc"], p.c["pl"]
        with nc.Block() as blk:
            @blk.gpsimd
            def _(g):
                g.dma_start(out=bs4[:, :].rearrange("p (r c) -> p r c", r=4),
                            in_=io["bs_all"].rearrange("r p c -> p r c")).then_inc(p.h["ld"], 16)
                g.dma_start(out=mask[:, :], in_=io["mask_pp"]).then_inc(p.h["ld"], 16)
                g.memset(offs[:, :], 0.0).then_inc(p.h["pl"], 1)

            @blk.vector
            def _(v):
                vcc = vc0
                v.wait_ge(p.h["ld"], ld0 + 32)
                v.wait_ge(p.h["pl"], pl0 + 1)
                for r in range(NRANK):
                    v.wait_ge(p.h["vc"], vcc)
                    v.scalar_tensor_tensor(offs[:, :], bs4[:, r * NEC:(r + 1) * NEC],
                                           mask[:, r:r + 1], offs[:, :],
                                           op0=ALU.mult, op1=ALU.add).then_inc(p.h["vc"], 1)
                    vcc += 1
        p.c["ld"] = ld0 + 32
        p.c["pl"] = pl0 + 1
        p.c["vc"] = vc0 + NRANK
        vc0 = p.c["vc"]
        with nc.Block() as blk:
            @blk.vector
            def _(v):
                vcc = vc0
                for ec in range(NEC):
                    v.wait_ge(p.h["vc"], vcc)
                    v.tensor_scalar(kvf[:, :], kvt[:, ec * TPC:(ec + 1) * TPC],
                                    offs[:, ec:ec + 1], None, op0=ALU.add).then_inc(p.h["vc"], 1)
                    vcc += 1
                    v.wait_ge(p.h["vc"], vcc)
                    v.scalar_tensor_tensor(vhat[:, ec * TPC:(ec + 1) * TPC],
                                           qT1[:, ec * TPC:(ec + 1) * TPC], L1_DENOM, kvf[:, :],
                                           op0=ALU.mult, op1=ALU.mult).then_inc(p.h["vc"], 1)
                    vcc += 1
        p.c["vc"] = vc0 + 2 * NEC


def build_l1a():
    p = P()
    io = {
        "ident": p.dram_in("ident", [128, 128], F32),
        "stream_in": p.dram_in("stream_in", [128, NEC * TPC], BF16),
        "l1_wq_b": p.dram_in("l1_wq_b", [NEC, NEC, 128, 128], F32),
        "l1_wk_a": p.dram_in("l1_wk_a", [NEC, NEG, 128, 512], F32),
        "l1_wv_a": p.dram_in("l1_wv_a", [NEC, NEG, 128, 512], F32),
    }
    o_q = p.dram_out("o_q", [128, NEC * TPC], BF16)
    o_kvt = p.dram_out("o_kvt", [128, NEC * TPC], BF16)
    o_bs = p.dram_out("o_bs", [128, NEC], F32)
    with p.ctx:
        k = _consts(p)
        stream = p.sb("stream", [128, NEC * TPC], BF16)
        qT1 = p.sb("qT1", [128, NEC * TPC], BF16)
        kvT = p.sb("kvT", [128, NEC * TPC], BF16)
        blocksums = p.sb("blocksums", [128, NEC], F32)
        _emit_init(p, k, io)
        _emit_load_sb(p, [(stream[:, :], io["stream_in"])])
        _emit_proj_b(p, k, io, "l1_wq_b", stream, None, "copy", dst=qT1)
        with ExitStack() as ctx2:
            k_tok = ctx2.enter_context(p.nc.sbuf_tensor(p.uname("k_tok"), [128, NTT * D], BF16))
            v_tok = ctx2.enter_context(p.nc.sbuf_tensor(p.uname("v_tok"), [128, NTT * D], BF16))
            _emit_qkv(p, k, io, stream,
                      outs={"k": k_tok, "v": v_tok},
                      wnames={"k": "l1_wk_a", "v": "l1_wv_a"},
                      biases={}, rope=set())
            _emit_kv_transpose(p, k, k_tok, v_tok, kvT)
        kvt_loc = p.sb("kvt_loc", [128, NEC * TPC], BF16)
        _emit_scan(p, k, kvT, kvt_loc, blocksums)
        _emit_store(p, [(o_q, qT1[:, :]), (o_kvt, kvt_loc[:, :]), (o_bs, blocksums[:, :])])
    return p.nc


def build_l1b():
    p = P()
    io = {
        "ident": p.dram_in("ident", [128, 128], F32),
        "stream_in": p.dram_in("stream_in", [128, NEC * TPC], BF16),
        "q_in": p.dram_in("q_in", [128, NEC * TPC], BF16),
        "kvt_in": p.dram_in("kvt_in", [128, NEC * TPC], BF16),
        "bs_all": p.dram_in("bs_all", [NRANK, 128, NEC], F32),
        "mask_pp": p.dram_in("mask_pp", [128, NRANK], F32),
        "l1_wo_b": p.dram_in("l1_wo_b", [NEC, NEC, 128, 128], F32),
    }
    o_stream = p.dram_out("o_stream", [128, NEC * TPC], BF16)
    with p.ctx:
        k = _consts(p)
        stream = p.sb("stream", [128, NEC * TPC], BF16)
        qT1 = p.sb("qT1", [128, NEC * TPC], BF16)
        kvt = p.sb("kvt", [128, NEC * TPC], BF16)
        vhat = p.sb("vhat", [128, NEC * TPC], BF16)
        _emit_init(p, k, io)
        _emit_load_sb(p, [(stream[:, :], io["stream_in"]),
                          (qT1[:, :], io["q_in"]), (kvt[:, :], io["kvt_in"])])
        _emit_l1b_vhat(p, k, io, qT1, kvt, vhat)
        with ExitStack() as ctx2:
            sn = _ln_scratch(p, ctx2)
            _emit_proj_b(p, k, io, "l1_wo_b", vhat, stream, "resid_ln", sn=sn)
        _emit_store(p, [(o_stream, stream[:, :])])
    return p.nc


def build_l0a(debug=False):
    p = P()
    io = {
        "ident": p.dram_in("ident", [128, 128], F32),
        "x": p.dram_in("x", [TPC, D], F32),
        "l0_wk_a": p.dram_in("l0_wk_a", [NEC, NEG, 128, 512], F32),
        "l0_wv_a": p.dram_in("l0_wv_a", [NEC, NEG, 128, 512], F32),
        "l0_wq_a": p.dram_in("l0_wq_a", [NEC, NEG, 128, 512], F32),
        "l0_bq_r": p.dram_in("l0_bq_r", [1, D], F32),
        "l0_bk_r": p.dram_in("l0_bk_r", [1, D], F32),
        "l0_bv_r": p.dram_in("l0_bv_r", [1, D], F32),
        "ropecq": p.dram_in("ropecq", [TPC, TPC], F32),
        "ropesq": p.dram_in("ropesq", [TPC, TPC], F32),
        "ropeck": p.dram_in("ropeck", [TPC, TPC], F32),
        "ropesk": p.dram_in("ropesk", [TPC, TPC], F32),
    }
    o_stream = p.dram_out("o_stream", [128, NEC * TPC], BF16)
    o_attn = p.dram_out("o_attn", [128, NHEAD * TPC], BF16)
    if debug:
        o_q = p.dram_out("o_q", [128, NTT * D], BF16)
        o_k = p.dram_out("o_k", [128, NTT * D], BF16)
        o_v = p.dram_out("o_v", [128, NTT * D], BF16)

    with p.ctx:
        k = _consts(p)
        stream = p.sb("stream", [128, NEC * TPC], BF16)
        q_tok = p.sb("q_tok", [128, NTT * D], BF16)
        k_tok = p.sb("k_tok", [128, NTT * D], BF16)
        v_tok = p.sb("v_tok", [128, NTT * D], BF16)
        _emit_init(p, k, io)
        _emit_load_stream(p, k, io["x"], stream)
        _emit_qkv(p, k, io, stream,
                  outs={"k": k_tok, "v": v_tok, "q": q_tok},
                  wnames={"k": "l0_wk_a", "v": "l0_wv_a", "q": "l0_wq_a"},
                  biases={"k": "l0_bk_r", "v": "l0_bv_r", "q": "l0_bq_r"},
                  rope={"q", "k"})
        attn_fm = p.sb("attn_fm", [128, NHEAD * TPC], BF16)
        bufs = {"q_tok": q_tok, "k_tok": k_tok, "v_tok": v_tok, "attn_fm": attn_fm}
        _emit_l0_attn(p, k, io, bufs)
        stores = [
            (o_stream, stream[:, :]),
            (o_attn, attn_fm[:, :]),
        ]
        if debug:
            stores += [(o_q, q_tok[:, :]), (o_k, k_tok[:, :]), (o_v, v_tok[:, :])]
        _emit_store(p, stores)
    return p.nc


def _l0a_inmap(sh, pc):
    return {"ident": sh["ident"],
            "l0_wk_a": sh["l0_wk_a"], "l0_wv_a": sh["l0_wv_a"], "l0_wq_a": sh["l0_wq_a"],
            "l0_bq_r": sh["l0_bq_r"], "l0_bk_r": sh["l0_bk_r"], "l0_bv_r": sh["l0_bv_r"],
            "x": pc["x"],
            "ropecq": pc["ropecq"], "ropesq": pc["ropesq"],
            "ropeck": pc["ropeck"], "ropesk": pc["ropesk"]}


def _emit_load_sb(p, pairs):
    sd0 = p.c["sd"]
    n = len(pairs)
    with p.nc.Block() as blk:
        @blk.sync
        def _(sy):
            for sb, dram in pairs:
                sy.dma_start(out=sb, in_=dram).then_inc(p.h["sd"], 16)
            sy.wait_ge(p.h["sd"], sd0 + 16 * n)
    p.c["sd"] = sd0 + 16 * n


def build_l0b():
    p = P()
    io = {
        "ident": p.dram_in("ident", [128, 128], F32),
        "a2a": p.dram_in("a2a", [2 * NRANK * 128, 4 * TPC], BF16),
        "sel_pp": p.dram_in("sel_pp", [128, 2], F32),
        "stream_in": p.dram_in("stream_in", [128, NEC * TPC], BF16),
        "l0_wo_b": p.dram_in("l0_wo_b", [NEC, NEC, 128, 128], F32),
        "l0_bo_pp": p.dram_in("l0_bo_pp", [128, NEC], F32),
    }
    o_stream = p.dram_out("o_stream", [128, NEC * TPC], BF16)
    with p.ctx:
        k = _consts(p)
        stream = p.sb("stream", [128, NEC * TPC], BF16)
        a2abuf = p.sb("a2abuf", [128, 2 * NEC * TPC], BF16)
        out_scrT = p.sb("out_scrT", [128, NEC * TPC], BF16)
        sel_sb = p.sb("sel_sb", [128, 2], F32)
        _emit_init(p, k, io)
        _emit_load_sb(p, [(stream[:, :], io["stream_in"]), (sel_sb[:, :], io["sel_pp"])])
        _emit_scramble8(p, io, a2abuf, out_scrT, sel_sb)
        with ExitStack() as ctx2:
            sn = _ln_scratch(p, ctx2)
            _emit_proj_b(p, k, io, "l0_wo_b", out_scrT, stream, "resid_ln",
                         bias_pp="l0_bo_pp", sn=sn)
        _emit_store(p, [(o_stream, stream[:, :])])
    return p.nc


def _host_a2a(attn_fms, b, sl):
    blocks = [np.asarray(attn_fms[g])[:, 4 * sl * TPC:(4 * sl + 4) * TPC]
              for g in range(2 * NRANK)]
    return np.ascontiguousarray(np.concatenate(blocks, axis=0))


def build_ffn(l, final=False):
    p = P()
    io = {
        "ident": p.dram_in("ident", [128, 128], F32),
        "stream_in": p.dram_in("stream_in", [128, NEC * TPC], BF16),
        "wg": p.dram_in("wg", [NIC // 2, 2 * NEC, 128, 128], I8),
        "wu": p.dram_in("wu", [NIC // 2, 2 * NEC, 128, 128], I8),
        "wd": p.dram_in("wd", [NEC, NIC, 128, 128], I8),
        "sg": p.dram_in("sg", [128, NIC], F32),
        "su": p.dram_in("su", [128, NIC], F32),
        "sd": p.dram_in("sd", [128, NEC], F32),
    }
    o_stream = p.dram_out("o_stream", [128, NEC * TPC], BF16)
    o_y = p.dram_out("o_y", [TPC, D], F32) if final else None
    with p.ctx:
        k = _consts(p)
        stream = p.sb("stream", [128, NEC * TPC], BF16)
        _emit_init(p, k, io)
        _emit_load_sb(p, [(stream[:, :], io["stream_in"])])
        with ExitStack() as ctx2:
            fuseT = ctx2.enter_context(p.nc.sbuf_tensor(p.uname("fuseT"), [128, NIC * 512], BF16))
            xTf32 = (ctx2.enter_context(p.nc.sbuf_tensor(p.uname("xTf32"), [128, NEC * 512], F32))
                     if final else None)
            _emit_ffn(p, k, io, stream, "wg", "wu", "wd", "sg", "su", "sd",
                      fuseT, out_f32=xTf32, o_y=o_y)
        if not final:
            _emit_store(p, [(o_stream, stream[:, :])])
        else:
            _emit_store(p, [(o_stream, stream[:, :])])
    return p.nc


def _emit_final_store_half(p, k, xTf32, o_y, half, trp):
    nc = p.nc
    with ExitStack() as ctx:
        E = ctx.enter_context
        stg = [E(nc.sbuf_tensor(p.uname(f"fs_st{i}"), [128, 128], F32)) for i in range(4)]
        pe0, sc0 = p.c["pe"], p.c["sc"]
        fs = ("fsa", "fsb", "fsc", "fsd")
        fs0 = [p.c[n] for n in fs]
        NB = NEC * 4
        with nc.Block() as blk:
            @blk.tensor
            def _(t):
                for i in range(NB):
                    ec, tc = divmod(i, 4)
                    if i >= 2:
                        t.wait_ge(p.h["sc"], sc0 + i - 1)
                    t.transpose(trp[i % 2][:, 0:128],
                                xTf32[:, ec * 512 + tc * 128:ec * 512 + (tc + 1) * 128],
                                k["ident_f"][:, :]).then_inc(p.h["pe"], 1)

            @blk.scalar
            def _(s):
                for i in range(NB):
                    s.wait_ge(p.h["pe"], pe0 + i + 1)
                    if i >= 4:
                        s.wait_ge(p.h[fs[i % 4]], fs0[i % 4] + 16 * (i // 4))
                    s.copy(stg[i % 4][:, :], trp[i % 2][:, 0:128]).then_inc(p.h["sc"], 1)

            @blk.sync
            def _(sy, half=half):
                for i in range(NB):
                    ec, tc = divmod(i, 4)
                    sy.wait_ge(p.h["sc"], sc0 + i + 1)
                    sy.dma_start(
                        out=o_y[half * 512 + tc * 128:half * 512 + (tc + 1) * 128,
                                ec * 128:(ec + 1) * 128],
                        in_=stg[i % 4][:, :]).then_inc(p.h[fs[i % 4]], 16)
                for j in range(4):
                    sy.wait_ge(p.h[fs[j]], fs0[j] + 16 * (NB // 4))
        p.c["pe"] = pe0 + NB
        p.c["sc"] = sc0 + NB
        for j in range(4):
            p.c[fs[j]] += 16 * (NB // 4)


def _emit_cc(p, kind, groups, in_ap, out_ap, src_pairs, ccname, ici=False):
    ld0 = p.c["ld"]
    n = len(src_pairs)
    with p.nc.Block() as blk:
        @blk.gpsimd
        def _(g):
            for dst, sb in src_pairs:
                g.dma_start(out=dst, in_=sb).then_inc(p.h["ld"], 16)
            g.wait_ge(p.h["ld"], ld0 + 16 * n)
            if ici:
                with allow_ici_all_to_all():
                    g.collective_compute(kind, ALU.bypass, replica_groups=groups,
                                         ins=[in_ap], outs=[out_ap]).then_inc(p.h[ccname])
            else:
                g.collective_compute(kind, ALU.bypass, replica_groups=groups,
                                     ins=[in_ap], outs=[out_ap]).then_inc(p.h[ccname])
            g.wait_ge(p.h[ccname], 1)
    p.c["ld"] = ld0 + 16 * n
    p.c[ccname] = 1


def build_fused():
    p = P()
    nc = p.nc
    io = {
        "ident": p.dram_in("ident", [128, 128], F32),
        "x": p.dram_in("x", [TPC, D], F32),
        "l0_wk_a": p.dram_in("l0_wk_a", [NEC, NEG, 128, 512], F32),
        "l0_wv_a": p.dram_in("l0_wv_a", [NEC, NEG, 128, 512], F32),
        "l0_wq_a": p.dram_in("l0_wq_a", [NEC, NEG, 128, 512], F32),
        "l0_bq_r": p.dram_in("l0_bq_r", [1, D], F32),
        "l0_bk_r": p.dram_in("l0_bk_r", [1, D], F32),
        "l0_bv_r": p.dram_in("l0_bv_r", [1, D], F32),
        "ropecq": p.dram_in("ropecq", [TPC, TPC], F32),
        "ropesq": p.dram_in("ropesq", [TPC, TPC], F32),
        "ropeck": p.dram_in("ropeck", [TPC, TPC], F32),
        "ropesk": p.dram_in("ropesk", [TPC, TPC], F32),
        "l0_wo_b": p.dram_in("l0_wo_b", [NEC, NEC, 128, 128], F32),
        "l0_bo_pp": p.dram_in("l0_bo_pp", [128, NEC], F32),
        "f0_wg": p.dram_in("f0_wg", [NIC // 2, 2 * NEC, 128, 128], I8),
        "f0_wu": p.dram_in("f0_wu", [NIC // 2, 2 * NEC, 128, 128], I8),
        "f0_wd": p.dram_in("f0_wd", [NEC, NIC, 128, 128], I8),
        "f0_sg": p.dram_in("f0_sg", [128, NIC], F32),
        "f0_su": p.dram_in("f0_su", [128, NIC], F32),
        "f0_sd": p.dram_in("f0_sd", [128, NEC], F32),
        "l1_wq_b": p.dram_in("l1_wq_b", [NEC, NEC, 128, 128], F32),
        "l1_wk_a": p.dram_in("l1_wk_a", [NEC, NEG, 128, 512], F32),
        "l1_wv_a": p.dram_in("l1_wv_a", [NEC, NEG, 128, 512], F32),
        "l1_wo_b": p.dram_in("l1_wo_b", [NEC, NEC, 128, 128], F32),
        "mask_pp": p.dram_in("mask_pp", [128, NRANK], F32),
        "sel_pp": p.dram_in("sel_pp", [128, 2], F32),
        "f1_wg": p.dram_in("f1_wg", [NIC // 2, 2 * NEC, 128, 128], I8),
        "f1_wu": p.dram_in("f1_wu", [NIC // 2, 2 * NEC, 128, 128], I8),
        "f1_wd": p.dram_in("f1_wd", [NEC, NIC, 128, 128], I8),
        "f1_sg": p.dram_in("f1_sg", [128, NIC], F32),
        "f1_su": p.dram_in("f1_su", [128, NIC], F32),
        "f1_sd": p.dram_in("f1_sd", [128, NEC], F32),
    }
    o_y = p.dram_out("o_y", [TPC, D], F32)
    cc1i = [nc.dram_tensor(f"cc1i{t}", [2 * NRANK * 128, 512], BF16) for t in range(NTT)]
    cc1o = [nc.dram_tensor(f"cc1o{t}", [2 * NRANK * 128, 512], BF16) for t in range(NTT)]
    cc2_in = nc.dram_tensor("cc2_in", [128, NEC], F32)
    cc2_out = nc.dram_tensor("cc2_out", [NRANK * 128, NEC], F32)
    groups = [[0, 1, 2, 3], [4, 5, 6, 7]]

    with p.ctx:
        k = _consts(p)
        stream = p.sb("stream", [128, NEC * TPC], BF16)
        _emit_init(p, k, io)
        _emit_load_stream(p, k, io["x"], stream)
        with ExitStack() as sA:
            E = sA.enter_context
            q_tok = E(nc.sbuf_tensor("q_tok", [128, NTT * D], BF16))
            k_tok = E(nc.sbuf_tensor("k_tok", [128, NTT * D], BF16))
            v_tok = E(nc.sbuf_tensor("v_tok", [128, NTT * D], BF16))
            _emit_qkv(p, k, io, stream,
                      outs={"k": k_tok, "v": v_tok, "q": q_tok},
                      wnames={"k": "l0_wk_a", "v": "l0_wv_a", "q": "l0_wq_a"},
                      biases={"k": "l0_bk_r", "v": "l0_bv_r", "q": "l0_bq_r"},
                      rope={"q", "k"})
            attn_fm = E(nc.sbuf_tensor("attn_fm", [128, NHEAD * TPC], BF16))
            _emit_l0_attn(p, k, io, {"q_tok": q_tok, "k_tok": k_tok,
                                     "v_tok": v_tok, "attn_fm": attn_fm},
                          cc=(cc1i, cc1o, [list(range(2 * NRANK))]))
        with ExitStack() as sB:
            E = sB.enter_context
            a2abuf = E(nc.sbuf_tensor("a2abuf", [128, 2 * NEC * TPC], BF16))
            out_scrT = E(nc.sbuf_tensor("out_scrT", [128, NEC * TPC], BF16))
            sel_sb = E(nc.sbuf_tensor("sel_sb", [128, 2], F32))
            _emit_load_sb(p, [(sel_sb[:, :], io["sel_pp"])])
            _emit_scramble8(p, {}, a2abuf, out_scrT, sel_sb, cc1o=cc1o)
            sn = _ln_scratch(p, sB)
            _emit_proj_b(p, k, io, "l0_wo_b", out_scrT, stream, "resid_ln",
                         bias_pp="l0_bo_pp", sn=sn)
        with ExitStack() as sC:
            fuseT = sC.enter_context(nc.sbuf_tensor(p.uname("fuseT"), [128, NIC * 512], BF16))
            _emit_ffn(p, k, io, stream, "f0_wg", "f0_wu", "f0_wd",
                      "f0_sg", "f0_su", "f0_sd", fuseT)
        with ExitStack() as sD:
            E = sD.enter_context
            qT1 = E(nc.sbuf_tensor("qT1", [128, NEC * TPC], BF16))
            kvT = E(nc.sbuf_tensor("kvT", [128, NEC * TPC], BF16))
            blocksums = E(nc.sbuf_tensor("blocksums", [128, NEC], F32))
            with ExitStack() as sD1:
                k_tok2 = sD1.enter_context(nc.sbuf_tensor(p.uname("k_tok2"), [128, NTT * D], BF16))
                v_tok2 = sD1.enter_context(nc.sbuf_tensor(p.uname("v_tok2"), [128, NTT * D], BF16))
                _emit_proj_b(p, k, io, "l1_wq_b", stream, None, "copy", dst=qT1)
                _emit_qkv(p, k, io, stream,
                          outs={"k": k_tok2, "v": v_tok2},
                          wnames={"k": "l1_wk_a", "v": "l1_wv_a"},
                          biases={}, rope=set())
                _emit_kv_transpose(p, k, k_tok2, v_tok2, kvT)
            kvt_loc = E(nc.sbuf_tensor("kvt_loc", [128, NEC * TPC], BF16))
            _emit_scan(p, k, kvT, kvt_loc, blocksums)
            _emit_cc(p, "AllGather", groups, cc2_in.ap(), cc2_out.ap(),
                     [(cc2_in.ap(), blocksums[:, :])], "cc2")
            vhat = E(nc.sbuf_tensor("vhat", [128, NEC * TPC], BF16))
            _emit_l1b_vhat(p, k, {"bs_all": cc2_out.ap().rearrange("(r p) c -> r p c", p=128),
                                  "mask_pp": io["mask_pp"]}, qT1, kvt_loc, vhat)
            sn = _ln_scratch(p, sD)
            _emit_proj_b(p, k, io, "l1_wo_b", vhat, stream, "resid_ln", sn=sn)
        with ExitStack() as sF:
            fuseT2 = sF.enter_context(nc.sbuf_tensor(p.uname("fuseT"), [128, NIC * 512], BF16))
            xTf32 = sF.enter_context(nc.sbuf_tensor(p.uname("xTf32"), [128, NEC * 512], F32))
            _emit_ffn(p, k, io, stream, "f1_wg", "f1_wu", "f1_wd",
                      "f1_sg", "f1_su", "f1_sd", fuseT2, out_f32=xTf32, o_y=o_y)
    return p.nc


def _fused_inmap(sh, pc):
    m = {"ident": sh["ident"], "x": pc["x"], "mask_pp": pc["mask_pp"],
         "sel_pp": pc["sel_pp"],
         "ropecq": pc["ropecq"], "ropesq": pc["ropesq"],
         "ropeck": pc["ropeck"], "ropesk": pc["ropesk"]}
    for n in ("l0_wq_a", "l0_wk_a", "l0_wv_a", "l0_wo_b", "l0_bq_r", "l0_bk_r",
              "l0_bv_r", "l0_bo_pp", "f0_wg", "f0_wu", "f0_wd", "f0_sg", "f0_su",
              "f0_sd", "l1_wq_b", "l1_wk_a", "l1_wv_a", "l1_wo_b", "f1_wg",
              "f1_wu", "f1_wd", "f1_sg", "f1_su", "f1_sd"):
        m[n] = sh[n]
    return m


def kernel_fused(**inputs):
    sh, percore = _stage(inputs)
    cores = list(range(NCORE))
    res = run_bass_kernel_spmd(build_fused(), [_fused_inmap(sh, percore[c]) for c in cores],
                               core_ids=cores)
    out = np.zeros((2, SEQ, D), dtype=np.float32)
    for c in cores:
        b, sl = divmod(c, NRANK)
        out[b, sl * TPC:(sl + 1) * TPC, :] = np.asarray(res.results[c]["o_y"])
    return out


def kernel_unfused(**inputs):
    sh, percore = _stage(inputs)
    cores = list(range(NCORE))
    run = lambda nc, maps: run_bass_kernel_spmd(nc, maps, core_ids=cores)
    col = lambda res, key: [np.asarray(r[key]) for r in res.results]

    res = run(build_l0a(), [_l0a_inmap(sh, percore[c]) for c in cores])
    streams = col(res, "o_stream")
    attns = col(res, "o_attn")

    maps = []
    for c in cores:
        b, sl = divmod(c, NRANK)
        maps.append({"ident": sh["ident"], "a2a": _host_a2a(attns, b, sl),
                     "sel_pp": percore[c]["sel_pp"],
                     "stream_in": streams[c], "l0_wo_b": sh["l0_wo_b"],
                     "l0_bo_pp": sh["l0_bo_pp"]})
    streams = col(run(build_l0b(), maps), "o_stream")

    maps = [{"ident": sh["ident"], "stream_in": streams[c],
             "wg": sh["f0_wg"], "wu": sh["f0_wu"], "wd": sh["f0_wd"],
             "sg": sh["f0_sg"], "su": sh["f0_su"], "sd": sh["f0_sd"]} for c in cores]
    streams = col(run(build_ffn(0), maps), "o_stream")

    maps = [{"ident": sh["ident"], "stream_in": streams[c],
             "l1_wq_b": sh["l1_wq_b"], "l1_wk_a": sh["l1_wk_a"],
             "l1_wv_a": sh["l1_wv_a"]} for c in cores]
    res = run(build_l1a(), maps)
    qs, kvts, bss = col(res, "o_q"), col(res, "o_kvt"), col(res, "o_bs")

    maps = []
    for c in cores:
        b, sl = divmod(c, NRANK)
        bs_all = np.ascontiguousarray(np.stack([bss[b * NRANK + r] for r in range(NRANK)]))
        maps.append({"ident": sh["ident"], "stream_in": streams[c], "q_in": qs[c],
                     "kvt_in": kvts[c], "bs_all": bs_all,
                     "mask_pp": percore[c]["mask_pp"], "l1_wo_b": sh["l1_wo_b"]})
    streams = col(run(build_l1b(), maps), "o_stream")

    maps = [{"ident": sh["ident"], "stream_in": streams[c],
             "wg": sh["f1_wg"], "wu": sh["f1_wu"], "wd": sh["f1_wd"],
             "sg": sh["f1_sg"], "su": sh["f1_su"], "sd": sh["f1_sd"]} for c in cores]
    ys = col(run(build_ffn(1, final=True), maps), "o_y")

    out = np.zeros((2, SEQ, D), dtype=np.float32)
    for c in cores:
        b, sl = divmod(c, NRANK)
        out[b, sl * TPC:(sl + 1) * TPC, :] = ys[c]
    return out


def kernel(**inputs):
    return kernel_fused(**inputs)


# version 12
# speedup vs baseline: 1.0608x; 1.0608x over previous
import numpy as np
import ml_dtypes
from contextlib import ExitStack
import concourse.bass as bass
import concourse.mybir as mybir
from concourse.bass_utils import run_bass_kernel_spmd
from concourse.aot_env import allow_ici_all_to_all

F32 = mybir.dt.float32
BF16 = mybir.dt.bfloat16
I8 = mybir.dt.int8
AF = mybir.ActivationFunctionType
ALU = mybir.AluOpType
AX = mybir.AxisListType
BFNP = ml_dtypes.bfloat16

NCORE = 8
TPC = 1024
NTT = TPC // 128
D = 2048
NEC = D // 128
NEG = 4
INTER = 8192
NIC = INTER // 128
NHEAD = 16
HD = 128
SEQ = 4096
NRANK = 4
INV_SQRT_HD = 1.0 / float(np.sqrt(np.float32(HD)))
L1_DENOM = float(np.float32(np.sqrt(np.float32(D))) / np.float32(SEQ))
LN_EPS = 1e-5



def _tile_b(w):
    K, M = w.shape
    return np.ascontiguousarray(w.reshape(K // 128, 128, M // 128, 128).transpose(2, 0, 1, 3))


def _tile_a(w):
    K, M = w.shape
    return np.ascontiguousarray(w.reshape(K // 128, 128, M // 512, 512).transpose(0, 2, 1, 3))


def _pp(vec):
    n = vec.shape[0] // 128
    return np.ascontiguousarray(vec.reshape(n, 128).T)


def _rope_tables(sl, scale):
    inv = (1.0 / (10000.0 ** (np.arange(0, HD, 2, dtype=np.float32) / np.float32(HD)))).astype(np.float32)
    pos = np.arange(sl * TPC, (sl + 1) * TPC, dtype=np.float32)
    fr = np.outer(pos, inv).astype(np.float32)
    cos = (np.cos(fr) * scale).astype(np.float32)
    sin = (np.sin(fr) * scale).astype(np.float32)
    rep = lambda t: np.ascontiguousarray(np.tile(t, (1, NHEAD)))
    return rep(cos), rep(sin)


def _stage(inputs):
    sh = {}
    sh["ident"] = np.eye(128, dtype=np.float32)
    T = lambda a: np.ascontiguousarray(np.asarray(a).T)
    sh["l0_wq_a"] = _tile_a(T(inputs["L0_attn_Wq"]))
    sh["l0_wk_a"] = _tile_a(T(inputs["L0_attn_Wk"]))
    sh["l0_wv_a"] = _tile_a(T(inputs["L0_attn_Wv"]))
    sh["l0_wo_b"] = _tile_b(T(inputs["L0_attn_Wo"]))
    sh["l0_bq_r"] = np.ascontiguousarray(np.asarray(inputs["L0_attn_bq"]).reshape(1, D))
    sh["l0_bk_r"] = np.ascontiguousarray(np.asarray(inputs["L0_attn_bk"]).reshape(1, D))
    sh["l0_bv_r"] = np.ascontiguousarray(np.asarray(inputs["L0_attn_bv"]).reshape(1, D))
    sh["l0_bo_pp"] = _pp(np.asarray(inputs["L0_attn_bo"]))
    for l in (0, 1):
        sh[f"f{l}_wg"] = np.ascontiguousarray(
            _tile_b(np.asarray(inputs[f"L{l}_ffn_w_gate"])).reshape(NIC // 2, 2 * NEC, 128, 128))
        sh[f"f{l}_wu"] = np.ascontiguousarray(
            _tile_b(np.asarray(inputs[f"L{l}_ffn_w_up"])).reshape(NIC // 2, 2 * NEC, 128, 128))
        sh[f"f{l}_wd"] = _tile_b(np.asarray(inputs[f"L{l}_ffn_w_down"]))
        sh[f"f{l}_sg"] = _pp(np.asarray(inputs[f"L{l}_ffn_s_gate"]))
        sh[f"f{l}_su"] = _pp(np.asarray(inputs[f"L{l}_ffn_s_up"]))
        sh[f"f{l}_sd"] = _pp(np.asarray(inputs[f"L{l}_ffn_s_down"]))
    sh["l1_wq_b"] = _tile_b(T(inputs["L1_attn_Wq"]))
    sh["l1_wk_a"] = _tile_a(T(inputs["L1_attn_Wk"]))
    sh["l1_wv_a"] = _tile_a(T(inputs["L1_attn_Wv"]))
    sh["l1_wo_b"] = _tile_b(T(inputs["L1_attn_Wo"]))

    x = np.asarray(inputs["L0_attn_x"])
    percore = []
    for c in range(NCORE):
        b, sl = divmod(c, NRANK)
        pc = {"x": np.ascontiguousarray(x[b, sl * TPC:(sl + 1) * TPC, :])}
        pc["ropecq"], pc["ropesq"] = _rope_tables(sl, INV_SQRT_HD)
        pc["ropeck"], pc["ropesk"] = _rope_tables(sl, 1.0)
        pc["mask_pp"] = np.ascontiguousarray(
            np.broadcast_to((np.arange(NRANK) < sl).astype(np.float32)[None, :], (128, NRANK)).copy())
        sel = np.array([1.0, 0.0] if b == 0 else [0.0, 1.0], dtype=np.float32)
        pc["sel_pp"] = np.ascontiguousarray(np.broadcast_to(sel[None, :], (128, 2)).copy())
        percore.append(pc)
    return sh, percore



class P:

    def __init__(self):
        self.nc = bass.Bass("TRN2", target_bir_lowering=False, debug=False)
        self.ctx = ExitStack()
        self.h = {}
        self.c = {}
        for n in ("pe", "vc", "sc", "pl", "ld", "lwa", "lwb", "lwc", "lwd", "sd",
                  "cc1", "cc2", "fsa", "fsb", "fsc", "fsd"):
            self.h[n] = self.ctx.enter_context(self.nc.semaphore(n))
            self.c[n] = 0

    _n = 0

    def uname(self, base):
        P._n += 1
        return f"{base}_{P._n}"

    def sb(self, name, shape, dtype):
        return self.ctx.enter_context(self.nc.sbuf_tensor(name, shape, dtype))

    def dram_in(self, name, shape, dtype):
        return self.nc.dram_tensor(name, list(shape), dtype, kind="ExternalInput").ap()

    def dram_out(self, name, shape, dtype):
        return self.nc.dram_tensor(name, list(shape), dtype, kind="ExternalOutput").ap()


def _consts(p):
    k = {
        "ident_b": p.sb("ident_b", [128, 128], BF16),
        "ident_f": p.sb("ident_f", [128, 128], F32),
        "ones_row_b": p.sb("ones_row_b", [1, 128], BF16),
        "ones_row_f": p.sb("ones_row_f", [1, 128], F32),
        "ones_col_b": p.sb("ones_col_b", [128, 1], BF16),
        "eps11": p.sb("eps11", [1, 1], F32),
    }
    return k


def _emit_init(p, k, io):
    ld0 = p.c["ld"]
    with p.nc.Block() as blk:
        @blk.gpsimd
        def _(g):
            g.dma_start(out=k["ident_b"][:, :], in_=io["ident"]).then_inc(p.h["ld"], 16)
            g.dma_start(out=k["ident_f"][:, :], in_=io["ident"]).then_inc(p.h["ld"], 16)
            g.memset(k["ones_row_b"][:, :], 1.0)
            g.memset(k["ones_row_f"][:, :], 1.0)
            g.memset(k["eps11"][:, :], LN_EPS)
            g.memset(k["ones_col_b"][:, :], 1.0).then_inc(p.h["pl"], 1)
            g.wait_ge(p.h["ld"], ld0 + 32)
    p.c["ld"] = ld0 + 32
    p.c["pl"] += 1


def _emit_load_stream(p, k, x_ap, stream):
    nc = p.nc
    with ExitStack() as ctx:
        E = ctx.enter_context
        xr = E(nc.sbuf_tensor(p.uname("ls_xr"), [128, D], BF16))
        trp = [E(nc.psum_tensor(p.uname(f"ls_tr{i}"), [128, 1024], BF16)) for i in range(4)]
        for tc in range(NTT):
            ld0, pe0, vc0 = p.c["ld"], p.c["pe"], p.c["vc"]
            with nc.Block() as blk:
                @blk.gpsimd
                def _(g, tc=tc):
                    g.dma_start(out=xr[:, :], in_=x_ap[tc * 128:(tc + 1) * 128, :]).then_inc(p.h["ld"], 16)

                @blk.tensor
                def _(t, tc=tc, ld0=ld0, vc0=vc0):
                    t.wait_ge(p.h["ld"], ld0 + 16)
                    t.wait_ge(p.h["pl"], 1)
                    for ec in range(NEC):
                        if ec >= 4:
                            t.wait_ge(p.h["vc"], vc0 + ec - 3)
                        t.transpose(trp[ec % 4][:, 0:128], xr[:, ec * 128:(ec + 1) * 128],
                                    k["ident_b"][:, :]).then_inc(p.h["pe"], 1)

                @blk.vector
                def _(v, tc=tc, pe0=pe0):
                    for ec in range(NEC):
                        v.wait_ge(p.h["pe"], pe0 + ec + 1)
                        v.tensor_copy(stream[:, ec * TPC + tc * 128: ec * TPC + (tc + 1) * 128],
                                      trp[ec % 4][:, 0:128]).then_inc(p.h["vc"], 1)
            p.c["ld"] = ld0 + 16
            p.c["pe"] = pe0 + NEC
            p.c["vc"] = vc0 + NEC


def _emit_qkv(p, k, io, stream, outs, wnames, biases, rope, l1=False):
    nc = p.nc
    names = list(outs.keys())
    with ExitStack() as ctx:
        E = ctx.enter_context
        wrhs = [E(nc.sbuf_tensor(p.uname(f"qk_w{i}"), [128, NEC * 512], BF16)) for i in range(2)]
        bias_r = E(nc.sbuf_tensor(p.uname("qk_bias"), [1, D], BF16))
        if rope:
            tab_c = E(nc.sbuf_tensor(p.uname("qk_tc"), [128, NTT * 256], F32))
            tab_s = E(nc.sbuf_tensor(p.uname("qk_ts"), [128, NTT * 256], F32))
            qtmp = [E(nc.sbuf_tensor(p.uname(f"qk_t{i}"), [128, 512], F32)) for i in range(4)]
            m1 = E(nc.sbuf_tensor(p.uname("qk_m1"), [128, 256], F32))
            m2 = E(nc.sbuf_tensor(p.uname("qk_m2"), [128, 256], F32))
            m3 = E(nc.sbuf_tensor(p.uname("qk_m3"), [128, 256], F32))
            m4 = E(nc.sbuf_tensor(p.uname("qk_m4"), [128, 256], F32))
        acc = [E(nc.psum_tensor(p.uname(f"qk_ps{i}"), [128, 512], F32)) for i in range(4)]

        lwsem = ("lwa", "lwb")
        with nc.Block() as blk:
            @blk.gpsimd
            def _(g):
                g.dma_start(out=wrhs[0][:, :].rearrange("p (d e) -> p d e", d=NEC),
                            in_=io[wnames[names[0]]][:, 0, :, :].rearrange("d p e -> p d e")).then_inc(p.h["lwa"], 16)
        p.c["lwa"] += 16

        for ti, tname in enumerate(names):
            do_rope = tname in rope
            has_bias = biases.get(tname) is not None
            dst = outs[tname]
            if do_rope:
                ctab = io["ropecq"] if tname == "q" else io["ropeck"]
                stab = io["ropesq"] if tname == "q" else io["ropesk"]
            for eg in range(NEG):
                wslot = (ti * NEG + eg) % 2
                ld0, pe0, vc0, pl0 = (p.c["ld"], p.c["pe"], p.c["vc"], p.c["pl"])
                wsem = lwsem[wslot]
                lw_need = p.c[wsem]
                n_tab = 2 if do_rope else 0
                has_pref = not (ti == len(names) - 1 and eg == NEG - 1)
                sd0 = p.c["sd"]
                load_bias = has_bias and eg == 0
                with nc.Block() as blk:
                    @blk.gpsimd
                    def _(g, ti=ti, eg=eg, wslot=wslot, has_bias=has_bias,
                          tname=tname, has_pref=has_pref, load_bias=load_bias):
                        if load_bias:
                            g.dma_start(out=bias_r[:, :], in_=io[biases[tname]]).then_inc(p.h["ld"], 16)
                        if has_pref:
                            nt, ne = (ti, eg + 1) if eg + 1 < NEG else (ti + 1, 0)
                            g.dma_start(
                                out=wrhs[1 - wslot][:, :].rearrange("p (d e) -> p d e", d=NEC),
                                in_=io[wnames[names[nt]]][:, ne, :, :].rearrange("d p e -> p d e")).then_inc(p.h[lwsem[1 - wslot]], 16)

                    if do_rope:
                        @blk.sync
                        def _(sy, ti=ti, eg=eg, tname=tname):
                            sy.dma_start(
                                out=tab_c[:, :].rearrange("p (t e) -> p t e", t=NTT),
                                in_=(io["ropecq"] if tname == "q" else io["ropeck"])
                                .rearrange("(t p) e -> p t e", p=128)[:, :, eg * 256:(eg + 1) * 256],
                            ).then_inc(p.h["sd"], 16)
                            sy.dma_start(
                                out=tab_s[:, :].rearrange("p (t e) -> p t e", t=NTT),
                                in_=(io["ropesq"] if tname == "q" else io["ropesk"])
                                .rearrange("(t p) e -> p t e", p=128)[:, :, eg * 256:(eg + 1) * 256],
                            ).then_inc(p.h["sd"], 16)
                    n_dma = 1 if load_bias else 0
                    ld_data = ld0 + 16 * n_dma
                    sd_data = sd0 + (32 if do_rope else 0)

                    v_rope = 3 if do_rope else 0
                    v_per_sweep = 4 + 4 * v_rope if do_rope else 4
                    v_total = 2 * v_per_sweep

                    @blk.tensor
                    def _(t, eg=eg, wslot=wslot, vc0=vc0, ld_data=ld_data, lw_need=lw_need,
                          has_bias=has_bias, wsem=wsem):
                        t.wait_ge(p.h["ld"], ld_data)
                        t.wait_ge(p.h[wsem], lw_need)
                        for sw in range(2):
                            if sw == 1:
                                t.wait_ge(p.h["vc"], vc0 + 4)
                            for tt in range(4):
                                tc = sw * 4 + tt
                                for dc in range(NEC):
                                    mm = t.matmul(acc[tt][:, :],
                                                  stream[:, dc * TPC + tc * 128:dc * TPC + (tc + 1) * 128],
                                                  wrhs[wslot][:, dc * 512:(dc + 1) * 512],
                                                  start=(dc == 0),
                                                  stop=(not has_bias) and dc == NEC - 1)
                                    if (not has_bias) and dc == NEC - 1:
                                        mm.then_inc(p.h["pe"], 1)
                                if has_bias:
                                    t.matmul(acc[tt][:, :], k["ones_row_b"][:, :],
                                             bias_r[:, eg * 512:(eg + 1) * 512],
                                             start=False, stop=True).then_inc(p.h["pe"], 1)

                    if do_rope:
                        @blk.vector
                        def _(v, eg=eg, pe0=pe0, pl0=pl0, vc0=vc0, ld_data=ld_data, dst=dst,
                              sd_data=sd_data):
                            v.wait_ge(p.h["ld"], ld_data)
                            v.wait_ge(p.h["sd"], sd_data)
                            vcc = vc0
                            for sw in range(2):
                                for tt in range(4):
                                    v.wait_ge(p.h["pe"], pe0 + sw * 4 + tt + 1)
                                    if sw == 1:
                                        v.wait_ge(p.h["pl"], pl0 + tt * 3 + 2)
                                        v.wait_ge(p.h["vc"], vcc)
                                    v.tensor_copy(qtmp[tt][:, :], acc[tt][:, :]).then_inc(p.h["vc"], 1)
                                    vcc += 1
                                for tt in range(4):
                                    tc = sw * 4 + tt
                                    ts = qtmp[tt]
                                    x0 = ts[:, 0:512:2]
                                    x1 = ts[:, 1:512:2]
                                    tcs = tab_c[:, tc * 256:(tc + 1) * 256]
                                    plb = pl0 + sw * 12 + tt * 3
                                    dste = dst[:, tc * D + eg * 512 + 0: tc * D + (eg + 1) * 512: 2]
                                    v.wait_ge(p.h["vc"], vcc)
                                    v.tensor_tensor(m1[:, :], x0, tcs, op=ALU.mult).then_inc(p.h["vc"], 1)
                                    vcc += 1
                                    v.wait_ge(p.h["pl"], plb + 1)
                                    v.wait_ge(p.h["vc"], vcc)
                                    v.tensor_tensor(dste, m1[:, :], m2[:, :],
                                                    op=ALU.subtract).then_inc(p.h["vc"], 1)
                                    vcc += 1
                                    v.tensor_tensor(m4[:, :], x1, tcs, op=ALU.mult).then_inc(p.h["vc"], 1)
                                    vcc += 1

                        @blk.gpsimd
                        def _(g, eg=eg, pl0=pl0, vc0=vc0, ld_data=ld_data, dst=dst,
                              v_per_sweep=v_per_sweep, sd_data=sd_data):
                            g.wait_ge(p.h["ld"], ld_data)
                            g.wait_ge(p.h["sd"], sd_data)
                            plc = pl0
                            for sw in range(2):
                                for tt in range(4):
                                    tc = sw * 4 + tt
                                    ts = qtmp[tt]
                                    x0 = ts[:, 0:512:2]
                                    x1 = ts[:, 1:512:2]
                                    tss = tab_s[:, tc * 256:(tc + 1) * 256]
                                    vb = vc0 + sw * v_per_sweep + 4 + tt * 3
                                    evb = vc0 + sw * v_per_sweep + tt + 1
                                    dsto = dst[:, tc * D + eg * 512 + 1: tc * D + (eg + 1) * 512: 2]
                                    g.wait_ge(p.h["vc"], evb)
                                    g.wait_ge(p.h["pl"], plc)
                                    g.tensor_tensor(m2[:, :], x1, tss, op=ALU.mult).then_inc(p.h["pl"], 1)
                                    plc += 1
                                    g.tensor_tensor(m3[:, :], x0, tss, op=ALU.mult).then_inc(p.h["pl"], 1)
                                    plc += 1
                                    g.wait_ge(p.h["vc"], vb + 3)
                                    g.wait_ge(p.h["pl"], plc)
                                    g.tensor_tensor(dsto, m3[:, :], m4[:, :],
                                                    op=ALU.add).then_inc(p.h["pl"], 1)
                                    plc += 1
                    else:
                        @blk.vector
                        def _(v, eg=eg, pe0=pe0, dst=dst):
                            for sw in range(2):
                                for tt in range(4):
                                    tc = sw * 4 + tt
                                    v.wait_ge(p.h["pe"], pe0 + sw * 4 + tt + 1)
                                    v.tensor_copy(dst[:, tc * D + eg * 512:tc * D + (eg + 1) * 512],
                                                  acc[tt][:, :]).then_inc(p.h["vc"], 1)

                p.c["ld"] = ld0 + 16 * n_dma
                p.c["sd"] = sd0 + (32 if do_rope else 0)
                if has_pref:
                    p.c[lwsem[1 - wslot]] += 16
                p.c["pe"] = pe0 + 8
                p.c["vc"] = vc0 + v_total
                p.c["pl"] = pl0 + (24 if do_rope else 0)


def _emit_l0_attn(p, k, io, bufs, cc=None):
    nc = p.nc
    q_tok, k_tok, v_tok, attn_fm = bufs["q_tok"], bufs["k_tok"], bufs["v_tok"], bufs["attn_fm"]
    NPV = 12
    with ExitStack() as ctx:
        E = ctx.enter_context
        prodS = [E(nc.sbuf_tensor(p.uname(f"at_ps{i}"), [128, 2048], BF16)) for i in range(2)]
        prodP = [E(nc.sbuf_tensor(p.uname(f"at_pp{i}"), [128, 2048], BF16)) for i in range(2)]
        prodV = E(nc.sbuf_tensor(p.uname("at_pv"), [128, 4 * 2048], BF16))
        sc = [E(nc.sbuf_tensor(p.uname(f"at_sc{i}"), [128, 256], F32)) for i in range(2)]
        ex = [E(nc.sbuf_tensor(p.uname(f"at_ex{i}"), [128, 256], F32)) for i in range(2)]
        ssum = [E(nc.sbuf_tensor(p.uname(f"at_ss{i}"), [128, 16], F32)) for i in range(2)]
        rcp = [E(nc.sbuf_tensor(p.uname(f"at_rc{i}"), [128, 16], F32)) for i in range(2)]
        pn = [E(nc.sbuf_tensor(p.uname(f"at_pn{i}"), [128, 256], F32)) for i in range(2)]
        otok = [E(nc.sbuf_tensor(p.uname(f"at_ot{i}"), [128, 2048], BF16)) for i in range(2)]
        trp = [E(nc.psum_tensor(p.uname(f"at_tr{i}"), [128, 1024], BF16)) for i in range(2)]

        pe0, vc0, pl0, sc0, ld0, cc0 = (p.c["pe"], p.c["vc"], p.c["pl"], p.c["sc"],
                                        p.c["ld"], p.c["cc1"])
        vbase = [vc0]
        for tc in range(NTT):
            vbase.append(vbase[-1] + 35 + (16 if tc >= 1 else 0) + 4)
        vfinal = vbase[NTT] + 16
        sbase = [sc0]
        for tc in range(NTT):
            sbase.append(sbase[-1] + 1 + (16 if tc >= 1 else 0))
        sfinal = sbase[NTT] + 16

        def pncnt(tc):
            return vbase[tc] + 35

        def redcnt(tcp, h):
            if tcp < NTT - 1:
                return vbase[tcp + 1] + 35 + h + 1
            return vbase[NTT] + h + 1

        def expcnt(tc):
            return sbase[tc] + 1

        def cpycnt(tcp, h):
            if tcp < NTT - 1:
                return sbase[tcp + 1] + 1 + h + 1
            return sbase[NTT] + h + 1

        with nc.Block() as blk:
            @blk.vector
            def _(v):
                vcc = vc0

                def emit_pvreds(tcp):
                    nonlocal vcc
                    ot = otok[tcp % 2]
                    if tcp >= 2:
                        v.wait_ge(p.h["pe"], pe0 + (tcp - 2) * 16 + 16)
                    for h in range(NHEAD):
                        if h < NPV:
                            v.wait_ge(p.h["pl"], pl0 + tcp * NPV + h + 1)
                            pr = prodP[(tcp * NPV + h) % 2][:, :]
                        else:
                            v.wait_ge(p.h["vc"], vcc)
                            pr = prodV[:, (h - NPV) * 2048:(h - NPV + 1) * 2048]
                        with p.nc.allow_low_precision(reason="DVE reduce accumulates fp32; bf16 store"):
                            v.tensor_reduce(ot[:, h * 128:(h + 1) * 128],
                                            pr.rearrange("p (d h) -> p d h", d=128),
                                            axis=AX.X, op=ALU.add).then_inc(p.h["vc"], 1)
                        vcc += 1

                for tc in range(NTT):
                    s_ = sc[tc % 2]
                    e_ = ex[tc % 2]
                    if tc >= 2:
                        v.wait_ge(p.h["sc"], expcnt(tc - 2))
                    for h in range(NHEAD):
                        qv = bass.AP(q_tok, tc * D + h * 128, [[NTT * D, 128], [0, 16], [1, 128]])
                        kv = bass.AP(k_tok, tc * D, [[NTT * D, 128], [128, 16], [1, 128]])
                        v.wait_ge(p.h["vc"], vcc)
                        v.tensor_tensor(prodS[h % 2][:, :].rearrange("p (h d) -> p h d", h=16),
                                        qv, kv, op=ALU.mult).then_inc(p.h["vc"], 1)
                        vcc += 1
                        v.wait_ge(p.h["vc"], vcc)
                        v.tensor_reduce(s_[:, h * 16:(h + 1) * 16],
                                        prodS[h % 2][:, :].rearrange("p (h d) -> p h d", h=16),
                                        axis=AX.X, op=ALU.add).then_inc(p.h["vc"], 1)
                        vcc += 1
                    v.wait_ge(p.h["sc"], expcnt(tc))
                    v.tensor_reduce(ssum[tc % 2][:, :], e_[:, :].rearrange("p (h j) -> p h j", h=16),
                                    axis=AX.X, op=ALU.add).then_inc(p.h["vc"], 1)
                    vcc += 1
                    v.wait_ge(p.h["vc"], vcc)
                    v.reciprocal(rcp[tc % 2][:, :], ssum[tc % 2][:, :]).then_inc(p.h["vc"], 1)
                    vcc += 1
                    rcv = bass.AP(rcp[tc % 2], 0, [[16, 128], [1, 16], [0, 16]])
                    v.wait_ge(p.h["vc"], vcc)
                    if tc >= 2:
                        v.wait_ge(p.h["pl"], pl0 + (tc - 1) * NPV)
                    v.tensor_tensor(pn[tc % 2][:, :].rearrange("p (h j) -> p h j", h=16),
                                    e_[:, :].rearrange("p (h j) -> p h j", h=16),
                                    rcv, op=ALU.mult).then_inc(p.h["vc"], 1)
                    vcc += 1
                    if tc >= 1:
                        emit_pvreds(tc - 1)
                    for i in range(4):
                        h = NPV + i
                        pv = bass.AP(pn[tc % 2], h * 16, [[256, 128], [0, 128], [1, 16]])
                        vv = bass.AP(v_tok, tc * D, [[NTT * D, 128], [1, 128], [128, 16]])
                        v.wait_ge(p.h["vc"], vcc)
                        v.tensor_tensor(prodV[:, i * 2048:(i + 1) * 2048]
                                        .rearrange("p (d h) -> p d h", d=128),
                                        pv, vv, op=ALU.mult).then_inc(p.h["vc"], 1)
                        vcc += 1
                emit_pvreds(NTT - 1)

            @blk.scalar
            def _(s):
                def emit_copies(tcp):
                    for h in range(NHEAD):
                        s.wait_ge(p.h["pe"], pe0 + tcp * 16 + h + 1)
                        s.copy(attn_fm[:, h * TPC + tcp * 128:h * TPC + (tcp + 1) * 128],
                               trp[(tcp * 16 + h) % 2][:, 0:128]).then_inc(p.h["sc"], 1)
                for tc in range(NTT):
                    s.wait_ge(p.h["vc"], vbase[tc] + 32)
                    s.activation(ex[tc % 2][:, :], sc[tc % 2][:, :], AF.Exp).then_inc(p.h["sc"], 1)
                    if tc >= 1:
                        emit_copies(tc - 1)
                emit_copies(NTT - 1)

            @blk.gpsimd
            def _(g):
                plc = pl0
                ldc = ld0

                def emit_cc(tcp):
                    nonlocal ldc
                    cc1i, cc1o, groups = cc
                    g.wait_ge(p.h["sc"], cpycnt(tcp, 15))
                    for j in range(2 * NRANK):
                        g.dma_start(
                            out=cc1i[tcp].ap()[j * 128:(j + 1) * 128, :]
                            .rearrange("p (h s) -> p h s", h=4),
                            in_=bass.AP(attn_fm, 4 * (j % NRANK) * TPC + tcp * 128,
                                        [[NHEAD * TPC, 128], [TPC, 4], [1, 128]]),
                        ).then_inc(p.h["ld"], 16)
                    ldc += 16 * 2 * NRANK
                    g.wait_ge(p.h["ld"], ldc)
                    with allow_ici_all_to_all():
                        g.collective_compute(
                            "AllToAll", ALU.bypass, replica_groups=groups,
                            ins=[cc1i[tcp].ap()], outs=[cc1o[tcp].ap()]).then_inc(p.h["cc1"])

                for tc in range(NTT):
                    if cc is not None and tc >= 1:
                        emit_cc(tc - 1)
                    for h in range(NPV):
                        j = tc * NPV + h
                        w = pncnt(tc)
                        if j >= 2:
                            tcp, hp = divmod(j - 2, NPV)
                            w = max(w, redcnt(tcp, hp))
                        g.wait_ge(p.h["vc"], w)
                        g.wait_ge(p.h["pl"], plc)
                        pv = bass.AP(pn[tc % 2], h * 16, [[256, 128], [0, 128], [1, 16]])
                        vv = bass.AP(v_tok, tc * D, [[NTT * D, 128], [1, 128], [128, 16]])
                        g.tensor_tensor(prodP[j % 2][:, :].rearrange("p (d h) -> p d h", d=128),
                                        pv, vv, op=ALU.mult).then_inc(p.h["pl"], 1)
                        plc += 1
                if cc is not None:
                    emit_cc(NTT - 1)

            @blk.tensor
            def _(t):
                for tc in range(NTT):
                    ot = otok[tc % 2]
                    for h in range(NHEAD):
                        i = tc * 16 + h
                        t.wait_ge(p.h["vc"], redcnt(tc, h))
                        if i >= 2:
                            tcp, hp = divmod(i - 2, 16)
                            t.wait_ge(p.h["sc"], cpycnt(tcp, hp))
                        t.transpose(trp[i % 2][:, 0:128], ot[:, h * 128:(h + 1) * 128],
                                    k["ident_b"][:, :]).then_inc(p.h["pe"], 1)

        p.c["pe"] = pe0 + 16 * NTT
        p.c["vc"] = vfinal
        p.c["pl"] = pl0 + NPV * NTT
        p.c["sc"] = sfinal
        if cc is not None:
            p.c["ld"] = ld0 + 16 * 2 * NRANK * NTT
            p.c["cc1"] = cc0 + NTT


def _emit_store(p, aps):
    sd0 = p.c["sd"]
    n = len(aps)
    with p.nc.Block() as blk:
        @blk.sync
        def _(sy):
            for dram, sb in aps:
                sy.dma_start(out=dram, in_=sb).then_inc(p.h["sd"], 16)
            sy.wait_ge(p.h["sd"], sd0 + 16 * n)
    p.c["sd"] = sd0 + 16 * n



def _emit_scramble8(p, io, a2abuf8, out_scrT, sel_sb, cc1o=None):
    nc = p.nc
    ld0, vc0 = p.c["ld"], p.c["vc"]
    per_tc = cc1o is not None
    with nc.Block() as blk:
        @blk.gpsimd
        def _(g):
            if per_tc:
                for tc in range(NTT):
                    g.wait_ge(p.h["cc1"], tc + 1)
                    for gg in range(2 * NRANK):
                        g.dma_start(
                            out=bass.AP(a2abuf8, gg * 4096 + tc * 128,
                                        [[32 * TPC, 128], [TPC, 4], [1, 128]]),
                            in_=cc1o[tc].ap()[gg * 128:(gg + 1) * 128, :]
                            .rearrange("p (h c) -> p h c", h=4),
                        ).then_inc(p.h["ld"], 16)
            else:
                g.dma_start(
                    out=a2abuf8[:, :].rearrange("p (r f) -> p r f", r=2 * NRANK),
                    in_=io["a2a"].rearrange("(r p) f -> p r f", p=128),
                ).then_inc(p.h["ld"], 16)

        @blk.vector
        def _(v):
            vcc = vc0
            v.wait_ge(p.h["ld"], ld0 + 16 * (NTT * 2 * NRANK if per_tc else 1))
            for r in range(NRANK):
                for hl in range(4):
                    va = bass.AP(a2abuf8, (r * 4 + hl) * TPC,
                                 [[32 * TPC, 128], [1, 16], [16, 64]])
                    vb = bass.AP(a2abuf8, ((r + NRANK) * 4 + hl) * TPC,
                                 [[32 * TPC, 128], [1, 16], [16, 64]])
                    dstv = bass.AP(out_scrT, hl * 256 + r * 64,
                                   [[16 * TPC, 128], [1024, 16], [1, 64]])
                    v.wait_ge(p.h["vc"], vcc)
                    v.tensor_scalar(dstv, va, sel_sb[:, 0:1], None, op0=ALU.mult).then_inc(p.h["vc"], 1)
                    vcc += 1
                    v.wait_ge(p.h["vc"], vcc)
                    v.scalar_tensor_tensor(dstv, vb, sel_sb[:, 1:2], dstv,
                                           op0=ALU.mult, op1=ALU.add).then_inc(p.h["vc"], 1)
                    vcc += 1
    p.c["ld"] = ld0 + 16 * (NTT * 2 * NRANK if per_tc else 1)
    p.c["vc"] = vc0 + 32


def _ln_scratch(p, ctx, mbc=None, rbc=None, pool_split=True):
    E = ctx.enter_context
    nc = p.nc
    return {
        "mean": E(nc.sbuf_tensor(p.uname("ln_mean"), [1, 512], F32)),
        "msq": E(nc.sbuf_tensor(p.uname("ln_msq"), [1, 512], F32)),
        "var": E(nc.sbuf_tensor(p.uname("ln_var"), [1, 512], F32)),
        "sqt": E(nc.sbuf_tensor(p.uname("ln_sqt"), [1, 512], F32)),
        "rstd": E(nc.sbuf_tensor(p.uname("ln_rstd"), [1, 512], F32)),
        "tmp": E(nc.sbuf_tensor(p.uname("ln_tmp"), [128, 512], F32)),
        "tmp2": E(nc.sbuf_tensor(p.uname("ln_tmp2"), [128, 512], F32)) if pool_split else None,
        "mbs": E(nc.sbuf_tensor(p.uname("ln_mbs"), [128, 512], F32)) if pool_split else None,
        "rbs": E(nc.sbuf_tensor(p.uname("ln_rbs"), [128, 512], F32)) if pool_split else None,
        "mbc": mbc if mbc is not None else E(nc.psum_tensor(p.uname("ln_mbc"), [128, 512], F32)),
        "rbc": rbc if rbc is not None else E(nc.psum_tensor(p.uname("ln_rbc"), [128, 512], F32)),
    }


def _ln_finalize(p, k, stream, half, ps_s, ps_sq, sn, out_f32=None):
    pool_split = sn["mbs"] is not None
    """Finalize post-LN for a 512-token half from stats psums [1,512].

    Chunk normalization split between VectorE (ec 0-7, psum operands) and
    GpSimd (ec 8-15, from sbuf copies of the broadcasts).
    """
    nc = p.nc
    pe0, vc0, sc0, pl0 = p.c["pe"], p.c["vc"], p.c["sc"], p.c["pl"]
    with nc.Block() as blk:
        @blk.vector
        def _(v, half=half):
            vcc = vc0
            v.tensor_scalar(sn["mean"][:, :], ps_s[0:1, :], 1.0 / D, None, op0=ALU.mult).then_inc(p.h["vc"], 1)
            vcc += 1
            v.wait_ge(p.h["vc"], vcc)
            v.tensor_tensor(sn["msq"][:, :], sn["mean"][:, :], sn["mean"][:, :], op=ALU.mult).then_inc(p.h["vc"], 1)
            vcc += 1
            v.wait_ge(p.h["vc"], vcc)
            v.scalar_tensor_tensor(sn["var"][:, :], ps_sq[0:1, :], 1.0 / D, sn["msq"][:, :],
                                   op0=ALU.mult, op1=ALU.subtract).then_inc(p.h["vc"], 1)
            vcc += 1
            v.wait_ge(p.h["sc"], sc0 + 1)
            v.reciprocal(sn["rstd"][:, :], sn["sqt"][:, :]).then_inc(p.h["vc"], 1)
            vcc += 1
            v.wait_ge(p.h["pe"], pe0 + 2)
            if pool_split:
                v.wait_ge(p.h["sc"], sc0 + 3)
            for ec in range(8 if pool_split else NEC):
                sl_ = stream[:, ec * TPC + half * 512: ec * TPC + half * 512 + 512]
                v.wait_ge(p.h["vc"], vcc)
                v.tensor_tensor(sn["tmp"][:, :], sl_, sn["mbc"][:, :], op=ALU.subtract).then_inc(p.h["vc"], 1)
                vcc += 1
                v.wait_ge(p.h["vc"], vcc)
                if out_f32 is not None:
                    v.tensor_tensor(out_f32[:, ec * 512:(ec + 1) * 512],
                                    sn["tmp"][:, :], sn["rbc"][:, :], op=ALU.mult).then_inc(p.h["vc"], 1)
                    vcc += 1
                    v.wait_ge(p.h["vc"], vcc)
                v.tensor_tensor(sl_, sn["tmp"][:, :], sn["rbc"][:, :], op=ALU.mult).then_inc(p.h["vc"], 1)
                vcc += 1

        @blk.scalar
        def _(s):
            s.wait_ge(p.h["vc"], vc0 + 3)
            s.activation(sn["sqt"][:, :], sn["var"][:, :], AF.Sqrt, bias=k["eps11"][:, :]).then_inc(p.h["sc"], 1)
            if pool_split:
                s.wait_ge(p.h["pe"], pe0 + 2)
                s.copy(sn["mbs"][:, :], sn["mbc"][:, :]).then_inc(p.h["sc"], 1)
                s.copy(sn["rbs"][:, :], sn["rbc"][:, :]).then_inc(p.h["sc"], 1)

        @blk.gpsimd
        def _(g, half=half):
            if not pool_split:
                return
            plc = pl0
            g.wait_ge(p.h["sc"], sc0 + 3)
            for ec in range(8, NEC):
                sl_ = stream[:, ec * TPC + half * 512: ec * TPC + half * 512 + 512]
                g.wait_ge(p.h["pl"], plc)
                g.tensor_tensor(sn["tmp2"][:, :], sl_, sn["mbs"][:, :], op=ALU.subtract).then_inc(p.h["pl"], 1)
                plc += 1
                g.wait_ge(p.h["pl"], plc)
                if out_f32 is not None:
                    g.tensor_tensor(out_f32[:, ec * 512:(ec + 1) * 512],
                                    sn["tmp2"][:, :], sn["rbs"][:, :], op=ALU.mult).then_inc(p.h["pl"], 1)
                    plc += 1
                    g.wait_ge(p.h["pl"], plc)
                g.tensor_tensor(sl_, sn["tmp2"][:, :], sn["rbs"][:, :], op=ALU.mult).then_inc(p.h["pl"], 1)
                plc += 1

        @blk.tensor
        def _(t):
            t.wait_ge(p.h["vc"], vc0 + 1)
            t.matmul(sn["mbc"][:, :], k["ones_row_f"][:, :], sn["mean"][:, :],
                     start=True, stop=True).then_inc(p.h["pe"], 1)
            t.wait_ge(p.h["vc"], vc0 + 4)
            t.matmul(sn["rbc"][:, :], k["ones_row_f"][:, :], sn["rstd"][:, :],
                     start=True, stop=True).then_inc(p.h["pe"], 1)
    nf = 3 if out_f32 is not None else 2
    p.c["pe"] = pe0 + 2
    if pool_split:
        p.c["vc"] = vc0 + 4 + 8 * nf
        p.c["sc"] = sc0 + 3
        p.c["pl"] = pl0 + 8 * nf
    else:
        p.c["vc"] = vc0 + 4 + NEC * nf
        p.c["sc"] = sc0 + 1


def _emit_proj_b(p, k, io, wkey, rhs_buf, stream, mode, bias_pp=None, dst=None,
                 sn=None, out_f32=None, n_out=NEC, n_k=NEC):
    nc = p.nc
    with ExitStack() as ctx:
        E = ctx.enter_context
        wb = [E(nc.sbuf_tensor(p.uname(f"pb_w{i}"), [128, n_k * 128], BF16)) for i in range(2)]
        acc = [E(nc.psum_tensor(p.uname(f"pb_ps{i}"), [128, 512], F32)) for i in range(2)]
        wsems = ("lwa", "lwb")
        if mode == "resid_ln":
            ps_s = [E(nc.psum_tensor(p.uname(f"pb_s{i}"), [128, 512], F32)) for i in range(2)]
            ps_sq = [E(nc.psum_tensor(p.uname(f"pb_q{i}"), [128, 512], F32)) for i in range(2)]
            ysq = [E(nc.sbuf_tensor(p.uname(f"pb_y{i}"), [128, 512], BF16)) for i in range(2)]
            bias_sb = E(nc.sbuf_tensor(p.uname("pb_bias"), [128, NEC], F32)) if bias_pp else None
            if bias_pp:
                ld0 = p.c["ld"]
                with nc.Block() as blk:
                    @blk.gpsimd
                    def _(g):
                        g.dma_start(out=bias_sb[:, :], in_=io[bias_pp]).then_inc(p.h["ld"], 16)
                        g.wait_ge(p.h["ld"], ld0 + 16)
                p.c["ld"] = ld0 + 16
        with nc.Block() as blk:
            @blk.gpsimd
            def _(g):
                g.dma_start(out=wb[0][:, :].rearrange("p (c m) -> p c m", c=n_k),
                            in_=io[wkey][0].rearrange("c p m -> p c m")).then_inc(p.h["lwa"], 16)
        p.c["lwa"] += 16

        pe0, vc0 = p.c["pe"], p.c["vc"]
        resid = mode == "resid_ln"
        pe_pos = {}
        cnt = 0
        order = []
        for oc in range(n_out):
            order += [("c", oc, 0), ("c", oc, 1)]
            if resid and oc >= 1:
                for half in range(2):
                    order += [("s", oc - 1, half), ("q", oc - 1, half)]
        if resid:
            for half in range(2):
                order += [("s", n_out - 1, half), ("q", n_out - 1, half)]
        for it in order:
            cnt += 1
            pe_pos[it] = cnt
        vper = 4 if resid else 2
        cur = {w: p.c[w] for w in wsems}
        lw_cnt = {0: cur[wsems[0]]}
        for oc in range(1, n_out):
            cur[wsems[oc % 2]] += 16
            lw_cnt[oc] = cur[wsems[oc % 2]]

        with nc.Block() as blk:
            @blk.gpsimd
            def _(g):
                for oc in range(1, n_out):
                    if oc >= 2:
                        g.wait_ge(p.h["pe"], pe0 + pe_pos[("c", oc - 2, 1)])
                    g.dma_start(out=wb[oc % 2][:, :].rearrange("p (c m) -> p c m", c=n_k),
                                in_=io[wkey][oc].rearrange("c p m -> p c m")
                                ).then_inc(p.h[wsems[oc % 2]], 16)

            @blk.tensor
            def _(t):
                def stats(j):
                    for half in range(2):
                        t.wait_ge(p.h["vc"], vc0 + vper * j + 2 * half + 2)
                        t.matmul(ps_s[half][0:1, :], k["ones_col_b"][:, :],
                                 stream[:, j * TPC + half * 512: j * TPC + half * 512 + 512],
                                 start=(j == 0), stop=(j == n_out - 1)).then_inc(p.h["pe"], 1)
                        t.matmul(ps_sq[half][0:1, :], k["ones_col_b"][:, :],
                                 ysq[half][:, :],
                                 start=(j == 0), stop=(j == n_out - 1)).then_inc(p.h["pe"], 1)
                for oc in range(n_out):
                    t.wait_ge(p.h[wsems[oc % 2]], lw_cnt[oc])
                    for half in range(2):
                        if oc >= 1:
                            t.wait_ge(p.h["vc"], vc0 + vper * (oc - 1) + (2 * half if resid else half) + 1)
                        for kc in range(n_k):
                            mm = t.matmul(acc[half][:, :],
                                          wb[oc % 2][:, kc * 128:(kc + 1) * 128],
                                          rhs_buf[:, kc * TPC + half * 512:kc * TPC + half * 512 + 512],
                                          start=(kc == 0), stop=(kc == n_k - 1))
                            if kc == n_k - 1:
                                mm.then_inc(p.h["pe"], 1)
                    if resid and oc >= 1:
                        stats(oc - 1)
                if resid:
                    stats(n_out - 1)

            @blk.vector
            def _(v):
                vcc = vc0
                for oc in range(n_out):
                    for half in range(2):
                        v.wait_ge(p.h["pe"], pe0 + pe_pos[("c", oc, half)])
                        if resid:
                            sl_ = stream[:, oc * TPC + half * 512: oc * TPC + half * 512 + 512]
                            if bias_pp:
                                v.scalar_tensor_tensor(sl_, acc[half][:, :],
                                                       bias_sb[:, oc:oc + 1], sl_,
                                                       op0=ALU.add, op1=ALU.add).then_inc(p.h["vc"], 1)
                            else:
                                v.tensor_tensor(sl_, acc[half][:, :], sl_,
                                                op=ALU.add).then_inc(p.h["vc"], 1)
                            vcc += 1
                            v.wait_ge(p.h["vc"], vcc)
                            if oc >= 1:
                                v.wait_ge(p.h["pe"], pe0 + pe_pos[("q", oc - 1, half)])
                            v.tensor_tensor(ysq[half][:, :], sl_, sl_, op=ALU.mult).then_inc(p.h["vc"], 1)
                            vcc += 1
                        else:
                            v.tensor_copy(dst[:, oc * TPC + half * 512: oc * TPC + half * 512 + 512],
                                          acc[half][:, :]).then_inc(p.h["vc"], 1)
                            vcc += 1
        p.c["pe"] = pe0 + len(order)
        p.c["vc"] = vc0 + vper * n_out
        for w in wsems:
            p.c[w] = cur[w]
        if mode == "resid_ln":
            for half in range(2):
                _ln_finalize(p, k, stream, half, ps_s[half], ps_sq[half], sn,
                             out_f32=out_f32)


def _emit_ffn(p, k, io, stream, wg_key, wu_key, wd_key, sg_key, su_key, sd_key,
              fuseT, out_f32=None, o_y=None):
    nc = p.nc
    with ExitStack() as ctx:
        E = ctx.enter_context
        wgb = [E(nc.sbuf_tensor(p.uname(f"ff_wg{i}"), [128, 2 * NEC * 128], BF16)) for i in range(2)]
        wub = [E(nc.sbuf_tensor(p.uname(f"ff_wu{i}"), [128, 2 * NEC * 128], BF16)) for i in range(2)]
        sg_sb = E(nc.sbuf_tensor(p.uname("ff_sg"), [128, NIC], F32))
        su_sb = E(nc.sbuf_tensor(p.uname("ff_su"), [128, NIC], F32))
        sd_sb = E(nc.sbuf_tensor(p.uname("ff_sd"), [128, NEC], F32))
        silu_sb = [E(nc.sbuf_tensor(p.uname(f"ff_si{i}"), [128, 512], BF16)) for i in range(2)]
        ysq = [E(nc.sbuf_tensor(p.uname(f"ff_ysq{i}"), [128, 512], BF16)) for i in range(2)]
        pg = [E(nc.psum_tensor(p.uname(f"ff_pg{i}"), [128, 512], F32)) for i in range(2)]
        pu = [E(nc.psum_tensor(p.uname(f"ff_pu{i}"), [128, 512], F32)) for i in range(2)]
        py = [E(nc.psum_tensor(p.uname(f"ff_py{i}"), [128, 512], F32)) for i in range(2)]
        ps_s = E(nc.psum_tensor(p.uname("ff_s"), [128, 512], F32))
        ps_sq = E(nc.psum_tensor(p.uname("ff_q"), [128, 512], F32))

        ld0 = p.c["ld"]
        with nc.Block() as blk:
            @blk.gpsimd
            def _(g):
                g.dma_start(out=sg_sb[:, :], in_=io[sg_key]).then_inc(p.h["ld"], 16)
                g.dma_start(out=su_sb[:, :], in_=io[su_key]).then_inc(p.h["ld"], 16)
                g.dma_start(out=sd_sb[:, :], in_=io[sd_key]).then_inc(p.h["ld"], 16)
                g.wait_ge(p.h["ld"], ld0 + 48)
        p.c["ld"] = ld0 + 48

        for half in range(2):
            with nc.Block() as blk:
                @blk.gpsimd
                def _(g):
                    g.dma_start(out=wgb[0][:, :].rearrange("p (c m) -> p c m", c=2 * NEC),
                                in_=io[wg_key][0].rearrange("c p m -> p c m")).then_inc(p.h["lwa"], 16)
                    g.dma_start(out=wub[0][:, :].rearrange("p (c m) -> p c m", c=2 * NEC),
                                in_=io[wu_key][0].rearrange("c p m -> p c m")).then_inc(p.h["lwc"], 16)
            p.c["lwa"] += 16
            p.c["lwc"] += 16
            pe0, vc0, sc0 = p.c["pe"], p.c["vc"], p.c["sc"]
            lwg = [p.c["lwa"], p.c["lwb"]]
            lwu = [p.c["lwc"], p.c["lwd"]]
            need_g = []
            need_u = []
            for q in range(NIC // 2):
                need_g.append(lwg[q % 2])
                need_u.append(lwu[q % 2])
                if q + 1 < NIC // 2:
                    lwg[(q + 1) % 2] += 16
                    lwu[(q + 1) % 2] += 16
            gsem = ("lwa", "lwb")
            usem = ("lwc", "lwd")
            with nc.Block() as blk:
                @blk.gpsimd
                def _(g, half=half):
                    for q in range(NIC // 2):
                        if q + 1 < NIC // 2:
                            if q >= 1:
                                g.wait_ge(p.h["pe"], pe0 + 2 * (2 * q - 1) + 2)
                            g.dma_start(out=wgb[(q + 1) % 2][:, :].rearrange("p (c m) -> p c m", c=2 * NEC),
                                        in_=io[wg_key][q + 1].rearrange("c p m -> p c m")
                                        ).then_inc(p.h[gsem[(q + 1) % 2]], 16)
                            g.dma_start(out=wub[(q + 1) % 2][:, :].rearrange("p (c m) -> p c m", c=2 * NEC),
                                        in_=io[wu_key][q + 1].rearrange("c p m -> p c m")
                                        ).then_inc(p.h[usem[(q + 1) % 2]], 16)

                @blk.tensor
                def _(t, half=half):
                    for ic in range(NIC):
                        q, qi = divmod(ic, 2)
                        if qi == 0:
                            t.wait_ge(p.h[gsem[q % 2]], need_g[q])
                            t.wait_ge(p.h[usem[q % 2]], need_u[q])
                        if ic >= 2:
                            t.wait_ge(p.h["sc"], sc0 + (ic - 2) + 1)
                            t.wait_ge(p.h["vc"], vc0 + (ic - 2) + 1)
                        for dc in range(NEC):
                            t.matmul(pg[ic % 2][:, :], wgb[q % 2][:, (qi * NEC + dc) * 128:(qi * NEC + dc + 1) * 128],
                                     stream[:, dc * TPC + half * 512:dc * TPC + half * 512 + 512],
                                     start=(dc == 0), stop=(dc == NEC - 1)
                                     ).then_inc(p.h["pe"], 1) if dc == NEC - 1 else t.matmul(
                                pg[ic % 2][:, :], wgb[q % 2][:, (qi * NEC + dc) * 128:(qi * NEC + dc + 1) * 128],
                                stream[:, dc * TPC + half * 512:dc * TPC + half * 512 + 512],
                                start=(dc == 0), stop=False)
                        for dc in range(NEC):
                            t.matmul(pu[ic % 2][:, :], wub[q % 2][:, (qi * NEC + dc) * 128:(qi * NEC + dc + 1) * 128],
                                     stream[:, dc * TPC + half * 512:dc * TPC + half * 512 + 512],
                                     start=(dc == 0), stop=(dc == NEC - 1)
                                     ).then_inc(p.h["pe"], 1) if dc == NEC - 1 else t.matmul(
                                pu[ic % 2][:, :], wub[q % 2][:, (qi * NEC + dc) * 128:(qi * NEC + dc + 1) * 128],
                                stream[:, dc * TPC + half * 512:dc * TPC + half * 512 + 512],
                                start=(dc == 0), stop=False)

                @blk.scalar
                def _(s, half=half):
                    s.wait_ge(p.h["ld"], ld0 + 48)
                    for ic in range(NIC):
                        s.wait_ge(p.h["pe"], pe0 + 2 * ic + 1)
                        if ic >= 2:
                            s.wait_ge(p.h["vc"], vc0 + (ic - 2) + 1)
                        s.activation(silu_sb[ic % 2][:, :], pg[ic % 2][:, :], AF.Silu,
                                     scale=sg_sb[:, ic:ic + 1]).then_inc(p.h["sc"], 1)

                @blk.vector
                def _(v, half=half):
                    for ic in range(NIC):
                        v.wait_ge(p.h["pe"], pe0 + 2 * ic + 2)
                        v.wait_ge(p.h["sc"], sc0 + ic + 1)
                        v.scalar_tensor_tensor(fuseT[:, ic * 512:(ic + 1) * 512],
                                               pu[ic % 2][:, :], su_sb[:, ic:ic + 1],
                                               silu_sb[ic % 2][:, :],
                                               op0=ALU.mult, op1=ALU.mult).then_inc(p.h["vc"], 1)
            p.c["pe"] = pe0 + 2 * NIC
            p.c["sc"] = sc0 + NIC
            p.c["vc"] = vc0 + NIC
            p.c["lwa"], p.c["lwb"] = lwg[0], lwg[1]
            p.c["lwc"], p.c["lwd"] = lwu[0], lwu[1]

            ctxd = ExitStack()
            wdb = [ctxd.enter_context(nc.sbuf_tensor(p.uname(f"ff_wd{i}"), [128, 64 * 128], BF16))
                   for i in range(2)]
            with nc.Block() as blk:
                @blk.gpsimd
                def _(g):
                    g.dma_start(out=wdb[0][:, :].rearrange("p (c m) -> p c m", c=64),
                                in_=io[wd_key][0].rearrange("c p m -> p c m")).then_inc(p.h["lwa"], 16)
            p.c["lwa"] += 16
            pe0, vc0 = p.c["pe"], p.c["vc"]
            lwd_ = [p.c["lwa"], p.c["lwb"]]
            need_d = []
            for dc in range(NEC):
                need_d.append(lwd_[dc % 2])
                if dc + 1 < NEC:
                    lwd_[(dc + 1) % 2] += 16
            dsem = ("lwa", "lwb")
            with nc.Block() as blk:
                @blk.gpsimd
                def _(g):
                    for dc in range(1, NEC):
                        if dc >= 2:
                            dcp = dc - 2
                            g.wait_ge(p.h["pe"], pe0 + (1 if dcp == 0 else 3 * dcp - 1))
                        g.dma_start(out=wdb[dc % 2][:, :].rearrange("p (c m) -> p c m", c=64),
                                    in_=io[wd_key][dc].rearrange("c p m -> p c m")
                                    ).then_inc(p.h[dsem[dc % 2]], 16)

                @blk.tensor
                def _(t, half=half):
                    def chain_inc(dc):
                        return pe0 + (1 if dc == 0 else 3 * dc - 1)
                    def emit_stats(j):
                        t.wait_ge(p.h["vc"], vc0 + 2 * j + 2)
                        t.matmul(ps_s[0:1, :], k["ones_col_b"][:, :],
                                 stream[:, j * TPC + half * 512:j * TPC + half * 512 + 512],
                                 start=(j == 0), stop=(j == NEC - 1)).then_inc(p.h["pe"], 1)
                        t.matmul(ps_sq[0:1, :], k["ones_col_b"][:, :], ysq[j % 2][:, :],
                                 start=(j == 0), stop=(j == NEC - 1)).then_inc(p.h["pe"], 1)
                    for dc in range(NEC):
                        if dc >= 2:
                            t.wait_ge(p.h["vc"], vc0 + 2 * (dc - 2) + 2)
                        t.wait_ge(p.h[dsem[dc % 2]], need_d[dc])
                        for ic in range(NIC):
                            mm = t.matmul(py[dc % 2][:, :],
                                          wdb[dc % 2][:, ic * 128:(ic + 1) * 128],
                                          fuseT[:, ic * 512:(ic + 1) * 512],
                                          start=(ic == 0), stop=(ic == NIC - 1))
                            if ic == NIC - 1:
                                mm.then_inc(p.h["pe"], 1)
                        if dc >= 1:
                            emit_stats(dc - 1)
                    emit_stats(NEC - 1)

                @blk.vector
                def _(v, half=half):
                    vcc = vc0
                    for dc in range(NEC):
                        v.wait_ge(p.h["pe"], pe0 + (1 if dc == 0 else 3 * dc - 1))
                        if dc >= 2:
                            v.wait_ge(p.h["pe"], pe0 + 3 * (dc - 2) + 4)
                        sl_ = stream[:, dc * TPC + half * 512:dc * TPC + half * 512 + 512]
                        v.scalar_tensor_tensor(sl_, py[dc % 2][:, :], sd_sb[:, dc:dc + 1], sl_,
                                               op0=ALU.mult, op1=ALU.add).then_inc(p.h["vc"], 1)
                        vcc += 1
                        v.wait_ge(p.h["vc"], vcc)
                        v.tensor_tensor(ysq[dc % 2][:, :], sl_, sl_, op=ALU.mult).then_inc(p.h["vc"], 1)
                        vcc += 1
            p.c["pe"] = pe0 + 3 * NEC
            p.c["vc"] = vc0 + 2 * NEC
            p.c["lwa"], p.c["lwb"] = lwd_[0], lwd_[1]
            ctxd.close()

            with ExitStack() as ctx2:
                sn = _ln_scratch(p, ctx2, mbc=pg[0], rbc=pg[1],
                                 pool_split=(out_f32 is None))
                _ln_finalize(p, k, stream, half, ps_s, ps_sq, sn, out_f32=out_f32)
            if out_f32 is not None and o_y is not None:
                _emit_final_store_half(p, k, out_f32, o_y, half, pg)



def _emit_kv_transpose(p, k, k_tok, v_tok, kvT):
    nc = p.nc
    with ExitStack() as ctx:
        E = ctx.enter_context
        kvpc = [E(nc.sbuf_tensor(p.uname(f"kv_pc{i}"), [128, D], BF16)) for i in range(1)]
        trp = [E(nc.psum_tensor(p.uname(f"kv_tr{i}"), [128, 1024], BF16)) for i in range(2)]
        for tc in range(NTT):
            pe0, vc0 = p.c["pe"], p.c["vc"]
            with nc.Block() as blk:
                @blk.vector
                def _(v, tc=tc, pe0=pe0, vc0=vc0):
                    v.tensor_tensor(kvpc[0][:, :], k_tok[:, tc * D:(tc + 1) * D],
                                    v_tok[:, tc * D:(tc + 1) * D], op=ALU.mult).then_inc(p.h["vc"], 1)
                    for ec in range(NEC):
                        v.wait_ge(p.h["pe"], pe0 + ec + 1)
                        v.tensor_copy(kvT[:, ec * TPC + tc * 128:ec * TPC + (tc + 1) * 128],
                                      trp[ec % 2][:, 0:128]).then_inc(p.h["vc"], 1)

                @blk.tensor
                def _(t, tc=tc, pe0=pe0, vc0=vc0):
                    t.wait_ge(p.h["vc"], vc0 + 1)
                    for ec in range(NEC):
                        if ec >= 2:
                            t.wait_ge(p.h["vc"], vc0 + 1 + ec - 1)
                        t.transpose(trp[ec % 2][:, 0:128], kvpc[0][:, ec * 128:(ec + 1) * 128],
                                    k["ident_b"][:, :]).then_inc(p.h["pe"], 1)
            p.c["pe"] = pe0 + NEC
            p.c["vc"] = vc0 + 1 + NEC


def _emit_scan(p, k, kvT, kvt_loc, blocksums):
    nc = p.nc
    with ExitStack() as ctx:
        E = ctx.enter_context
        zeros = E(nc.sbuf_tensor(p.uname("sc_zero"), [128, TPC], F32))
        pl0 = p.c["pl"]
        with nc.Block() as blk:
            @blk.gpsimd
            def _(g):
                g.memset(zeros[:, :], 0.0).then_inc(p.h["pl"], 1)
        p.c["pl"] = pl0 + 1
        vc0 = p.c["vc"]
        with nc.Block() as blk:
            @blk.vector
            def _(v):
                vcc = vc0
                v.wait_ge(p.h["pl"], pl0 + 1)
                for ec in range(NEC):
                    with p.nc.allow_low_precision(reason="scan state fp32; bf16 store"):
                        v.tensor_tensor_scan(kvt_loc[:, ec * TPC:(ec + 1) * TPC],
                                             kvT[:, ec * TPC:(ec + 1) * TPC], zeros[:, :],
                                             0.0, op0=ALU.add, op1=ALU.add).then_inc(p.h["vc"], 1)
                    vcc += 1
                    v.wait_ge(p.h["vc"], vcc)
                    v.tensor_copy(blocksums[:, ec:ec + 1],
                                  kvt_loc[:, (ec + 1) * TPC - 1:(ec + 1) * TPC]).then_inc(p.h["vc"], 1)
                    vcc += 1
        p.c["vc"] = vc0 + 2 * NEC


def _emit_l1b_vhat(p, k, io, qT1, kvt, vhat):
    nc = p.nc
    with ExitStack() as ctx:
        E = ctx.enter_context
        bs4 = E(nc.sbuf_tensor(p.uname("vb_bs4"), [128, 4 * NEC], F32))
        mask = E(nc.sbuf_tensor(p.uname("vb_mask"), [128, NRANK], F32))
        offs = E(nc.sbuf_tensor(p.uname("vb_offs"), [128, NEC], F32))
        kvf = E(nc.sbuf_tensor(p.uname("vb_kvf"), [128, TPC], F32))
        ld0, vc0, pl0 = p.c["ld"], p.c["vc"], p.c["pl"]
        with nc.Block() as blk:
            @blk.gpsimd
            def _(g):
                g.dma_start(out=bs4[:, :].rearrange("p (r c) -> p r c", r=4),
                            in_=io["bs_all"].rearrange("r p c -> p r c")).then_inc(p.h["ld"], 16)
                g.dma_start(out=mask[:, :], in_=io["mask_pp"]).then_inc(p.h["ld"], 16)
                g.memset(offs[:, :], 0.0).then_inc(p.h["pl"], 1)

            @blk.vector
            def _(v):
                vcc = vc0
                v.wait_ge(p.h["ld"], ld0 + 32)
                v.wait_ge(p.h["pl"], pl0 + 1)
                for r in range(NRANK):
                    v.wait_ge(p.h["vc"], vcc)
                    v.scalar_tensor_tensor(offs[:, :], bs4[:, r * NEC:(r + 1) * NEC],
                                           mask[:, r:r + 1], offs[:, :],
                                           op0=ALU.mult, op1=ALU.add).then_inc(p.h["vc"], 1)
                    vcc += 1
        p.c["ld"] = ld0 + 32
        p.c["pl"] = pl0 + 1
        p.c["vc"] = vc0 + NRANK
        vc0 = p.c["vc"]
        with nc.Block() as blk:
            @blk.vector
            def _(v):
                vcc = vc0
                for ec in range(NEC):
                    v.wait_ge(p.h["vc"], vcc)
                    v.tensor_scalar(kvf[:, :], kvt[:, ec * TPC:(ec + 1) * TPC],
                                    offs[:, ec:ec + 1], None, op0=ALU.add).then_inc(p.h["vc"], 1)
                    vcc += 1
                    v.wait_ge(p.h["vc"], vcc)
                    v.scalar_tensor_tensor(vhat[:, ec * TPC:(ec + 1) * TPC],
                                           qT1[:, ec * TPC:(ec + 1) * TPC], L1_DENOM, kvf[:, :],
                                           op0=ALU.mult, op1=ALU.mult).then_inc(p.h["vc"], 1)
                    vcc += 1
        p.c["vc"] = vc0 + 2 * NEC


def build_l1a():
    p = P()
    io = {
        "ident": p.dram_in("ident", [128, 128], F32),
        "stream_in": p.dram_in("stream_in", [128, NEC * TPC], BF16),
        "l1_wq_b": p.dram_in("l1_wq_b", [NEC, NEC, 128, 128], F32),
        "l1_wk_a": p.dram_in("l1_wk_a", [NEC, NEG, 128, 512], F32),
        "l1_wv_a": p.dram_in("l1_wv_a", [NEC, NEG, 128, 512], F32),
    }
    o_q = p.dram_out("o_q", [128, NEC * TPC], BF16)
    o_kvt = p.dram_out("o_kvt", [128, NEC * TPC], BF16)
    o_bs = p.dram_out("o_bs", [128, NEC], F32)
    with p.ctx:
        k = _consts(p)
        stream = p.sb("stream", [128, NEC * TPC], BF16)
        qT1 = p.sb("qT1", [128, NEC * TPC], BF16)
        kvT = p.sb("kvT", [128, NEC * TPC], BF16)
        blocksums = p.sb("blocksums", [128, NEC], F32)
        _emit_init(p, k, io)
        _emit_load_sb(p, [(stream[:, :], io["stream_in"])])
        _emit_proj_b(p, k, io, "l1_wq_b", stream, None, "copy", dst=qT1)
        with ExitStack() as ctx2:
            k_tok = ctx2.enter_context(p.nc.sbuf_tensor(p.uname("k_tok"), [128, NTT * D], BF16))
            v_tok = ctx2.enter_context(p.nc.sbuf_tensor(p.uname("v_tok"), [128, NTT * D], BF16))
            _emit_qkv(p, k, io, stream,
                      outs={"k": k_tok, "v": v_tok},
                      wnames={"k": "l1_wk_a", "v": "l1_wv_a"},
                      biases={}, rope=set())
            _emit_kv_transpose(p, k, k_tok, v_tok, kvT)
        kvt_loc = p.sb("kvt_loc", [128, NEC * TPC], BF16)
        _emit_scan(p, k, kvT, kvt_loc, blocksums)
        _emit_store(p, [(o_q, qT1[:, :]), (o_kvt, kvt_loc[:, :]), (o_bs, blocksums[:, :])])
    return p.nc


def build_l1b():
    p = P()
    io = {
        "ident": p.dram_in("ident", [128, 128], F32),
        "stream_in": p.dram_in("stream_in", [128, NEC * TPC], BF16),
        "q_in": p.dram_in("q_in", [128, NEC * TPC], BF16),
        "kvt_in": p.dram_in("kvt_in", [128, NEC * TPC], BF16),
        "bs_all": p.dram_in("bs_all", [NRANK, 128, NEC], F32),
        "mask_pp": p.dram_in("mask_pp", [128, NRANK], F32),
        "l1_wo_b": p.dram_in("l1_wo_b", [NEC, NEC, 128, 128], F32),
    }
    o_stream = p.dram_out("o_stream", [128, NEC * TPC], BF16)
    with p.ctx:
        k = _consts(p)
        stream = p.sb("stream", [128, NEC * TPC], BF16)
        qT1 = p.sb("qT1", [128, NEC * TPC], BF16)
        kvt = p.sb("kvt", [128, NEC * TPC], BF16)
        vhat = p.sb("vhat", [128, NEC * TPC], BF16)
        _emit_init(p, k, io)
        _emit_load_sb(p, [(stream[:, :], io["stream_in"]),
                          (qT1[:, :], io["q_in"]), (kvt[:, :], io["kvt_in"])])
        _emit_l1b_vhat(p, k, io, qT1, kvt, vhat)
        with ExitStack() as ctx2:
            sn = _ln_scratch(p, ctx2)
            _emit_proj_b(p, k, io, "l1_wo_b", vhat, stream, "resid_ln", sn=sn)
        _emit_store(p, [(o_stream, stream[:, :])])
    return p.nc


def build_l0a(debug=False):
    p = P()
    io = {
        "ident": p.dram_in("ident", [128, 128], F32),
        "x": p.dram_in("x", [TPC, D], F32),
        "l0_wk_a": p.dram_in("l0_wk_a", [NEC, NEG, 128, 512], F32),
        "l0_wv_a": p.dram_in("l0_wv_a", [NEC, NEG, 128, 512], F32),
        "l0_wq_a": p.dram_in("l0_wq_a", [NEC, NEG, 128, 512], F32),
        "l0_bq_r": p.dram_in("l0_bq_r", [1, D], F32),
        "l0_bk_r": p.dram_in("l0_bk_r", [1, D], F32),
        "l0_bv_r": p.dram_in("l0_bv_r", [1, D], F32),
        "ropecq": p.dram_in("ropecq", [TPC, TPC], F32),
        "ropesq": p.dram_in("ropesq", [TPC, TPC], F32),
        "ropeck": p.dram_in("ropeck", [TPC, TPC], F32),
        "ropesk": p.dram_in("ropesk", [TPC, TPC], F32),
    }
    o_stream = p.dram_out("o_stream", [128, NEC * TPC], BF16)
    o_attn = p.dram_out("o_attn", [128, NHEAD * TPC], BF16)
    if debug:
        o_q = p.dram_out("o_q", [128, NTT * D], BF16)
        o_k = p.dram_out("o_k", [128, NTT * D], BF16)
        o_v = p.dram_out("o_v", [128, NTT * D], BF16)

    with p.ctx:
        k = _consts(p)
        stream = p.sb("stream", [128, NEC * TPC], BF16)
        q_tok = p.sb("q_tok", [128, NTT * D], BF16)
        k_tok = p.sb("k_tok", [128, NTT * D], BF16)
        v_tok = p.sb("v_tok", [128, NTT * D], BF16)
        _emit_init(p, k, io)
        _emit_load_stream(p, k, io["x"], stream)
        _emit_qkv(p, k, io, stream,
                  outs={"k": k_tok, "v": v_tok, "q": q_tok},
                  wnames={"k": "l0_wk_a", "v": "l0_wv_a", "q": "l0_wq_a"},
                  biases={"k": "l0_bk_r", "v": "l0_bv_r", "q": "l0_bq_r"},
                  rope={"q", "k"})
        attn_fm = p.sb("attn_fm", [128, NHEAD * TPC], BF16)
        bufs = {"q_tok": q_tok, "k_tok": k_tok, "v_tok": v_tok, "attn_fm": attn_fm}
        _emit_l0_attn(p, k, io, bufs)
        stores = [
            (o_stream, stream[:, :]),
            (o_attn, attn_fm[:, :]),
        ]
        if debug:
            stores += [(o_q, q_tok[:, :]), (o_k, k_tok[:, :]), (o_v, v_tok[:, :])]
        _emit_store(p, stores)
    return p.nc


def _l0a_inmap(sh, pc):
    return {"ident": sh["ident"],
            "l0_wk_a": sh["l0_wk_a"], "l0_wv_a": sh["l0_wv_a"], "l0_wq_a": sh["l0_wq_a"],
            "l0_bq_r": sh["l0_bq_r"], "l0_bk_r": sh["l0_bk_r"], "l0_bv_r": sh["l0_bv_r"],
            "x": pc["x"],
            "ropecq": pc["ropecq"], "ropesq": pc["ropesq"],
            "ropeck": pc["ropeck"], "ropesk": pc["ropesk"]}


def _emit_load_sb(p, pairs):
    sd0 = p.c["sd"]
    n = len(pairs)
    with p.nc.Block() as blk:
        @blk.sync
        def _(sy):
            for sb, dram in pairs:
                sy.dma_start(out=sb, in_=dram).then_inc(p.h["sd"], 16)
            sy.wait_ge(p.h["sd"], sd0 + 16 * n)
    p.c["sd"] = sd0 + 16 * n


def build_l0b():
    p = P()
    io = {
        "ident": p.dram_in("ident", [128, 128], F32),
        "a2a": p.dram_in("a2a", [2 * NRANK * 128, 4 * TPC], BF16),
        "sel_pp": p.dram_in("sel_pp", [128, 2], F32),
        "stream_in": p.dram_in("stream_in", [128, NEC * TPC], BF16),
        "l0_wo_b": p.dram_in("l0_wo_b", [NEC, NEC, 128, 128], F32),
        "l0_bo_pp": p.dram_in("l0_bo_pp", [128, NEC], F32),
    }
    o_stream = p.dram_out("o_stream", [128, NEC * TPC], BF16)
    with p.ctx:
        k = _consts(p)
        stream = p.sb("stream", [128, NEC * TPC], BF16)
        a2abuf = p.sb("a2abuf", [128, 2 * NEC * TPC], BF16)
        out_scrT = p.sb("out_scrT", [128, NEC * TPC], BF16)
        sel_sb = p.sb("sel_sb", [128, 2], F32)
        _emit_init(p, k, io)
        _emit_load_sb(p, [(stream[:, :], io["stream_in"]), (sel_sb[:, :], io["sel_pp"])])
        _emit_scramble8(p, io, a2abuf, out_scrT, sel_sb)
        with ExitStack() as ctx2:
            sn = _ln_scratch(p, ctx2)
            _emit_proj_b(p, k, io, "l0_wo_b", out_scrT, stream, "resid_ln",
                         bias_pp="l0_bo_pp", sn=sn)
        _emit_store(p, [(o_stream, stream[:, :])])
    return p.nc


def _host_a2a(attn_fms, b, sl):
    blocks = [np.asarray(attn_fms[g])[:, 4 * sl * TPC:(4 * sl + 4) * TPC]
              for g in range(2 * NRANK)]
    return np.ascontiguousarray(np.concatenate(blocks, axis=0))


def build_ffn(l, final=False):
    p = P()
    io = {
        "ident": p.dram_in("ident", [128, 128], F32),
        "stream_in": p.dram_in("stream_in", [128, NEC * TPC], BF16),
        "wg": p.dram_in("wg", [NIC // 2, 2 * NEC, 128, 128], I8),
        "wu": p.dram_in("wu", [NIC // 2, 2 * NEC, 128, 128], I8),
        "wd": p.dram_in("wd", [NEC, NIC, 128, 128], I8),
        "sg": p.dram_in("sg", [128, NIC], F32),
        "su": p.dram_in("su", [128, NIC], F32),
        "sd": p.dram_in("sd", [128, NEC], F32),
    }
    o_stream = p.dram_out("o_stream", [128, NEC * TPC], BF16)
    o_y = p.dram_out("o_y", [TPC, D], F32) if final else None
    with p.ctx:
        k = _consts(p)
        stream = p.sb("stream", [128, NEC * TPC], BF16)
        _emit_init(p, k, io)
        _emit_load_sb(p, [(stream[:, :], io["stream_in"])])
        with ExitStack() as ctx2:
            fuseT = ctx2.enter_context(p.nc.sbuf_tensor(p.uname("fuseT"), [128, NIC * 512], BF16))
            xTf32 = (ctx2.enter_context(p.nc.sbuf_tensor(p.uname("xTf32"), [128, NEC * 512], F32))
                     if final else None)
            _emit_ffn(p, k, io, stream, "wg", "wu", "wd", "sg", "su", "sd",
                      fuseT, out_f32=xTf32, o_y=o_y)
        if not final:
            _emit_store(p, [(o_stream, stream[:, :])])
        else:
            _emit_store(p, [(o_stream, stream[:, :])])
    return p.nc


def _emit_final_store_half(p, k, xTf32, o_y, half, trp):
    nc = p.nc
    with ExitStack() as ctx:
        E = ctx.enter_context
        stg = [E(nc.sbuf_tensor(p.uname(f"fs_st{i}"), [128, 128], F32)) for i in range(4)]
        pe0, sc0 = p.c["pe"], p.c["sc"]
        fs = ("fsa", "fsb", "fsc", "fsd")
        fs0 = [p.c[n] for n in fs]
        NB = NEC * 4
        with nc.Block() as blk:
            @blk.tensor
            def _(t):
                for i in range(NB):
                    ec, tc = divmod(i, 4)
                    if i >= 2:
                        t.wait_ge(p.h["sc"], sc0 + i - 1)
                    t.transpose(trp[i % 2][:, 0:128],
                                xTf32[:, ec * 512 + tc * 128:ec * 512 + (tc + 1) * 128],
                                k["ident_f"][:, :]).then_inc(p.h["pe"], 1)

            @blk.scalar
            def _(s):
                for i in range(NB):
                    s.wait_ge(p.h["pe"], pe0 + i + 1)
                    if i >= 4:
                        s.wait_ge(p.h[fs[i % 4]], fs0[i % 4] + 16 * (i // 4))
                    s.copy(stg[i % 4][:, :], trp[i % 2][:, 0:128]).then_inc(p.h["sc"], 1)

            @blk.sync
            def _(sy, half=half):
                for i in range(NB):
                    ec, tc = divmod(i, 4)
                    sy.wait_ge(p.h["sc"], sc0 + i + 1)
                    sy.dma_start(
                        out=o_y[half * 512 + tc * 128:half * 512 + (tc + 1) * 128,
                                ec * 128:(ec + 1) * 128],
                        in_=stg[i % 4][:, :]).then_inc(p.h[fs[i % 4]], 16)
                for j in range(4):
                    sy.wait_ge(p.h[fs[j]], fs0[j] + 16 * (NB // 4))
        p.c["pe"] = pe0 + NB
        p.c["sc"] = sc0 + NB
        for j in range(4):
            p.c[fs[j]] += 16 * (NB // 4)


def _emit_cc(p, kind, groups, in_ap, out_ap, src_pairs, ccname, ici=False):
    ld0 = p.c["ld"]
    n = len(src_pairs)
    with p.nc.Block() as blk:
        @blk.gpsimd
        def _(g):
            for dst, sb in src_pairs:
                g.dma_start(out=dst, in_=sb).then_inc(p.h["ld"], 16)
            g.wait_ge(p.h["ld"], ld0 + 16 * n)
            if ici:
                with allow_ici_all_to_all():
                    g.collective_compute(kind, ALU.bypass, replica_groups=groups,
                                         ins=[in_ap], outs=[out_ap]).then_inc(p.h[ccname])
            else:
                g.collective_compute(kind, ALU.bypass, replica_groups=groups,
                                     ins=[in_ap], outs=[out_ap]).then_inc(p.h[ccname])
            g.wait_ge(p.h[ccname], 1)
    p.c["ld"] = ld0 + 16 * n
    p.c[ccname] = 1


def build_fused():
    p = P()
    nc = p.nc
    io = {
        "ident": p.dram_in("ident", [128, 128], F32),
        "x": p.dram_in("x", [TPC, D], F32),
        "l0_wk_a": p.dram_in("l0_wk_a", [NEC, NEG, 128, 512], F32),
        "l0_wv_a": p.dram_in("l0_wv_a", [NEC, NEG, 128, 512], F32),
        "l0_wq_a": p.dram_in("l0_wq_a", [NEC, NEG, 128, 512], F32),
        "l0_bq_r": p.dram_in("l0_bq_r", [1, D], F32),
        "l0_bk_r": p.dram_in("l0_bk_r", [1, D], F32),
        "l0_bv_r": p.dram_in("l0_bv_r", [1, D], F32),
        "ropecq": p.dram_in("ropecq", [TPC, TPC], F32),
        "ropesq": p.dram_in("ropesq", [TPC, TPC], F32),
        "ropeck": p.dram_in("ropeck", [TPC, TPC], F32),
        "ropesk": p.dram_in("ropesk", [TPC, TPC], F32),
        "l0_wo_b": p.dram_in("l0_wo_b", [NEC, NEC, 128, 128], F32),
        "l0_bo_pp": p.dram_in("l0_bo_pp", [128, NEC], F32),
        "f0_wg": p.dram_in("f0_wg", [NIC // 2, 2 * NEC, 128, 128], I8),
        "f0_wu": p.dram_in("f0_wu", [NIC // 2, 2 * NEC, 128, 128], I8),
        "f0_wd": p.dram_in("f0_wd", [NEC, NIC, 128, 128], I8),
        "f0_sg": p.dram_in("f0_sg", [128, NIC], F32),
        "f0_su": p.dram_in("f0_su", [128, NIC], F32),
        "f0_sd": p.dram_in("f0_sd", [128, NEC], F32),
        "l1_wq_b": p.dram_in("l1_wq_b", [NEC, NEC, 128, 128], F32),
        "l1_wk_a": p.dram_in("l1_wk_a", [NEC, NEG, 128, 512], F32),
        "l1_wv_a": p.dram_in("l1_wv_a", [NEC, NEG, 128, 512], F32),
        "l1_wo_b": p.dram_in("l1_wo_b", [NEC, NEC, 128, 128], F32),
        "mask_pp": p.dram_in("mask_pp", [128, NRANK], F32),
        "sel_pp": p.dram_in("sel_pp", [128, 2], F32),
        "f1_wg": p.dram_in("f1_wg", [NIC // 2, 2 * NEC, 128, 128], I8),
        "f1_wu": p.dram_in("f1_wu", [NIC // 2, 2 * NEC, 128, 128], I8),
        "f1_wd": p.dram_in("f1_wd", [NEC, NIC, 128, 128], I8),
        "f1_sg": p.dram_in("f1_sg", [128, NIC], F32),
        "f1_su": p.dram_in("f1_su", [128, NIC], F32),
        "f1_sd": p.dram_in("f1_sd", [128, NEC], F32),
    }
    o_y = p.dram_out("o_y", [TPC, D], F32)
    cc1i = [nc.dram_tensor(f"cc1i{t}", [2 * NRANK * 128, 512], BF16) for t in range(NTT)]
    cc1o = [nc.dram_tensor(f"cc1o{t}", [2 * NRANK * 128, 512], BF16) for t in range(NTT)]
    cc2_in = nc.dram_tensor("cc2_in", [128, NEC], F32)
    cc2_out = nc.dram_tensor("cc2_out", [NRANK * 128, NEC], F32)
    groups = [[0, 1, 2, 3], [4, 5, 6, 7]]

    with p.ctx:
        k = _consts(p)
        stream = p.sb("stream", [128, NEC * TPC], BF16)
        _emit_init(p, k, io)
        _emit_load_stream(p, k, io["x"], stream)
        with ExitStack() as sA:
            E = sA.enter_context
            q_tok = E(nc.sbuf_tensor("q_tok", [128, NTT * D], BF16))
            k_tok = E(nc.sbuf_tensor("k_tok", [128, NTT * D], BF16))
            v_tok = E(nc.sbuf_tensor("v_tok", [128, NTT * D], BF16))
            _emit_qkv(p, k, io, stream,
                      outs={"k": k_tok, "v": v_tok, "q": q_tok},
                      wnames={"k": "l0_wk_a", "v": "l0_wv_a", "q": "l0_wq_a"},
                      biases={"k": "l0_bk_r", "v": "l0_bv_r", "q": "l0_bq_r"},
                      rope={"q", "k"})
            attn_fm = E(nc.sbuf_tensor("attn_fm", [128, NHEAD * TPC], BF16))
            _emit_l0_attn(p, k, io, {"q_tok": q_tok, "k_tok": k_tok,
                                     "v_tok": v_tok, "attn_fm": attn_fm},
                          cc=(cc1i, cc1o, [list(range(2 * NRANK))]))
        with ExitStack() as sB:
            E = sB.enter_context
            a2abuf = E(nc.sbuf_tensor("a2abuf", [128, 2 * NEC * TPC], BF16))
            out_scrT = E(nc.sbuf_tensor("out_scrT", [128, NEC * TPC], BF16))
            sel_sb = E(nc.sbuf_tensor("sel_sb", [128, 2], F32))
            _emit_load_sb(p, [(sel_sb[:, :], io["sel_pp"])])
            _emit_scramble8(p, {}, a2abuf, out_scrT, sel_sb, cc1o=cc1o)
            sn = _ln_scratch(p, sB)
            _emit_proj_b(p, k, io, "l0_wo_b", out_scrT, stream, "resid_ln",
                         bias_pp="l0_bo_pp", sn=sn)
        with ExitStack() as sC:
            fuseT = sC.enter_context(nc.sbuf_tensor(p.uname("fuseT"), [128, NIC * 512], BF16))
            _emit_ffn(p, k, io, stream, "f0_wg", "f0_wu", "f0_wd",
                      "f0_sg", "f0_su", "f0_sd", fuseT)
        with ExitStack() as sD:
            E = sD.enter_context
            qT1 = E(nc.sbuf_tensor("qT1", [128, NEC * TPC], BF16))
            kvT = E(nc.sbuf_tensor("kvT", [128, NEC * TPC], BF16))
            blocksums = E(nc.sbuf_tensor("blocksums", [128, NEC], F32))
            with ExitStack() as sD1:
                k_tok2 = sD1.enter_context(nc.sbuf_tensor(p.uname("k_tok2"), [128, NTT * D], BF16))
                v_tok2 = sD1.enter_context(nc.sbuf_tensor(p.uname("v_tok2"), [128, NTT * D], BF16))
                _emit_proj_b(p, k, io, "l1_wq_b", stream, None, "copy", dst=qT1)
                _emit_qkv(p, k, io, stream,
                          outs={"k": k_tok2, "v": v_tok2},
                          wnames={"k": "l1_wk_a", "v": "l1_wv_a"},
                          biases={}, rope=set())
                _emit_kv_transpose(p, k, k_tok2, v_tok2, kvT)
            kvt_loc = E(nc.sbuf_tensor("kvt_loc", [128, NEC * TPC], BF16))
            _emit_scan(p, k, kvT, kvt_loc, blocksums)
            _emit_cc(p, "AllGather", groups, cc2_in.ap(), cc2_out.ap(),
                     [(cc2_in.ap(), blocksums[:, :])], "cc2")
            vhat = E(nc.sbuf_tensor("vhat", [128, NEC * TPC], BF16))
            _emit_l1b_vhat(p, k, {"bs_all": cc2_out.ap().rearrange("(r p) c -> r p c", p=128),
                                  "mask_pp": io["mask_pp"]}, qT1, kvt_loc, vhat)
            sn = _ln_scratch(p, sD)
            _emit_proj_b(p, k, io, "l1_wo_b", vhat, stream, "resid_ln", sn=sn)
        with ExitStack() as sF:
            fuseT2 = sF.enter_context(nc.sbuf_tensor(p.uname("fuseT"), [128, NIC * 512], BF16))
            xTf32 = sF.enter_context(nc.sbuf_tensor(p.uname("xTf32"), [128, NEC * 512], F32))
            _emit_ffn(p, k, io, stream, "f1_wg", "f1_wu", "f1_wd",
                      "f1_sg", "f1_su", "f1_sd", fuseT2, out_f32=xTf32, o_y=o_y)
    return p.nc


def _fused_inmap(sh, pc):
    m = {"ident": sh["ident"], "x": pc["x"], "mask_pp": pc["mask_pp"],
         "sel_pp": pc["sel_pp"],
         "ropecq": pc["ropecq"], "ropesq": pc["ropesq"],
         "ropeck": pc["ropeck"], "ropesk": pc["ropesk"]}
    for n in ("l0_wq_a", "l0_wk_a", "l0_wv_a", "l0_wo_b", "l0_bq_r", "l0_bk_r",
              "l0_bv_r", "l0_bo_pp", "f0_wg", "f0_wu", "f0_wd", "f0_sg", "f0_su",
              "f0_sd", "l1_wq_b", "l1_wk_a", "l1_wv_a", "l1_wo_b", "f1_wg",
              "f1_wu", "f1_wd", "f1_sg", "f1_su", "f1_sd"):
        m[n] = sh[n]
    return m


def kernel_fused(**inputs):
    sh, percore = _stage(inputs)
    cores = list(range(NCORE))
    res = run_bass_kernel_spmd(build_fused(), [_fused_inmap(sh, percore[c]) for c in cores],
                               core_ids=cores)
    out = np.zeros((2, SEQ, D), dtype=np.float32)
    for c in cores:
        b, sl = divmod(c, NRANK)
        out[b, sl * TPC:(sl + 1) * TPC, :] = np.asarray(res.results[c]["o_y"])
    return out


def kernel_unfused(**inputs):
    sh, percore = _stage(inputs)
    cores = list(range(NCORE))
    run = lambda nc, maps: run_bass_kernel_spmd(nc, maps, core_ids=cores)
    col = lambda res, key: [np.asarray(r[key]) for r in res.results]

    res = run(build_l0a(), [_l0a_inmap(sh, percore[c]) for c in cores])
    streams = col(res, "o_stream")
    attns = col(res, "o_attn")

    maps = []
    for c in cores:
        b, sl = divmod(c, NRANK)
        maps.append({"ident": sh["ident"], "a2a": _host_a2a(attns, b, sl),
                     "sel_pp": percore[c]["sel_pp"],
                     "stream_in": streams[c], "l0_wo_b": sh["l0_wo_b"],
                     "l0_bo_pp": sh["l0_bo_pp"]})
    streams = col(run(build_l0b(), maps), "o_stream")

    maps = [{"ident": sh["ident"], "stream_in": streams[c],
             "wg": sh["f0_wg"], "wu": sh["f0_wu"], "wd": sh["f0_wd"],
             "sg": sh["f0_sg"], "su": sh["f0_su"], "sd": sh["f0_sd"]} for c in cores]
    streams = col(run(build_ffn(0), maps), "o_stream")

    maps = [{"ident": sh["ident"], "stream_in": streams[c],
             "l1_wq_b": sh["l1_wq_b"], "l1_wk_a": sh["l1_wk_a"],
             "l1_wv_a": sh["l1_wv_a"]} for c in cores]
    res = run(build_l1a(), maps)
    qs, kvts, bss = col(res, "o_q"), col(res, "o_kvt"), col(res, "o_bs")

    maps = []
    for c in cores:
        b, sl = divmod(c, NRANK)
        bs_all = np.ascontiguousarray(np.stack([bss[b * NRANK + r] for r in range(NRANK)]))
        maps.append({"ident": sh["ident"], "stream_in": streams[c], "q_in": qs[c],
                     "kvt_in": kvts[c], "bs_all": bs_all,
                     "mask_pp": percore[c]["mask_pp"], "l1_wo_b": sh["l1_wo_b"]})
    streams = col(run(build_l1b(), maps), "o_stream")

    maps = [{"ident": sh["ident"], "stream_in": streams[c],
             "wg": sh["f1_wg"], "wu": sh["f1_wu"], "wd": sh["f1_wd"],
             "sg": sh["f1_sg"], "su": sh["f1_su"], "sd": sh["f1_sd"]} for c in cores]
    ys = col(run(build_ffn(1, final=True), maps), "o_y")

    out = np.zeros((2, SEQ, D), dtype=np.float32)
    for c in cores:
        b, sl = divmod(c, NRANK)
        out[b, sl * TPC:(sl + 1) * TPC, :] = ys[c]
    return out


def kernel(**inputs):
    return kernel_fused(**inputs)


# version 13
# speedup vs baseline: 1.0792x; 1.0173x over previous
import numpy as np
import ml_dtypes
from contextlib import ExitStack
import concourse.bass as bass
import concourse.mybir as mybir
from concourse.bass_utils import run_bass_kernel_spmd
from concourse.aot_env import allow_ici_all_to_all

F32 = mybir.dt.float32
BF16 = mybir.dt.bfloat16
I8 = mybir.dt.int8
AF = mybir.ActivationFunctionType
ALU = mybir.AluOpType
AX = mybir.AxisListType
BFNP = ml_dtypes.bfloat16

NCORE = 8
TPC = 1024
NTT = TPC // 128
D = 2048
NEC = D // 128
NEG = 4
INTER = 8192
NIC = INTER // 128
NHEAD = 16
HD = 128
SEQ = 4096
NRANK = 4
INV_SQRT_HD = 1.0 / float(np.sqrt(np.float32(HD)))
L1_DENOM = float(np.float32(np.sqrt(np.float32(D))) / np.float32(SEQ))
LN_EPS = 1e-5



def _tile_b(w):
    K, M = w.shape
    return np.ascontiguousarray(w.reshape(K // 128, 128, M // 128, 128).transpose(2, 0, 1, 3))


def _tile_a(w):
    K, M = w.shape
    return np.ascontiguousarray(w.reshape(K // 128, 128, M // 512, 512).transpose(0, 2, 1, 3))


def _pp(vec):
    n = vec.shape[0] // 128
    return np.ascontiguousarray(vec.reshape(n, 128).T)


def _rope_tables(sl, scale):
    inv = (1.0 / (10000.0 ** (np.arange(0, HD, 2, dtype=np.float32) / np.float32(HD)))).astype(np.float32)
    pos = np.arange(sl * TPC, (sl + 1) * TPC, dtype=np.float32)
    fr = np.outer(pos, inv).astype(np.float32)
    cos = (np.cos(fr) * scale).astype(np.float32)
    sin = (np.sin(fr) * scale).astype(np.float32)
    rep = lambda t: np.ascontiguousarray(np.tile(t, (1, NHEAD)))
    return rep(cos), rep(sin)


def _stage(inputs):
    sh = {}
    sh["ident"] = np.eye(128, dtype=np.float32)
    T = lambda a: np.ascontiguousarray(np.asarray(a).T)
    sh["l0_wq_a"] = _tile_a(T(inputs["L0_attn_Wq"]))
    sh["l0_wk_a"] = _tile_a(T(inputs["L0_attn_Wk"]))
    sh["l0_wv_a"] = _tile_a(T(inputs["L0_attn_Wv"]))
    sh["l0_wo_b"] = _tile_b(T(inputs["L0_attn_Wo"]))
    sh["l0_bq_r"] = np.ascontiguousarray(np.asarray(inputs["L0_attn_bq"]).reshape(1, D))
    sh["l0_bk_r"] = np.ascontiguousarray(np.asarray(inputs["L0_attn_bk"]).reshape(1, D))
    sh["l0_bv_r"] = np.ascontiguousarray(np.asarray(inputs["L0_attn_bv"]).reshape(1, D))
    sh["l0_bo_pp"] = _pp(np.asarray(inputs["L0_attn_bo"]))
    for l in (0, 1):
        sh[f"f{l}_wg"] = np.ascontiguousarray(
            _tile_b(np.asarray(inputs[f"L{l}_ffn_w_gate"])).reshape(NIC // 2, 2 * NEC, 128, 128))
        sh[f"f{l}_wu"] = np.ascontiguousarray(
            _tile_b(np.asarray(inputs[f"L{l}_ffn_w_up"])).reshape(NIC // 2, 2 * NEC, 128, 128))
        sh[f"f{l}_wd"] = _tile_b(np.asarray(inputs[f"L{l}_ffn_w_down"]))
        sh[f"f{l}_sg"] = _pp(np.asarray(inputs[f"L{l}_ffn_s_gate"]))
        sh[f"f{l}_su"] = _pp(np.asarray(inputs[f"L{l}_ffn_s_up"]))
        sh[f"f{l}_sd"] = _pp(np.asarray(inputs[f"L{l}_ffn_s_down"]))
    sh["l1_wq_b"] = _tile_b(T(inputs["L1_attn_Wq"]))
    sh["l1_wk_a"] = _tile_a(T(inputs["L1_attn_Wk"]))
    sh["l1_wv_a"] = _tile_a(T(inputs["L1_attn_Wv"]))
    sh["l1_wo_b"] = _tile_b(T(inputs["L1_attn_Wo"]))

    x = np.asarray(inputs["L0_attn_x"])
    percore = []
    for c in range(NCORE):
        b, sl = divmod(c, NRANK)
        pc = {"x": np.ascontiguousarray(x[b, sl * TPC:(sl + 1) * TPC, :])}
        pc["ropecq"], pc["ropesq"] = _rope_tables(sl, INV_SQRT_HD)
        pc["ropeck"], pc["ropesk"] = _rope_tables(sl, 1.0)
        pc["mask_pp"] = np.ascontiguousarray(
            np.broadcast_to((np.arange(NRANK) < sl).astype(np.float32)[None, :], (128, NRANK)).copy())
        sel = np.array([1.0, 0.0] if b == 0 else [0.0, 1.0], dtype=np.float32)
        pc["sel_pp"] = np.ascontiguousarray(np.broadcast_to(sel[None, :], (128, 2)).copy())
        percore.append(pc)
    return sh, percore



class P:

    def __init__(self):
        self.nc = bass.Bass("TRN2", target_bir_lowering=False, debug=False)
        self.ctx = ExitStack()
        self.h = {}
        self.c = {}
        for n in ("pe", "vc", "sc", "pl", "ld", "lwa", "lwb", "lwc", "lwd", "sd",
                  "cc1", "cc2", "fsa", "fsb", "fsc", "fsd"):
            self.h[n] = self.ctx.enter_context(self.nc.semaphore(n))
            self.c[n] = 0

    _n = 0

    def uname(self, base):
        P._n += 1
        return f"{base}_{P._n}"

    def sb(self, name, shape, dtype):
        return self.ctx.enter_context(self.nc.sbuf_tensor(name, shape, dtype))

    def dram_in(self, name, shape, dtype):
        return self.nc.dram_tensor(name, list(shape), dtype, kind="ExternalInput").ap()

    def dram_out(self, name, shape, dtype):
        return self.nc.dram_tensor(name, list(shape), dtype, kind="ExternalOutput").ap()


def _consts(p):
    k = {
        "ident_b": p.sb("ident_b", [128, 128], BF16),
        "ident_f": p.sb("ident_f", [128, 128], F32),
        "ones_row_b": p.sb("ones_row_b", [1, 128], BF16),
        "ones_row_f": p.sb("ones_row_f", [1, 128], F32),
        "ones_col_b": p.sb("ones_col_b", [128, 1], BF16),
        "eps11": p.sb("eps11", [1, 1], F32),
    }
    return k


def _emit_init(p, k, io):
    ld0 = p.c["ld"]
    with p.nc.Block() as blk:
        @blk.gpsimd
        def _(g):
            g.dma_start(out=k["ident_b"][:, :], in_=io["ident"]).then_inc(p.h["ld"], 16)
            g.dma_start(out=k["ident_f"][:, :], in_=io["ident"]).then_inc(p.h["ld"], 16)
            g.memset(k["ones_row_b"][:, :], 1.0)
            g.memset(k["ones_row_f"][:, :], 1.0)
            g.memset(k["eps11"][:, :], LN_EPS)
            g.memset(k["ones_col_b"][:, :], 1.0).then_inc(p.h["pl"], 1)
            g.wait_ge(p.h["ld"], ld0 + 32)
    p.c["ld"] = ld0 + 32
    p.c["pl"] += 1


def _emit_load_stream(p, k, x_ap, stream):
    nc = p.nc
    with ExitStack() as ctx:
        E = ctx.enter_context
        xr = E(nc.sbuf_tensor(p.uname("ls_xr"), [128, D], BF16))
        trp = [E(nc.psum_tensor(p.uname(f"ls_tr{i}"), [128, 1024], BF16)) for i in range(4)]
        for tc in range(NTT):
            ld0, pe0, vc0 = p.c["ld"], p.c["pe"], p.c["vc"]
            with nc.Block() as blk:
                @blk.gpsimd
                def _(g, tc=tc):
                    g.dma_start(out=xr[:, :], in_=x_ap[tc * 128:(tc + 1) * 128, :]).then_inc(p.h["ld"], 16)

                @blk.tensor
                def _(t, tc=tc, ld0=ld0, vc0=vc0):
                    t.wait_ge(p.h["ld"], ld0 + 16)
                    t.wait_ge(p.h["pl"], 1)
                    for ec in range(NEC):
                        if ec >= 4:
                            t.wait_ge(p.h["vc"], vc0 + ec - 3)
                        t.transpose(trp[ec % 4][:, 0:128], xr[:, ec * 128:(ec + 1) * 128],
                                    k["ident_b"][:, :]).then_inc(p.h["pe"], 1)

                @blk.vector
                def _(v, tc=tc, pe0=pe0):
                    for ec in range(NEC):
                        v.wait_ge(p.h["pe"], pe0 + ec + 1)
                        v.tensor_copy(stream[:, ec * TPC + tc * 128: ec * TPC + (tc + 1) * 128],
                                      trp[ec % 4][:, 0:128]).then_inc(p.h["vc"], 1)
            p.c["ld"] = ld0 + 16
            p.c["pe"] = pe0 + NEC
            p.c["vc"] = vc0 + NEC


def _emit_qkv(p, k, io, stream, outs, wnames, biases, rope, l1=False):
    nc = p.nc
    names = list(outs.keys())
    with ExitStack() as ctx:
        E = ctx.enter_context
        wrhs = [E(nc.sbuf_tensor(p.uname(f"qk_w{i}"), [128, NEC * 512], BF16)) for i in range(2)]
        bias_r = E(nc.sbuf_tensor(p.uname("qk_bias"), [1, D], BF16))
        if rope:
            tab_c = E(nc.sbuf_tensor(p.uname("qk_tc"), [128, NTT * 256], F32))
            tab_s = E(nc.sbuf_tensor(p.uname("qk_ts"), [128, NTT * 256], F32))
            qtmp = [E(nc.sbuf_tensor(p.uname(f"qk_t{i}"), [128, 512], F32)) for i in range(4)]
            m1 = E(nc.sbuf_tensor(p.uname("qk_m1"), [128, 256], F32))
            m2 = E(nc.sbuf_tensor(p.uname("qk_m2"), [128, 256], F32))
            m3 = E(nc.sbuf_tensor(p.uname("qk_m3"), [128, 256], F32))
            m4 = E(nc.sbuf_tensor(p.uname("qk_m4"), [128, 256], F32))
        acc = [E(nc.psum_tensor(p.uname(f"qk_ps{i}"), [128, 512], F32)) for i in range(4)]

        lwsem = ("lwa", "lwb")
        with nc.Block() as blk:
            @blk.gpsimd
            def _(g):
                g.dma_start(out=wrhs[0][:, :].rearrange("p (d e) -> p d e", d=NEC),
                            in_=io[wnames[names[0]]][:, 0, :, :].rearrange("d p e -> p d e")).then_inc(p.h["lwa"], 16)
        p.c["lwa"] += 16

        for ti, tname in enumerate(names):
            do_rope = tname in rope
            has_bias = biases.get(tname) is not None
            dst = outs[tname]
            if do_rope:
                ctab = io["ropecq"] if tname == "q" else io["ropeck"]
                stab = io["ropesq"] if tname == "q" else io["ropesk"]
            for eg in range(NEG):
                wslot = (ti * NEG + eg) % 2
                ld0, pe0, vc0, pl0 = (p.c["ld"], p.c["pe"], p.c["vc"], p.c["pl"])
                wsem = lwsem[wslot]
                lw_need = p.c[wsem]
                n_tab = 2 if do_rope else 0
                has_pref = not (ti == len(names) - 1 and eg == NEG - 1)
                sd0 = p.c["sd"]
                load_bias = has_bias and eg == 0
                with nc.Block() as blk:
                    @blk.gpsimd
                    def _(g, ti=ti, eg=eg, wslot=wslot, has_bias=has_bias,
                          tname=tname, has_pref=has_pref, load_bias=load_bias):
                        if load_bias:
                            g.dma_start(out=bias_r[:, :], in_=io[biases[tname]]).then_inc(p.h["ld"], 16)
                        if has_pref:
                            nt, ne = (ti, eg + 1) if eg + 1 < NEG else (ti + 1, 0)
                            g.dma_start(
                                out=wrhs[1 - wslot][:, :].rearrange("p (d e) -> p d e", d=NEC),
                                in_=io[wnames[names[nt]]][:, ne, :, :].rearrange("d p e -> p d e")).then_inc(p.h[lwsem[1 - wslot]], 16)

                    if do_rope:
                        @blk.sync
                        def _(sy, ti=ti, eg=eg, tname=tname):
                            sy.dma_start(
                                out=tab_c[:, :].rearrange("p (t e) -> p t e", t=NTT),
                                in_=(io["ropecq"] if tname == "q" else io["ropeck"])
                                .rearrange("(t p) e -> p t e", p=128)[:, :, eg * 256:(eg + 1) * 256],
                            ).then_inc(p.h["sd"], 16)
                            sy.dma_start(
                                out=tab_s[:, :].rearrange("p (t e) -> p t e", t=NTT),
                                in_=(io["ropesq"] if tname == "q" else io["ropesk"])
                                .rearrange("(t p) e -> p t e", p=128)[:, :, eg * 256:(eg + 1) * 256],
                            ).then_inc(p.h["sd"], 16)
                    n_dma = 1 if load_bias else 0
                    ld_data = ld0 + 16 * n_dma
                    sd_data = sd0 + (32 if do_rope else 0)

                    v_rope = 3 if do_rope else 0
                    v_per_sweep = 4 + 4 * v_rope if do_rope else 4
                    v_total = 2 * v_per_sweep

                    @blk.tensor
                    def _(t, eg=eg, wslot=wslot, vc0=vc0, ld_data=ld_data, lw_need=lw_need,
                          has_bias=has_bias, wsem=wsem):
                        t.wait_ge(p.h["ld"], ld_data)
                        t.wait_ge(p.h[wsem], lw_need)
                        for sw in range(2):
                            if sw == 1:
                                t.wait_ge(p.h["vc"], vc0 + 4)
                            for tt in range(4):
                                tc = sw * 4 + tt
                                for dc in range(NEC):
                                    mm = t.matmul(acc[tt][:, :],
                                                  stream[:, dc * TPC + tc * 128:dc * TPC + (tc + 1) * 128],
                                                  wrhs[wslot][:, dc * 512:(dc + 1) * 512],
                                                  start=(dc == 0),
                                                  stop=(not has_bias) and dc == NEC - 1)
                                    if (not has_bias) and dc == NEC - 1:
                                        mm.then_inc(p.h["pe"], 1)
                                if has_bias:
                                    t.matmul(acc[tt][:, :], k["ones_row_b"][:, :],
                                             bias_r[:, eg * 512:(eg + 1) * 512],
                                             start=False, stop=True).then_inc(p.h["pe"], 1)

                    if do_rope:
                        @blk.vector
                        def _(v, eg=eg, pe0=pe0, pl0=pl0, vc0=vc0, ld_data=ld_data, dst=dst,
                              sd_data=sd_data):
                            v.wait_ge(p.h["ld"], ld_data)
                            v.wait_ge(p.h["sd"], sd_data)
                            vcc = vc0
                            for sw in range(2):
                                for tt in range(4):
                                    v.wait_ge(p.h["pe"], pe0 + sw * 4 + tt + 1)
                                    if sw == 1:
                                        v.wait_ge(p.h["pl"], pl0 + tt * 3 + 2)
                                        v.wait_ge(p.h["vc"], vcc)
                                    v.tensor_copy(qtmp[tt][:, :], acc[tt][:, :]).then_inc(p.h["vc"], 1)
                                    vcc += 1
                                for tt in range(4):
                                    tc = sw * 4 + tt
                                    ts = qtmp[tt]
                                    x0 = ts[:, 0:512:2]
                                    x1 = ts[:, 1:512:2]
                                    tcs = tab_c[:, tc * 256:(tc + 1) * 256]
                                    plb = pl0 + sw * 12 + tt * 3
                                    dste = dst[:, tc * D + eg * 512 + 0: tc * D + (eg + 1) * 512: 2]
                                    v.wait_ge(p.h["vc"], vcc)
                                    v.tensor_tensor(m1[:, :], x0, tcs, op=ALU.mult).then_inc(p.h["vc"], 1)
                                    vcc += 1
                                    v.wait_ge(p.h["pl"], plb + 1)
                                    v.wait_ge(p.h["vc"], vcc)
                                    v.tensor_tensor(dste, m1[:, :], m2[:, :],
                                                    op=ALU.subtract).then_inc(p.h["vc"], 1)
                                    vcc += 1
                                    v.tensor_tensor(m4[:, :], x1, tcs, op=ALU.mult).then_inc(p.h["vc"], 1)
                                    vcc += 1

                        @blk.gpsimd
                        def _(g, eg=eg, pl0=pl0, vc0=vc0, ld_data=ld_data, dst=dst,
                              v_per_sweep=v_per_sweep, sd_data=sd_data):
                            g.wait_ge(p.h["ld"], ld_data)
                            g.wait_ge(p.h["sd"], sd_data)
                            plc = pl0
                            for sw in range(2):
                                for tt in range(4):
                                    tc = sw * 4 + tt
                                    ts = qtmp[tt]
                                    x0 = ts[:, 0:512:2]
                                    x1 = ts[:, 1:512:2]
                                    tss = tab_s[:, tc * 256:(tc + 1) * 256]
                                    vb = vc0 + sw * v_per_sweep + 4 + tt * 3
                                    evb = vc0 + sw * v_per_sweep + tt + 1
                                    dsto = dst[:, tc * D + eg * 512 + 1: tc * D + (eg + 1) * 512: 2]
                                    g.wait_ge(p.h["vc"], evb)
                                    g.wait_ge(p.h["pl"], plc)
                                    g.tensor_tensor(m2[:, :], x1, tss, op=ALU.mult).then_inc(p.h["pl"], 1)
                                    plc += 1
                                    g.tensor_tensor(m3[:, :], x0, tss, op=ALU.mult).then_inc(p.h["pl"], 1)
                                    plc += 1
                                    g.wait_ge(p.h["vc"], vb + 3)
                                    g.wait_ge(p.h["pl"], plc)
                                    g.tensor_tensor(dsto, m3[:, :], m4[:, :],
                                                    op=ALU.add).then_inc(p.h["pl"], 1)
                                    plc += 1
                    else:
                        @blk.vector
                        def _(v, eg=eg, pe0=pe0, dst=dst):
                            for sw in range(2):
                                for tt in range(4):
                                    tc = sw * 4 + tt
                                    v.wait_ge(p.h["pe"], pe0 + sw * 4 + tt + 1)
                                    v.tensor_copy(dst[:, tc * D + eg * 512:tc * D + (eg + 1) * 512],
                                                  acc[tt][:, :]).then_inc(p.h["vc"], 1)

                p.c["ld"] = ld0 + 16 * n_dma
                p.c["sd"] = sd0 + (32 if do_rope else 0)
                if has_pref:
                    p.c[lwsem[1 - wslot]] += 16
                p.c["pe"] = pe0 + 8
                p.c["vc"] = vc0 + v_total
                p.c["pl"] = pl0 + (24 if do_rope else 0)


def _emit_l0_attn(p, k, io, bufs, cc=None):
    nc = p.nc
    q_tok, k_tok, v_tok, attn_fm = bufs["q_tok"], bufs["k_tok"], bufs["v_tok"], bufs["attn_fm"]
    NPV = 16
    with ExitStack() as ctx:
        E = ctx.enter_context
        prodS = [E(nc.sbuf_tensor(p.uname(f"at_ps{i}"), [128, 2048], BF16)) for i in range(2)]
        prodP = [E(nc.sbuf_tensor(p.uname(f"at_pp{i}"), [128, 2048], BF16)) for i in range(2)]
        prodV = E(nc.sbuf_tensor(p.uname("at_pv"), [128, 4 * 2048], BF16))
        sc = [E(nc.sbuf_tensor(p.uname(f"at_sc{i}"), [128, 256], F32)) for i in range(2)]
        ex = [E(nc.sbuf_tensor(p.uname(f"at_ex{i}"), [128, 256], F32)) for i in range(2)]
        ssum = [E(nc.sbuf_tensor(p.uname(f"at_ss{i}"), [128, 16], F32)) for i in range(2)]
        rcp = [E(nc.sbuf_tensor(p.uname(f"at_rc{i}"), [128, 16], F32)) for i in range(2)]
        pn = [E(nc.sbuf_tensor(p.uname(f"at_pn{i}"), [128, 256], F32)) for i in range(2)]
        otok = [E(nc.sbuf_tensor(p.uname(f"at_ot{i}"), [128, 2048], BF16)) for i in range(2)]
        trp = [E(nc.psum_tensor(p.uname(f"at_tr{i}"), [128, 1024], BF16)) for i in range(2)]

        pe0, vc0, pl0, sc0, ld0, cc0 = (p.c["pe"], p.c["vc"], p.c["pl"], p.c["sc"],
                                        p.c["ld"], p.c["cc1"])
        vbase = [vc0]
        for tc in range(NTT):
            vbase.append(vbase[-1] + 35 + (16 if tc >= 1 else 0) + (NHEAD - NPV))
        vfinal = vbase[NTT] + 16
        sbase = [sc0]
        for tc in range(NTT):
            sbase.append(sbase[-1] + 1 + (16 if tc >= 1 else 0))
        sfinal = sbase[NTT] + 16

        def pncnt(tc):
            return vbase[tc] + 35

        def redcnt(tcp, h):
            if tcp < NTT - 1:
                return vbase[tcp + 1] + 35 + h + 1
            return vbase[NTT] + h + 1

        def expcnt(tc):
            return sbase[tc] + 1

        def cpycnt(tcp, h):
            if tcp < NTT - 1:
                return sbase[tcp + 1] + 1 + h + 1
            return sbase[NTT] + h + 1

        with nc.Block() as blk:
            @blk.vector
            def _(v):
                vcc = vc0

                def emit_pvreds(tcp):
                    nonlocal vcc
                    ot = otok[tcp % 2]
                    if tcp >= 2:
                        v.wait_ge(p.h["pe"], pe0 + (tcp - 2) * 16 + 16)
                    for h in range(NHEAD):
                        if h < NPV:
                            v.wait_ge(p.h["pl"], pl0 + tcp * NPV + h + 1)
                            pr = prodP[(tcp * NPV + h) % 2][:, :]
                        else:
                            v.wait_ge(p.h["vc"], vcc)
                            pr = prodV[:, (h - NPV) * 2048:(h - NPV + 1) * 2048]
                        with p.nc.allow_low_precision(reason="DVE reduce accumulates fp32; bf16 store"):
                            v.tensor_reduce(ot[:, h * 128:(h + 1) * 128],
                                            pr.rearrange("p (d h) -> p d h", d=128),
                                            axis=AX.X, op=ALU.add).then_inc(p.h["vc"], 1)
                        vcc += 1

                for tc in range(NTT):
                    s_ = sc[tc % 2]
                    e_ = ex[tc % 2]
                    if tc >= 2:
                        v.wait_ge(p.h["sc"], expcnt(tc - 2))
                    for h in range(NHEAD):
                        qv = bass.AP(q_tok, tc * D + h * 128, [[NTT * D, 128], [0, 16], [1, 128]])
                        kv = bass.AP(k_tok, tc * D, [[NTT * D, 128], [128, 16], [1, 128]])
                        v.wait_ge(p.h["vc"], vcc)
                        v.tensor_tensor(prodS[h % 2][:, :].rearrange("p (h d) -> p h d", h=16),
                                        qv, kv, op=ALU.mult).then_inc(p.h["vc"], 1)
                        vcc += 1
                        v.wait_ge(p.h["vc"], vcc)
                        v.tensor_reduce(s_[:, h * 16:(h + 1) * 16],
                                        prodS[h % 2][:, :].rearrange("p (h d) -> p h d", h=16),
                                        axis=AX.X, op=ALU.add).then_inc(p.h["vc"], 1)
                        vcc += 1
                    v.wait_ge(p.h["sc"], expcnt(tc))
                    v.tensor_reduce(ssum[tc % 2][:, :], e_[:, :].rearrange("p (h j) -> p h j", h=16),
                                    axis=AX.X, op=ALU.add).then_inc(p.h["vc"], 1)
                    vcc += 1
                    v.wait_ge(p.h["vc"], vcc)
                    v.reciprocal(rcp[tc % 2][:, :], ssum[tc % 2][:, :]).then_inc(p.h["vc"], 1)
                    vcc += 1
                    rcv = bass.AP(rcp[tc % 2], 0, [[16, 128], [1, 16], [0, 16]])
                    v.wait_ge(p.h["vc"], vcc)
                    if tc >= 2:
                        v.wait_ge(p.h["pl"], pl0 + (tc - 1) * NPV)
                    v.tensor_tensor(pn[tc % 2][:, :].rearrange("p (h j) -> p h j", h=16),
                                    e_[:, :].rearrange("p (h j) -> p h j", h=16),
                                    rcv, op=ALU.mult).then_inc(p.h["vc"], 1)
                    vcc += 1
                    if tc >= 1:
                        emit_pvreds(tc - 1)
                    for i in range(NHEAD - NPV):
                        h = NPV + i
                        pv = bass.AP(pn[tc % 2], h * 16, [[256, 128], [0, 128], [1, 16]])
                        vv = bass.AP(v_tok, tc * D, [[NTT * D, 128], [1, 128], [128, 16]])
                        v.wait_ge(p.h["vc"], vcc)
                        v.tensor_tensor(prodV[:, i * 2048:(i + 1) * 2048]
                                        .rearrange("p (d h) -> p d h", d=128),
                                        pv, vv, op=ALU.mult).then_inc(p.h["vc"], 1)
                        vcc += 1
                emit_pvreds(NTT - 1)

            @blk.scalar
            def _(s):
                def emit_copies(tcp):
                    for h in range(NHEAD):
                        s.wait_ge(p.h["pe"], pe0 + tcp * 16 + h + 1)
                        s.copy(attn_fm[:, h * TPC + tcp * 128:h * TPC + (tcp + 1) * 128],
                               trp[(tcp * 16 + h) % 2][:, 0:128]).then_inc(p.h["sc"], 1)
                for tc in range(NTT):
                    s.wait_ge(p.h["vc"], vbase[tc] + 32)
                    s.activation(ex[tc % 2][:, :], sc[tc % 2][:, :], AF.Exp).then_inc(p.h["sc"], 1)
                    if tc >= 1:
                        emit_copies(tc - 1)
                emit_copies(NTT - 1)

            @blk.gpsimd
            def _(g):
                plc = pl0
                ldc = ld0

                def emit_cc(tcp):
                    nonlocal ldc
                    cc1i, cc1o, groups = cc
                    g.wait_ge(p.h["sc"], cpycnt(tcp, 15))
                    for j in range(2 * NRANK):
                        g.dma_start(
                            out=cc1i[tcp].ap()[j * 128:(j + 1) * 128, :]
                            .rearrange("p (h s) -> p h s", h=4),
                            in_=bass.AP(attn_fm, 4 * (j % NRANK) * TPC + tcp * 128,
                                        [[NHEAD * TPC, 128], [TPC, 4], [1, 128]]),
                        ).then_inc(p.h["ld"], 16)
                    ldc += 16 * 2 * NRANK
                    g.wait_ge(p.h["ld"], ldc)
                    with allow_ici_all_to_all():
                        g.collective_compute(
                            "AllToAll", ALU.bypass, replica_groups=groups,
                            ins=[cc1i[tcp].ap()], outs=[cc1o[tcp].ap()]).then_inc(p.h["cc1"])

                for tc in range(NTT):
                    if cc is not None and tc >= 1:
                        emit_cc(tc - 1)
                    for h in range(NPV):
                        j = tc * NPV + h
                        w = pncnt(tc)
                        if j >= 2:
                            tcp, hp = divmod(j - 2, NPV)
                            w = max(w, redcnt(tcp, hp))
                        g.wait_ge(p.h["vc"], w)
                        g.wait_ge(p.h["pl"], plc)
                        pv = bass.AP(pn[tc % 2], h * 16, [[256, 128], [0, 128], [1, 16]])
                        vv = bass.AP(v_tok, tc * D, [[NTT * D, 128], [1, 128], [128, 16]])
                        g.tensor_tensor(prodP[j % 2][:, :].rearrange("p (d h) -> p d h", d=128),
                                        pv, vv, op=ALU.mult).then_inc(p.h["pl"], 1)
                        plc += 1
                if cc is not None:
                    emit_cc(NTT - 1)

            @blk.tensor
            def _(t):
                for tc in range(NTT):
                    ot = otok[tc % 2]
                    for h in range(NHEAD):
                        i = tc * 16 + h
                        t.wait_ge(p.h["vc"], redcnt(tc, h))
                        if i >= 2:
                            tcp, hp = divmod(i - 2, 16)
                            t.wait_ge(p.h["sc"], cpycnt(tcp, hp))
                        t.transpose(trp[i % 2][:, 0:128], ot[:, h * 128:(h + 1) * 128],
                                    k["ident_b"][:, :]).then_inc(p.h["pe"], 1)

        p.c["pe"] = pe0 + 16 * NTT
        p.c["vc"] = vfinal
        p.c["pl"] = pl0 + NPV * NTT
        p.c["sc"] = sfinal
        if cc is not None:
            p.c["ld"] = ld0 + 16 * 2 * NRANK * NTT
            p.c["cc1"] = cc0 + NTT


def _emit_store(p, aps):
    sd0 = p.c["sd"]
    n = len(aps)
    with p.nc.Block() as blk:
        @blk.sync
        def _(sy):
            for dram, sb in aps:
                sy.dma_start(out=dram, in_=sb).then_inc(p.h["sd"], 16)
            sy.wait_ge(p.h["sd"], sd0 + 16 * n)
    p.c["sd"] = sd0 + 16 * n



def _emit_scramble8(p, io, a2abuf8, out_scrT, sel_sb, cc1o=None):
    nc = p.nc
    ld0, vc0 = p.c["ld"], p.c["vc"]
    per_tc = cc1o is not None
    with nc.Block() as blk:
        @blk.gpsimd
        def _(g):
            if per_tc:
                for tc in range(NTT):
                    g.wait_ge(p.h["cc1"], tc + 1)
                    for gg in range(2 * NRANK):
                        g.dma_start(
                            out=bass.AP(a2abuf8, gg * 4096 + tc * 128,
                                        [[32 * TPC, 128], [TPC, 4], [1, 128]]),
                            in_=cc1o[tc].ap()[gg * 128:(gg + 1) * 128, :]
                            .rearrange("p (h c) -> p h c", h=4),
                        ).then_inc(p.h["ld"], 16)
            else:
                g.dma_start(
                    out=a2abuf8[:, :].rearrange("p (r f) -> p r f", r=2 * NRANK),
                    in_=io["a2a"].rearrange("(r p) f -> p r f", p=128),
                ).then_inc(p.h["ld"], 16)

        @blk.vector
        def _(v):
            vcc = vc0
            v.wait_ge(p.h["ld"], ld0 + 16 * (NTT * 2 * NRANK if per_tc else 1))
            for r in range(NRANK):
                for hl in range(4):
                    va = bass.AP(a2abuf8, (r * 4 + hl) * TPC,
                                 [[32 * TPC, 128], [1, 16], [16, 64]])
                    vb = bass.AP(a2abuf8, ((r + NRANK) * 4 + hl) * TPC,
                                 [[32 * TPC, 128], [1, 16], [16, 64]])
                    dstv = bass.AP(out_scrT, hl * 256 + r * 64,
                                   [[16 * TPC, 128], [1024, 16], [1, 64]])
                    v.wait_ge(p.h["vc"], vcc)
                    v.tensor_scalar(dstv, va, sel_sb[:, 0:1], None, op0=ALU.mult).then_inc(p.h["vc"], 1)
                    vcc += 1
                    v.wait_ge(p.h["vc"], vcc)
                    v.scalar_tensor_tensor(dstv, vb, sel_sb[:, 1:2], dstv,
                                           op0=ALU.mult, op1=ALU.add).then_inc(p.h["vc"], 1)
                    vcc += 1
    p.c["ld"] = ld0 + 16 * (NTT * 2 * NRANK if per_tc else 1)
    p.c["vc"] = vc0 + 32


def _ln_scratch(p, ctx, mbc=None, rbc=None, pool_split=True):
    E = ctx.enter_context
    nc = p.nc
    return {
        "mean": E(nc.sbuf_tensor(p.uname("ln_mean"), [1, 512], F32)),
        "msq": E(nc.sbuf_tensor(p.uname("ln_msq"), [1, 512], F32)),
        "var": E(nc.sbuf_tensor(p.uname("ln_var"), [1, 512], F32)),
        "sqt": E(nc.sbuf_tensor(p.uname("ln_sqt"), [1, 512], F32)),
        "rstd": E(nc.sbuf_tensor(p.uname("ln_rstd"), [1, 512], F32)),
        "tmp": E(nc.sbuf_tensor(p.uname("ln_tmp"), [128, 512], F32)),
        "tmp2": E(nc.sbuf_tensor(p.uname("ln_tmp2"), [128, 512], F32)) if pool_split else None,
        "mbs": E(nc.sbuf_tensor(p.uname("ln_mbs"), [128, 512], F32)) if pool_split else None,
        "rbs": E(nc.sbuf_tensor(p.uname("ln_rbs"), [128, 512], F32)) if pool_split else None,
        "mbc": mbc if mbc is not None else E(nc.psum_tensor(p.uname("ln_mbc"), [128, 512], F32)),
        "rbc": rbc if rbc is not None else E(nc.psum_tensor(p.uname("ln_rbc"), [128, 512], F32)),
    }


def _ln_finalize(p, k, stream, half, ps_s, ps_sq, sn, out_f32=None):
    pool_split = sn["mbs"] is not None
    """Finalize post-LN for a 512-token half from stats psums [1,512].

    Chunk normalization split between VectorE (ec 0-7, psum operands) and
    GpSimd (ec 8-15, from sbuf copies of the broadcasts).
    """
    nc = p.nc
    pe0, vc0, sc0, pl0 = p.c["pe"], p.c["vc"], p.c["sc"], p.c["pl"]
    with nc.Block() as blk:
        @blk.vector
        def _(v, half=half):
            vcc = vc0
            v.tensor_scalar(sn["mean"][:, :], ps_s[0:1, :], 1.0 / D, None, op0=ALU.mult).then_inc(p.h["vc"], 1)
            vcc += 1
            v.wait_ge(p.h["vc"], vcc)
            v.tensor_tensor(sn["msq"][:, :], sn["mean"][:, :], sn["mean"][:, :], op=ALU.mult).then_inc(p.h["vc"], 1)
            vcc += 1
            v.wait_ge(p.h["vc"], vcc)
            v.scalar_tensor_tensor(sn["var"][:, :], ps_sq[0:1, :], 1.0 / D, sn["msq"][:, :],
                                   op0=ALU.mult, op1=ALU.subtract).then_inc(p.h["vc"], 1)
            vcc += 1
            v.wait_ge(p.h["sc"], sc0 + 1)
            v.reciprocal(sn["rstd"][:, :], sn["sqt"][:, :]).then_inc(p.h["vc"], 1)
            vcc += 1
            v.wait_ge(p.h["pe"], pe0 + 2)
            if pool_split:
                v.wait_ge(p.h["sc"], sc0 + 3)
            for ec in range(8 if pool_split else NEC):
                sl_ = stream[:, ec * TPC + half * 512: ec * TPC + half * 512 + 512]
                v.wait_ge(p.h["vc"], vcc)
                v.tensor_tensor(sn["tmp"][:, :], sl_, sn["mbc"][:, :], op=ALU.subtract).then_inc(p.h["vc"], 1)
                vcc += 1
                v.wait_ge(p.h["vc"], vcc)
                if out_f32 is not None:
                    v.tensor_tensor(out_f32[:, ec * 512:(ec + 1) * 512],
                                    sn["tmp"][:, :], sn["rbc"][:, :], op=ALU.mult).then_inc(p.h["vc"], 1)
                    vcc += 1
                    v.wait_ge(p.h["vc"], vcc)
                v.tensor_tensor(sl_, sn["tmp"][:, :], sn["rbc"][:, :], op=ALU.mult).then_inc(p.h["vc"], 1)
                vcc += 1

        @blk.scalar
        def _(s):
            s.wait_ge(p.h["vc"], vc0 + 3)
            s.activation(sn["sqt"][:, :], sn["var"][:, :], AF.Sqrt, bias=k["eps11"][:, :]).then_inc(p.h["sc"], 1)
            if pool_split:
                s.wait_ge(p.h["pe"], pe0 + 2)
                s.copy(sn["mbs"][:, :], sn["mbc"][:, :]).then_inc(p.h["sc"], 1)
                s.copy(sn["rbs"][:, :], sn["rbc"][:, :]).then_inc(p.h["sc"], 1)

        @blk.gpsimd
        def _(g, half=half):
            if not pool_split:
                return
            plc = pl0
            g.wait_ge(p.h["sc"], sc0 + 3)
            for ec in range(8, NEC):
                sl_ = stream[:, ec * TPC + half * 512: ec * TPC + half * 512 + 512]
                g.wait_ge(p.h["pl"], plc)
                g.tensor_tensor(sn["tmp2"][:, :], sl_, sn["mbs"][:, :], op=ALU.subtract).then_inc(p.h["pl"], 1)
                plc += 1
                g.wait_ge(p.h["pl"], plc)
                if out_f32 is not None:
                    g.tensor_tensor(out_f32[:, ec * 512:(ec + 1) * 512],
                                    sn["tmp2"][:, :], sn["rbs"][:, :], op=ALU.mult).then_inc(p.h["pl"], 1)
                    plc += 1
                    g.wait_ge(p.h["pl"], plc)
                g.tensor_tensor(sl_, sn["tmp2"][:, :], sn["rbs"][:, :], op=ALU.mult).then_inc(p.h["pl"], 1)
                plc += 1

        @blk.tensor
        def _(t):
            t.wait_ge(p.h["vc"], vc0 + 1)
            t.matmul(sn["mbc"][:, :], k["ones_row_f"][:, :], sn["mean"][:, :],
                     start=True, stop=True).then_inc(p.h["pe"], 1)
            t.wait_ge(p.h["vc"], vc0 + 4)
            t.matmul(sn["rbc"][:, :], k["ones_row_f"][:, :], sn["rstd"][:, :],
                     start=True, stop=True).then_inc(p.h["pe"], 1)
    nf = 3 if out_f32 is not None else 2
    p.c["pe"] = pe0 + 2
    if pool_split:
        p.c["vc"] = vc0 + 4 + 8 * nf
        p.c["sc"] = sc0 + 3
        p.c["pl"] = pl0 + 8 * nf
    else:
        p.c["vc"] = vc0 + 4 + NEC * nf
        p.c["sc"] = sc0 + 1


def _emit_proj_b(p, k, io, wkey, rhs_buf, stream, mode, bias_pp=None, dst=None,
                 sn=None, out_f32=None, n_out=NEC, n_k=NEC):
    nc = p.nc
    with ExitStack() as ctx:
        E = ctx.enter_context
        wb = [E(nc.sbuf_tensor(p.uname(f"pb_w{i}"), [128, n_k * 128], BF16)) for i in range(2)]
        acc = [E(nc.psum_tensor(p.uname(f"pb_ps{i}"), [128, 512], F32)) for i in range(2)]
        wsems = ("lwa", "lwb")
        if mode == "resid_ln":
            ps_s = [E(nc.psum_tensor(p.uname(f"pb_s{i}"), [128, 512], F32)) for i in range(2)]
            ps_sq = [E(nc.psum_tensor(p.uname(f"pb_q{i}"), [128, 512], F32)) for i in range(2)]
            ysq = [E(nc.sbuf_tensor(p.uname(f"pb_y{i}"), [128, 512], BF16)) for i in range(2)]
            bias_sb = E(nc.sbuf_tensor(p.uname("pb_bias"), [128, NEC], F32)) if bias_pp else None
            if bias_pp:
                ld0 = p.c["ld"]
                with nc.Block() as blk:
                    @blk.gpsimd
                    def _(g):
                        g.dma_start(out=bias_sb[:, :], in_=io[bias_pp]).then_inc(p.h["ld"], 16)
                        g.wait_ge(p.h["ld"], ld0 + 16)
                p.c["ld"] = ld0 + 16
        with nc.Block() as blk:
            @blk.gpsimd
            def _(g):
                g.dma_start(out=wb[0][:, :].rearrange("p (c m) -> p c m", c=n_k),
                            in_=io[wkey][0].rearrange("c p m -> p c m")).then_inc(p.h["lwa"], 16)
        p.c["lwa"] += 16

        pe0, vc0 = p.c["pe"], p.c["vc"]
        resid = mode == "resid_ln"
        pe_pos = {}
        cnt = 0
        order = []
        for oc in range(n_out):
            order += [("c", oc, 0), ("c", oc, 1)]
            if resid and oc >= 1:
                for half in range(2):
                    order += [("s", oc - 1, half), ("q", oc - 1, half)]
        if resid:
            for half in range(2):
                order += [("s", n_out - 1, half), ("q", n_out - 1, half)]
        for it in order:
            cnt += 1
            pe_pos[it] = cnt
        vper = 4 if resid else 2
        cur = {w: p.c[w] for w in wsems}
        lw_cnt = {0: cur[wsems[0]]}
        for oc in range(1, n_out):
            cur[wsems[oc % 2]] += 16
            lw_cnt[oc] = cur[wsems[oc % 2]]

        with nc.Block() as blk:
            @blk.gpsimd
            def _(g):
                for oc in range(1, n_out):
                    if oc >= 2:
                        g.wait_ge(p.h["pe"], pe0 + pe_pos[("c", oc - 2, 1)])
                    g.dma_start(out=wb[oc % 2][:, :].rearrange("p (c m) -> p c m", c=n_k),
                                in_=io[wkey][oc].rearrange("c p m -> p c m")
                                ).then_inc(p.h[wsems[oc % 2]], 16)

            @blk.tensor
            def _(t):
                def stats(j):
                    for half in range(2):
                        t.wait_ge(p.h["vc"], vc0 + vper * j + 2 * half + 2)
                        t.matmul(ps_s[half][0:1, :], k["ones_col_b"][:, :],
                                 stream[:, j * TPC + half * 512: j * TPC + half * 512 + 512],
                                 start=(j == 0), stop=(j == n_out - 1)).then_inc(p.h["pe"], 1)
                        t.matmul(ps_sq[half][0:1, :], k["ones_col_b"][:, :],
                                 ysq[half][:, :],
                                 start=(j == 0), stop=(j == n_out - 1)).then_inc(p.h["pe"], 1)
                for oc in range(n_out):
                    t.wait_ge(p.h[wsems[oc % 2]], lw_cnt[oc])
                    for half in range(2):
                        if oc >= 1:
                            t.wait_ge(p.h["vc"], vc0 + vper * (oc - 1) + (2 * half if resid else half) + 1)
                        for kc in range(n_k):
                            mm = t.matmul(acc[half][:, :],
                                          wb[oc % 2][:, kc * 128:(kc + 1) * 128],
                                          rhs_buf[:, kc * TPC + half * 512:kc * TPC + half * 512 + 512],
                                          start=(kc == 0), stop=(kc == n_k - 1))
                            if kc == n_k - 1:
                                mm.then_inc(p.h["pe"], 1)
                    if resid and oc >= 1:
                        stats(oc - 1)
                if resid:
                    stats(n_out - 1)

            @blk.vector
            def _(v):
                vcc = vc0
                for oc in range(n_out):
                    for half in range(2):
                        v.wait_ge(p.h["pe"], pe0 + pe_pos[("c", oc, half)])
                        if resid:
                            sl_ = stream[:, oc * TPC + half * 512: oc * TPC + half * 512 + 512]
                            if bias_pp:
                                v.scalar_tensor_tensor(sl_, acc[half][:, :],
                                                       bias_sb[:, oc:oc + 1], sl_,
                                                       op0=ALU.add, op1=ALU.add).then_inc(p.h["vc"], 1)
                            else:
                                v.tensor_tensor(sl_, acc[half][:, :], sl_,
                                                op=ALU.add).then_inc(p.h["vc"], 1)
                            vcc += 1
                            v.wait_ge(p.h["vc"], vcc)
                            if oc >= 1:
                                v.wait_ge(p.h["pe"], pe0 + pe_pos[("q", oc - 1, half)])
                            v.tensor_tensor(ysq[half][:, :], sl_, sl_, op=ALU.mult).then_inc(p.h["vc"], 1)
                            vcc += 1
                        else:
                            v.tensor_copy(dst[:, oc * TPC + half * 512: oc * TPC + half * 512 + 512],
                                          acc[half][:, :]).then_inc(p.h["vc"], 1)
                            vcc += 1
        p.c["pe"] = pe0 + len(order)
        p.c["vc"] = vc0 + vper * n_out
        for w in wsems:
            p.c[w] = cur[w]
        if mode == "resid_ln":
            for half in range(2):
                _ln_finalize(p, k, stream, half, ps_s[half], ps_sq[half], sn,
                             out_f32=out_f32)


def _emit_ffn(p, k, io, stream, wg_key, wu_key, wd_key, sg_key, su_key, sd_key,
              fuseT, out_f32=None, o_y=None):
    nc = p.nc
    with ExitStack() as ctx:
        E = ctx.enter_context
        wgb = [E(nc.sbuf_tensor(p.uname(f"ff_wg{i}"), [128, 2 * NEC * 128], BF16)) for i in range(2)]
        wub = [E(nc.sbuf_tensor(p.uname(f"ff_wu{i}"), [128, 2 * NEC * 128], BF16)) for i in range(2)]
        sg_sb = E(nc.sbuf_tensor(p.uname("ff_sg"), [128, NIC], F32))
        su_sb = E(nc.sbuf_tensor(p.uname("ff_su"), [128, NIC], F32))
        sd_sb = E(nc.sbuf_tensor(p.uname("ff_sd"), [128, NEC], F32))
        silu_sb = [E(nc.sbuf_tensor(p.uname(f"ff_si{i}"), [128, 512], BF16)) for i in range(2)]
        ysq = [E(nc.sbuf_tensor(p.uname(f"ff_ysq{i}"), [128, 512], BF16)) for i in range(2)]
        pg = [E(nc.psum_tensor(p.uname(f"ff_pg{i}"), [128, 512], F32)) for i in range(2)]
        pu = [E(nc.psum_tensor(p.uname(f"ff_pu{i}"), [128, 512], F32)) for i in range(2)]
        py = [E(nc.psum_tensor(p.uname(f"ff_py{i}"), [128, 512], F32)) for i in range(2)]
        ps_s = E(nc.psum_tensor(p.uname("ff_s"), [128, 512], F32))
        ps_sq = E(nc.psum_tensor(p.uname("ff_q"), [128, 512], F32))

        ld0 = p.c["ld"]
        with nc.Block() as blk:
            @blk.gpsimd
            def _(g):
                g.dma_start(out=sg_sb[:, :], in_=io[sg_key]).then_inc(p.h["ld"], 16)
                g.dma_start(out=su_sb[:, :], in_=io[su_key]).then_inc(p.h["ld"], 16)
                g.dma_start(out=sd_sb[:, :], in_=io[sd_key]).then_inc(p.h["ld"], 16)
                g.wait_ge(p.h["ld"], ld0 + 48)
        p.c["ld"] = ld0 + 48

        for half in range(2):
            with nc.Block() as blk:
                @blk.gpsimd
                def _(g):
                    g.dma_start(out=wgb[0][:, :].rearrange("p (c m) -> p c m", c=2 * NEC),
                                in_=io[wg_key][0].rearrange("c p m -> p c m")).then_inc(p.h["lwa"], 16)
                    g.dma_start(out=wub[0][:, :].rearrange("p (c m) -> p c m", c=2 * NEC),
                                in_=io[wu_key][0].rearrange("c p m -> p c m")).then_inc(p.h["lwc"], 16)
            p.c["lwa"] += 16
            p.c["lwc"] += 16
            pe0, vc0, sc0 = p.c["pe"], p.c["vc"], p.c["sc"]
            lwg = [p.c["lwa"], p.c["lwb"]]
            lwu = [p.c["lwc"], p.c["lwd"]]
            need_g = []
            need_u = []
            for q in range(NIC // 2):
                need_g.append(lwg[q % 2])
                need_u.append(lwu[q % 2])
                if q + 1 < NIC // 2:
                    lwg[(q + 1) % 2] += 16
                    lwu[(q + 1) % 2] += 16
            gsem = ("lwa", "lwb")
            usem = ("lwc", "lwd")
            with nc.Block() as blk:
                @blk.gpsimd
                def _(g, half=half):
                    for q in range(NIC // 2):
                        if q + 1 < NIC // 2:
                            if q >= 1:
                                g.wait_ge(p.h["pe"], pe0 + 2 * (2 * q - 1) + 2)
                            g.dma_start(out=wgb[(q + 1) % 2][:, :].rearrange("p (c m) -> p c m", c=2 * NEC),
                                        in_=io[wg_key][q + 1].rearrange("c p m -> p c m")
                                        ).then_inc(p.h[gsem[(q + 1) % 2]], 16)
                            g.dma_start(out=wub[(q + 1) % 2][:, :].rearrange("p (c m) -> p c m", c=2 * NEC),
                                        in_=io[wu_key][q + 1].rearrange("c p m -> p c m")
                                        ).then_inc(p.h[usem[(q + 1) % 2]], 16)

                @blk.tensor
                def _(t, half=half):
                    for ic in range(NIC):
                        q, qi = divmod(ic, 2)
                        if qi == 0:
                            t.wait_ge(p.h[gsem[q % 2]], need_g[q])
                            t.wait_ge(p.h[usem[q % 2]], need_u[q])
                        if ic >= 2:
                            t.wait_ge(p.h["sc"], sc0 + (ic - 2) + 1)
                            t.wait_ge(p.h["vc"], vc0 + (ic - 2) + 1)
                        for dc in range(NEC):
                            t.matmul(pg[ic % 2][:, :], wgb[q % 2][:, (qi * NEC + dc) * 128:(qi * NEC + dc + 1) * 128],
                                     stream[:, dc * TPC + half * 512:dc * TPC + half * 512 + 512],
                                     start=(dc == 0), stop=(dc == NEC - 1)
                                     ).then_inc(p.h["pe"], 1) if dc == NEC - 1 else t.matmul(
                                pg[ic % 2][:, :], wgb[q % 2][:, (qi * NEC + dc) * 128:(qi * NEC + dc + 1) * 128],
                                stream[:, dc * TPC + half * 512:dc * TPC + half * 512 + 512],
                                start=(dc == 0), stop=False)
                        for dc in range(NEC):
                            t.matmul(pu[ic % 2][:, :], wub[q % 2][:, (qi * NEC + dc) * 128:(qi * NEC + dc + 1) * 128],
                                     stream[:, dc * TPC + half * 512:dc * TPC + half * 512 + 512],
                                     start=(dc == 0), stop=(dc == NEC - 1)
                                     ).then_inc(p.h["pe"], 1) if dc == NEC - 1 else t.matmul(
                                pu[ic % 2][:, :], wub[q % 2][:, (qi * NEC + dc) * 128:(qi * NEC + dc + 1) * 128],
                                stream[:, dc * TPC + half * 512:dc * TPC + half * 512 + 512],
                                start=(dc == 0), stop=False)

                @blk.scalar
                def _(s, half=half):
                    s.wait_ge(p.h["ld"], ld0 + 48)
                    for ic in range(NIC):
                        s.wait_ge(p.h["pe"], pe0 + 2 * ic + 1)
                        if ic >= 2:
                            s.wait_ge(p.h["vc"], vc0 + (ic - 2) + 1)
                        s.activation(silu_sb[ic % 2][:, :], pg[ic % 2][:, :], AF.Silu,
                                     scale=sg_sb[:, ic:ic + 1]).then_inc(p.h["sc"], 1)

                @blk.vector
                def _(v, half=half):
                    for ic in range(NIC):
                        v.wait_ge(p.h["pe"], pe0 + 2 * ic + 2)
                        v.wait_ge(p.h["sc"], sc0 + ic + 1)
                        v.scalar_tensor_tensor(fuseT[:, ic * 512:(ic + 1) * 512],
                                               pu[ic % 2][:, :], su_sb[:, ic:ic + 1],
                                               silu_sb[ic % 2][:, :],
                                               op0=ALU.mult, op1=ALU.mult).then_inc(p.h["vc"], 1)
            p.c["pe"] = pe0 + 2 * NIC
            p.c["sc"] = sc0 + NIC
            p.c["vc"] = vc0 + NIC
            p.c["lwa"], p.c["lwb"] = lwg[0], lwg[1]
            p.c["lwc"], p.c["lwd"] = lwu[0], lwu[1]

            ctxd = ExitStack()
            wdb = [ctxd.enter_context(nc.sbuf_tensor(p.uname(f"ff_wd{i}"), [128, 64 * 128], BF16))
                   for i in range(2)]
            with nc.Block() as blk:
                @blk.gpsimd
                def _(g):
                    g.dma_start(out=wdb[0][:, :].rearrange("p (c m) -> p c m", c=64),
                                in_=io[wd_key][0].rearrange("c p m -> p c m")).then_inc(p.h["lwa"], 16)
            p.c["lwa"] += 16
            pe0, vc0 = p.c["pe"], p.c["vc"]
            lwd_ = [p.c["lwa"], p.c["lwb"]]
            need_d = []
            for dc in range(NEC):
                need_d.append(lwd_[dc % 2])
                if dc + 1 < NEC:
                    lwd_[(dc + 1) % 2] += 16
            dsem = ("lwa", "lwb")
            with nc.Block() as blk:
                @blk.gpsimd
                def _(g):
                    for dc in range(1, NEC):
                        if dc >= 2:
                            dcp = dc - 2
                            g.wait_ge(p.h["pe"], pe0 + (1 if dcp == 0 else 3 * dcp - 1))
                        g.dma_start(out=wdb[dc % 2][:, :].rearrange("p (c m) -> p c m", c=64),
                                    in_=io[wd_key][dc].rearrange("c p m -> p c m")
                                    ).then_inc(p.h[dsem[dc % 2]], 16)

                @blk.tensor
                def _(t, half=half):
                    def chain_inc(dc):
                        return pe0 + (1 if dc == 0 else 3 * dc - 1)
                    def emit_stats(j):
                        t.wait_ge(p.h["vc"], vc0 + 2 * j + 2)
                        t.matmul(ps_s[0:1, :], k["ones_col_b"][:, :],
                                 stream[:, j * TPC + half * 512:j * TPC + half * 512 + 512],
                                 start=(j == 0), stop=(j == NEC - 1)).then_inc(p.h["pe"], 1)
                        t.matmul(ps_sq[0:1, :], k["ones_col_b"][:, :], ysq[j % 2][:, :],
                                 start=(j == 0), stop=(j == NEC - 1)).then_inc(p.h["pe"], 1)
                    for dc in range(NEC):
                        if dc >= 2:
                            t.wait_ge(p.h["vc"], vc0 + 2 * (dc - 2) + 2)
                        t.wait_ge(p.h[dsem[dc % 2]], need_d[dc])
                        for ic in range(NIC):
                            mm = t.matmul(py[dc % 2][:, :],
                                          wdb[dc % 2][:, ic * 128:(ic + 1) * 128],
                                          fuseT[:, ic * 512:(ic + 1) * 512],
                                          start=(ic == 0), stop=(ic == NIC - 1))
                            if ic == NIC - 1:
                                mm.then_inc(p.h["pe"], 1)
                        if dc >= 1:
                            emit_stats(dc - 1)
                    emit_stats(NEC - 1)

                @blk.vector
                def _(v, half=half):
                    vcc = vc0
                    for dc in range(NEC):
                        v.wait_ge(p.h["pe"], pe0 + (1 if dc == 0 else 3 * dc - 1))
                        if dc >= 2:
                            v.wait_ge(p.h["pe"], pe0 + 3 * (dc - 2) + 4)
                        sl_ = stream[:, dc * TPC + half * 512:dc * TPC + half * 512 + 512]
                        v.scalar_tensor_tensor(sl_, py[dc % 2][:, :], sd_sb[:, dc:dc + 1], sl_,
                                               op0=ALU.mult, op1=ALU.add).then_inc(p.h["vc"], 1)
                        vcc += 1
                        v.wait_ge(p.h["vc"], vcc)
                        v.tensor_tensor(ysq[dc % 2][:, :], sl_, sl_, op=ALU.mult).then_inc(p.h["vc"], 1)
                        vcc += 1
            p.c["pe"] = pe0 + 3 * NEC
            p.c["vc"] = vc0 + 2 * NEC
            p.c["lwa"], p.c["lwb"] = lwd_[0], lwd_[1]
            ctxd.close()

            with ExitStack() as ctx2:
                sn = _ln_scratch(p, ctx2, mbc=pg[0], rbc=pg[1],
                                 pool_split=(out_f32 is None))
                _ln_finalize(p, k, stream, half, ps_s, ps_sq, sn, out_f32=out_f32)
            if out_f32 is not None and o_y is not None:
                _emit_final_store_half(p, k, out_f32, o_y, half, pg)



def _emit_kv_transpose(p, k, k_tok, v_tok, kvT):
    nc = p.nc
    with ExitStack() as ctx:
        E = ctx.enter_context
        kvpc = [E(nc.sbuf_tensor(p.uname(f"kv_pc{i}"), [128, D], BF16)) for i in range(1)]
        trp = [E(nc.psum_tensor(p.uname(f"kv_tr{i}"), [128, 1024], BF16)) for i in range(2)]
        for tc in range(NTT):
            pe0, vc0 = p.c["pe"], p.c["vc"]
            with nc.Block() as blk:
                @blk.vector
                def _(v, tc=tc, pe0=pe0, vc0=vc0):
                    v.tensor_tensor(kvpc[0][:, :], k_tok[:, tc * D:(tc + 1) * D],
                                    v_tok[:, tc * D:(tc + 1) * D], op=ALU.mult).then_inc(p.h["vc"], 1)
                    for ec in range(NEC):
                        v.wait_ge(p.h["pe"], pe0 + ec + 1)
                        v.tensor_copy(kvT[:, ec * TPC + tc * 128:ec * TPC + (tc + 1) * 128],
                                      trp[ec % 2][:, 0:128]).then_inc(p.h["vc"], 1)

                @blk.tensor
                def _(t, tc=tc, pe0=pe0, vc0=vc0):
                    t.wait_ge(p.h["vc"], vc0 + 1)
                    for ec in range(NEC):
                        if ec >= 2:
                            t.wait_ge(p.h["vc"], vc0 + 1 + ec - 1)
                        t.transpose(trp[ec % 2][:, 0:128], kvpc[0][:, ec * 128:(ec + 1) * 128],
                                    k["ident_b"][:, :]).then_inc(p.h["pe"], 1)
            p.c["pe"] = pe0 + NEC
            p.c["vc"] = vc0 + 1 + NEC


def _emit_scan(p, k, kvT, kvt_loc, blocksums):
    nc = p.nc
    with ExitStack() as ctx:
        E = ctx.enter_context
        zeros = E(nc.sbuf_tensor(p.uname("sc_zero"), [128, TPC], F32))
        pl0 = p.c["pl"]
        with nc.Block() as blk:
            @blk.gpsimd
            def _(g):
                g.memset(zeros[:, :], 0.0).then_inc(p.h["pl"], 1)
        p.c["pl"] = pl0 + 1
        vc0 = p.c["vc"]
        with nc.Block() as blk:
            @blk.vector
            def _(v):
                vcc = vc0
                v.wait_ge(p.h["pl"], pl0 + 1)
                for ec in range(NEC):
                    with p.nc.allow_low_precision(reason="scan state fp32; bf16 store"):
                        v.tensor_tensor_scan(kvt_loc[:, ec * TPC:(ec + 1) * TPC],
                                             kvT[:, ec * TPC:(ec + 1) * TPC], zeros[:, :],
                                             0.0, op0=ALU.add, op1=ALU.add).then_inc(p.h["vc"], 1)
                    vcc += 1
                    v.wait_ge(p.h["vc"], vcc)
                    v.tensor_copy(blocksums[:, ec:ec + 1],
                                  kvt_loc[:, (ec + 1) * TPC - 1:(ec + 1) * TPC]).then_inc(p.h["vc"], 1)
                    vcc += 1
        p.c["vc"] = vc0 + 2 * NEC


def _emit_l1b_vhat(p, k, io, qT1, kvt, vhat):
    nc = p.nc
    with ExitStack() as ctx:
        E = ctx.enter_context
        bs4 = E(nc.sbuf_tensor(p.uname("vb_bs4"), [128, 4 * NEC], F32))
        mask = E(nc.sbuf_tensor(p.uname("vb_mask"), [128, NRANK], F32))
        offs = E(nc.sbuf_tensor(p.uname("vb_offs"), [128, NEC], F32))
        kvf = E(nc.sbuf_tensor(p.uname("vb_kvf"), [128, TPC], F32))
        ld0, vc0, pl0 = p.c["ld"], p.c["vc"], p.c["pl"]
        with nc.Block() as blk:
            @blk.gpsimd
            def _(g):
                g.dma_start(out=bs4[:, :].rearrange("p (r c) -> p r c", r=4),
                            in_=io["bs_all"].rearrange("r p c -> p r c")).then_inc(p.h["ld"], 16)
                g.dma_start(out=mask[:, :], in_=io["mask_pp"]).then_inc(p.h["ld"], 16)
                g.memset(offs[:, :], 0.0).then_inc(p.h["pl"], 1)

            @blk.vector
            def _(v):
                vcc = vc0
                v.wait_ge(p.h["ld"], ld0 + 32)
                v.wait_ge(p.h["pl"], pl0 + 1)
                for r in range(NRANK):
                    v.wait_ge(p.h["vc"], vcc)
                    v.scalar_tensor_tensor(offs[:, :], bs4[:, r * NEC:(r + 1) * NEC],
                                           mask[:, r:r + 1], offs[:, :],
                                           op0=ALU.mult, op1=ALU.add).then_inc(p.h["vc"], 1)
                    vcc += 1
        p.c["ld"] = ld0 + 32
        p.c["pl"] = pl0 + 1
        p.c["vc"] = vc0 + NRANK
        vc0 = p.c["vc"]
        with nc.Block() as blk:
            @blk.vector
            def _(v):
                vcc = vc0
                for ec in range(NEC):
                    v.wait_ge(p.h["vc"], vcc)
                    v.tensor_scalar(kvf[:, :], kvt[:, ec * TPC:(ec + 1) * TPC],
                                    offs[:, ec:ec + 1], None, op0=ALU.add).then_inc(p.h["vc"], 1)
                    vcc += 1
                    v.wait_ge(p.h["vc"], vcc)
                    v.scalar_tensor_tensor(vhat[:, ec * TPC:(ec + 1) * TPC],
                                           qT1[:, ec * TPC:(ec + 1) * TPC], L1_DENOM, kvf[:, :],
                                           op0=ALU.mult, op1=ALU.mult).then_inc(p.h["vc"], 1)
                    vcc += 1
        p.c["vc"] = vc0 + 2 * NEC


def build_l1a():
    p = P()
    io = {
        "ident": p.dram_in("ident", [128, 128], F32),
        "stream_in": p.dram_in("stream_in", [128, NEC * TPC], BF16),
        "l1_wq_b": p.dram_in("l1_wq_b", [NEC, NEC, 128, 128], F32),
        "l1_wk_a": p.dram_in("l1_wk_a", [NEC, NEG, 128, 512], F32),
        "l1_wv_a": p.dram_in("l1_wv_a", [NEC, NEG, 128, 512], F32),
    }
    o_q = p.dram_out("o_q", [128, NEC * TPC], BF16)
    o_kvt = p.dram_out("o_kvt", [128, NEC * TPC], BF16)
    o_bs = p.dram_out("o_bs", [128, NEC], F32)
    with p.ctx:
        k = _consts(p)
        stream = p.sb("stream", [128, NEC * TPC], BF16)
        qT1 = p.sb("qT1", [128, NEC * TPC], BF16)
        kvT = p.sb("kvT", [128, NEC * TPC], BF16)
        blocksums = p.sb("blocksums", [128, NEC], F32)
        _emit_init(p, k, io)
        _emit_load_sb(p, [(stream[:, :], io["stream_in"])])
        _emit_proj_b(p, k, io, "l1_wq_b", stream, None, "copy", dst=qT1)
        with ExitStack() as ctx2:
            k_tok = ctx2.enter_context(p.nc.sbuf_tensor(p.uname("k_tok"), [128, NTT * D], BF16))
            v_tok = ctx2.enter_context(p.nc.sbuf_tensor(p.uname("v_tok"), [128, NTT * D], BF16))
            _emit_qkv(p, k, io, stream,
                      outs={"k": k_tok, "v": v_tok},
                      wnames={"k": "l1_wk_a", "v": "l1_wv_a"},
                      biases={}, rope=set())
            _emit_kv_transpose(p, k, k_tok, v_tok, kvT)
        kvt_loc = p.sb("kvt_loc", [128, NEC * TPC], BF16)
        _emit_scan(p, k, kvT, kvt_loc, blocksums)
        _emit_store(p, [(o_q, qT1[:, :]), (o_kvt, kvt_loc[:, :]), (o_bs, blocksums[:, :])])
    return p.nc


def build_l1b():
    p = P()
    io = {
        "ident": p.dram_in("ident", [128, 128], F32),
        "stream_in": p.dram_in("stream_in", [128, NEC * TPC], BF16),
        "q_in": p.dram_in("q_in", [128, NEC * TPC], BF16),
        "kvt_in": p.dram_in("kvt_in", [128, NEC * TPC], BF16),
        "bs_all": p.dram_in("bs_all", [NRANK, 128, NEC], F32),
        "mask_pp": p.dram_in("mask_pp", [128, NRANK], F32),
        "l1_wo_b": p.dram_in("l1_wo_b", [NEC, NEC, 128, 128], F32),
    }
    o_stream = p.dram_out("o_stream", [128, NEC * TPC], BF16)
    with p.ctx:
        k = _consts(p)
        stream = p.sb("stream", [128, NEC * TPC], BF16)
        qT1 = p.sb("qT1", [128, NEC * TPC], BF16)
        kvt = p.sb("kvt", [128, NEC * TPC], BF16)
        vhat = p.sb("vhat", [128, NEC * TPC], BF16)
        _emit_init(p, k, io)
        _emit_load_sb(p, [(stream[:, :], io["stream_in"]),
                          (qT1[:, :], io["q_in"]), (kvt[:, :], io["kvt_in"])])
        _emit_l1b_vhat(p, k, io, qT1, kvt, vhat)
        with ExitStack() as ctx2:
            sn = _ln_scratch(p, ctx2)
            _emit_proj_b(p, k, io, "l1_wo_b", vhat, stream, "resid_ln", sn=sn)
        _emit_store(p, [(o_stream, stream[:, :])])
    return p.nc


def build_l0a(debug=False):
    p = P()
    io = {
        "ident": p.dram_in("ident", [128, 128], F32),
        "x": p.dram_in("x", [TPC, D], F32),
        "l0_wk_a": p.dram_in("l0_wk_a", [NEC, NEG, 128, 512], F32),
        "l0_wv_a": p.dram_in("l0_wv_a", [NEC, NEG, 128, 512], F32),
        "l0_wq_a": p.dram_in("l0_wq_a", [NEC, NEG, 128, 512], F32),
        "l0_bq_r": p.dram_in("l0_bq_r", [1, D], F32),
        "l0_bk_r": p.dram_in("l0_bk_r", [1, D], F32),
        "l0_bv_r": p.dram_in("l0_bv_r", [1, D], F32),
        "ropecq": p.dram_in("ropecq", [TPC, TPC], F32),
        "ropesq": p.dram_in("ropesq", [TPC, TPC], F32),
        "ropeck": p.dram_in("ropeck", [TPC, TPC], F32),
        "ropesk": p.dram_in("ropesk", [TPC, TPC], F32),
    }
    o_stream = p.dram_out("o_stream", [128, NEC * TPC], BF16)
    o_attn = p.dram_out("o_attn", [128, NHEAD * TPC], BF16)
    if debug:
        o_q = p.dram_out("o_q", [128, NTT * D], BF16)
        o_k = p.dram_out("o_k", [128, NTT * D], BF16)
        o_v = p.dram_out("o_v", [128, NTT * D], BF16)

    with p.ctx:
        k = _consts(p)
        stream = p.sb("stream", [128, NEC * TPC], BF16)
        q_tok = p.sb("q_tok", [128, NTT * D], BF16)
        k_tok = p.sb("k_tok", [128, NTT * D], BF16)
        v_tok = p.sb("v_tok", [128, NTT * D], BF16)
        _emit_init(p, k, io)
        _emit_load_stream(p, k, io["x"], stream)
        _emit_qkv(p, k, io, stream,
                  outs={"k": k_tok, "v": v_tok, "q": q_tok},
                  wnames={"k": "l0_wk_a", "v": "l0_wv_a", "q": "l0_wq_a"},
                  biases={"k": "l0_bk_r", "v": "l0_bv_r", "q": "l0_bq_r"},
                  rope={"q", "k"})
        attn_fm = p.sb("attn_fm", [128, NHEAD * TPC], BF16)
        bufs = {"q_tok": q_tok, "k_tok": k_tok, "v_tok": v_tok, "attn_fm": attn_fm}
        _emit_l0_attn(p, k, io, bufs)
        stores = [
            (o_stream, stream[:, :]),
            (o_attn, attn_fm[:, :]),
        ]
        if debug:
            stores += [(o_q, q_tok[:, :]), (o_k, k_tok[:, :]), (o_v, v_tok[:, :])]
        _emit_store(p, stores)
    return p.nc


def _l0a_inmap(sh, pc):
    return {"ident": sh["ident"],
            "l0_wk_a": sh["l0_wk_a"], "l0_wv_a": sh["l0_wv_a"], "l0_wq_a": sh["l0_wq_a"],
            "l0_bq_r": sh["l0_bq_r"], "l0_bk_r": sh["l0_bk_r"], "l0_bv_r": sh["l0_bv_r"],
            "x": pc["x"],
            "ropecq": pc["ropecq"], "ropesq": pc["ropesq"],
            "ropeck": pc["ropeck"], "ropesk": pc["ropesk"]}


def _emit_load_sb(p, pairs):
    sd0 = p.c["sd"]
    n = len(pairs)
    with p.nc.Block() as blk:
        @blk.sync
        def _(sy):
            for sb, dram in pairs:
                sy.dma_start(out=sb, in_=dram).then_inc(p.h["sd"], 16)
            sy.wait_ge(p.h["sd"], sd0 + 16 * n)
    p.c["sd"] = sd0 + 16 * n


def build_l0b():
    p = P()
    io = {
        "ident": p.dram_in("ident", [128, 128], F32),
        "a2a": p.dram_in("a2a", [2 * NRANK * 128, 4 * TPC], BF16),
        "sel_pp": p.dram_in("sel_pp", [128, 2], F32),
        "stream_in": p.dram_in("stream_in", [128, NEC * TPC], BF16),
        "l0_wo_b": p.dram_in("l0_wo_b", [NEC, NEC, 128, 128], F32),
        "l0_bo_pp": p.dram_in("l0_bo_pp", [128, NEC], F32),
    }
    o_stream = p.dram_out("o_stream", [128, NEC * TPC], BF16)
    with p.ctx:
        k = _consts(p)
        stream = p.sb("stream", [128, NEC * TPC], BF16)
        a2abuf = p.sb("a2abuf", [128, 2 * NEC * TPC], BF16)
        out_scrT = p.sb("out_scrT", [128, NEC * TPC], BF16)
        sel_sb = p.sb("sel_sb", [128, 2], F32)
        _emit_init(p, k, io)
        _emit_load_sb(p, [(stream[:, :], io["stream_in"]), (sel_sb[:, :], io["sel_pp"])])
        _emit_scramble8(p, io, a2abuf, out_scrT, sel_sb)
        with ExitStack() as ctx2:
            sn = _ln_scratch(p, ctx2)
            _emit_proj_b(p, k, io, "l0_wo_b", out_scrT, stream, "resid_ln",
                         bias_pp="l0_bo_pp", sn=sn)
        _emit_store(p, [(o_stream, stream[:, :])])
    return p.nc


def _host_a2a(attn_fms, b, sl):
    blocks = [np.asarray(attn_fms[g])[:, 4 * sl * TPC:(4 * sl + 4) * TPC]
              for g in range(2 * NRANK)]
    return np.ascontiguousarray(np.concatenate(blocks, axis=0))


def build_ffn(l, final=False):
    p = P()
    io = {
        "ident": p.dram_in("ident", [128, 128], F32),
        "stream_in": p.dram_in("stream_in", [128, NEC * TPC], BF16),
        "wg": p.dram_in("wg", [NIC // 2, 2 * NEC, 128, 128], I8),
        "wu": p.dram_in("wu", [NIC // 2, 2 * NEC, 128, 128], I8),
        "wd": p.dram_in("wd", [NEC, NIC, 128, 128], I8),
        "sg": p.dram_in("sg", [128, NIC], F32),
        "su": p.dram_in("su", [128, NIC], F32),
        "sd": p.dram_in("sd", [128, NEC], F32),
    }
    o_stream = p.dram_out("o_stream", [128, NEC * TPC], BF16)
    o_y = p.dram_out("o_y", [TPC, D], F32) if final else None
    with p.ctx:
        k = _consts(p)
        stream = p.sb("stream", [128, NEC * TPC], BF16)
        _emit_init(p, k, io)
        _emit_load_sb(p, [(stream[:, :], io["stream_in"])])
        with ExitStack() as ctx2:
            fuseT = ctx2.enter_context(p.nc.sbuf_tensor(p.uname("fuseT"), [128, NIC * 512], BF16))
            xTf32 = (ctx2.enter_context(p.nc.sbuf_tensor(p.uname("xTf32"), [128, NEC * 512], F32))
                     if final else None)
            _emit_ffn(p, k, io, stream, "wg", "wu", "wd", "sg", "su", "sd",
                      fuseT, out_f32=xTf32, o_y=o_y)
        if not final:
            _emit_store(p, [(o_stream, stream[:, :])])
        else:
            _emit_store(p, [(o_stream, stream[:, :])])
    return p.nc


def _emit_final_store_half(p, k, xTf32, o_y, half, trp):
    nc = p.nc
    with ExitStack() as ctx:
        E = ctx.enter_context
        stg = [E(nc.sbuf_tensor(p.uname(f"fs_st{i}"), [128, 128], F32)) for i in range(4)]
        pe0, sc0 = p.c["pe"], p.c["sc"]
        fs = ("fsa", "fsb", "fsc", "fsd")
        fs0 = [p.c[n] for n in fs]
        NB = NEC * 4
        with nc.Block() as blk:
            @blk.tensor
            def _(t):
                for i in range(NB):
                    ec, tc = divmod(i, 4)
                    if i >= 2:
                        t.wait_ge(p.h["sc"], sc0 + i - 1)
                    t.transpose(trp[i % 2][:, 0:128],
                                xTf32[:, ec * 512 + tc * 128:ec * 512 + (tc + 1) * 128],
                                k["ident_f"][:, :]).then_inc(p.h["pe"], 1)

            @blk.scalar
            def _(s):
                for i in range(NB):
                    s.wait_ge(p.h["pe"], pe0 + i + 1)
                    if i >= 4:
                        s.wait_ge(p.h[fs[i % 4]], fs0[i % 4] + 16 * (i // 4))
                    s.copy(stg[i % 4][:, :], trp[i % 2][:, 0:128]).then_inc(p.h["sc"], 1)

            @blk.sync
            def _(sy, half=half):
                for i in range(NB):
                    ec, tc = divmod(i, 4)
                    sy.wait_ge(p.h["sc"], sc0 + i + 1)
                    sy.dma_start(
                        out=o_y[half * 512 + tc * 128:half * 512 + (tc + 1) * 128,
                                ec * 128:(ec + 1) * 128],
                        in_=stg[i % 4][:, :]).then_inc(p.h[fs[i % 4]], 16)
                for j in range(4):
                    sy.wait_ge(p.h[fs[j]], fs0[j] + 16 * (NB // 4))
        p.c["pe"] = pe0 + NB
        p.c["sc"] = sc0 + NB
        for j in range(4):
            p.c[fs[j]] += 16 * (NB // 4)


def _emit_cc(p, kind, groups, in_ap, out_ap, src_pairs, ccname, ici=False):
    ld0 = p.c["ld"]
    n = len(src_pairs)
    with p.nc.Block() as blk:
        @blk.gpsimd
        def _(g):
            for dst, sb in src_pairs:
                g.dma_start(out=dst, in_=sb).then_inc(p.h["ld"], 16)
            g.wait_ge(p.h["ld"], ld0 + 16 * n)
            if ici:
                with allow_ici_all_to_all():
                    g.collective_compute(kind, ALU.bypass, replica_groups=groups,
                                         ins=[in_ap], outs=[out_ap]).then_inc(p.h[ccname])
            else:
                g.collective_compute(kind, ALU.bypass, replica_groups=groups,
                                     ins=[in_ap], outs=[out_ap]).then_inc(p.h[ccname])
            g.wait_ge(p.h[ccname], 1)
    p.c["ld"] = ld0 + 16 * n
    p.c[ccname] = 1


def build_fused():
    p = P()
    nc = p.nc
    io = {
        "ident": p.dram_in("ident", [128, 128], F32),
        "x": p.dram_in("x", [TPC, D], F32),
        "l0_wk_a": p.dram_in("l0_wk_a", [NEC, NEG, 128, 512], F32),
        "l0_wv_a": p.dram_in("l0_wv_a", [NEC, NEG, 128, 512], F32),
        "l0_wq_a": p.dram_in("l0_wq_a", [NEC, NEG, 128, 512], F32),
        "l0_bq_r": p.dram_in("l0_bq_r", [1, D], F32),
        "l0_bk_r": p.dram_in("l0_bk_r", [1, D], F32),
        "l0_bv_r": p.dram_in("l0_bv_r", [1, D], F32),
        "ropecq": p.dram_in("ropecq", [TPC, TPC], F32),
        "ropesq": p.dram_in("ropesq", [TPC, TPC], F32),
        "ropeck": p.dram_in("ropeck", [TPC, TPC], F32),
        "ropesk": p.dram_in("ropesk", [TPC, TPC], F32),
        "l0_wo_b": p.dram_in("l0_wo_b", [NEC, NEC, 128, 128], F32),
        "l0_bo_pp": p.dram_in("l0_bo_pp", [128, NEC], F32),
        "f0_wg": p.dram_in("f0_wg", [NIC // 2, 2 * NEC, 128, 128], I8),
        "f0_wu": p.dram_in("f0_wu", [NIC // 2, 2 * NEC, 128, 128], I8),
        "f0_wd": p.dram_in("f0_wd", [NEC, NIC, 128, 128], I8),
        "f0_sg": p.dram_in("f0_sg", [128, NIC], F32),
        "f0_su": p.dram_in("f0_su", [128, NIC], F32),
        "f0_sd": p.dram_in("f0_sd", [128, NEC], F32),
        "l1_wq_b": p.dram_in("l1_wq_b", [NEC, NEC, 128, 128], F32),
        "l1_wk_a": p.dram_in("l1_wk_a", [NEC, NEG, 128, 512], F32),
        "l1_wv_a": p.dram_in("l1_wv_a", [NEC, NEG, 128, 512], F32),
        "l1_wo_b": p.dram_in("l1_wo_b", [NEC, NEC, 128, 128], F32),
        "mask_pp": p.dram_in("mask_pp", [128, NRANK], F32),
        "sel_pp": p.dram_in("sel_pp", [128, 2], F32),
        "f1_wg": p.dram_in("f1_wg", [NIC // 2, 2 * NEC, 128, 128], I8),
        "f1_wu": p.dram_in("f1_wu", [NIC // 2, 2 * NEC, 128, 128], I8),
        "f1_wd": p.dram_in("f1_wd", [NEC, NIC, 128, 128], I8),
        "f1_sg": p.dram_in("f1_sg", [128, NIC], F32),
        "f1_su": p.dram_in("f1_su", [128, NIC], F32),
        "f1_sd": p.dram_in("f1_sd", [128, NEC], F32),
    }
    o_y = p.dram_out("o_y", [TPC, D], F32)
    cc1i = [nc.dram_tensor(f"cc1i{t}", [2 * NRANK * 128, 512], BF16) for t in range(NTT)]
    cc1o = [nc.dram_tensor(f"cc1o{t}", [2 * NRANK * 128, 512], BF16) for t in range(NTT)]
    cc2_in = nc.dram_tensor("cc2_in", [128, NEC], F32)
    cc2_out = nc.dram_tensor("cc2_out", [NRANK * 128, NEC], F32)
    groups = [[0, 1, 2, 3], [4, 5, 6, 7]]

    with p.ctx:
        k = _consts(p)
        stream = p.sb("stream", [128, NEC * TPC], BF16)
        _emit_init(p, k, io)
        _emit_load_stream(p, k, io["x"], stream)
        with ExitStack() as sA:
            E = sA.enter_context
            q_tok = E(nc.sbuf_tensor("q_tok", [128, NTT * D], BF16))
            k_tok = E(nc.sbuf_tensor("k_tok", [128, NTT * D], BF16))
            v_tok = E(nc.sbuf_tensor("v_tok", [128, NTT * D], BF16))
            _emit_qkv(p, k, io, stream,
                      outs={"k": k_tok, "v": v_tok, "q": q_tok},
                      wnames={"k": "l0_wk_a", "v": "l0_wv_a", "q": "l0_wq_a"},
                      biases={"k": "l0_bk_r", "v": "l0_bv_r", "q": "l0_bq_r"},
                      rope={"q", "k"})
            attn_fm = E(nc.sbuf_tensor("attn_fm", [128, NHEAD * TPC], BF16))
            _emit_l0_attn(p, k, io, {"q_tok": q_tok, "k_tok": k_tok,
                                     "v_tok": v_tok, "attn_fm": attn_fm},
                          cc=(cc1i, cc1o, [list(range(2 * NRANK))]))
        with ExitStack() as sB:
            E = sB.enter_context
            a2abuf = E(nc.sbuf_tensor("a2abuf", [128, 2 * NEC * TPC], BF16))
            out_scrT = E(nc.sbuf_tensor("out_scrT", [128, NEC * TPC], BF16))
            sel_sb = E(nc.sbuf_tensor("sel_sb", [128, 2], F32))
            _emit_load_sb(p, [(sel_sb[:, :], io["sel_pp"])])
            _emit_scramble8(p, {}, a2abuf, out_scrT, sel_sb, cc1o=cc1o)
            sn = _ln_scratch(p, sB)
            _emit_proj_b(p, k, io, "l0_wo_b", out_scrT, stream, "resid_ln",
                         bias_pp="l0_bo_pp", sn=sn)
        with ExitStack() as sC:
            fuseT = sC.enter_context(nc.sbuf_tensor(p.uname("fuseT"), [128, NIC * 512], BF16))
            _emit_ffn(p, k, io, stream, "f0_wg", "f0_wu", "f0_wd",
                      "f0_sg", "f0_su", "f0_sd", fuseT)
        with ExitStack() as sD:
            E = sD.enter_context
            qT1 = E(nc.sbuf_tensor("qT1", [128, NEC * TPC], BF16))
            kvT = E(nc.sbuf_tensor("kvT", [128, NEC * TPC], BF16))
            blocksums = E(nc.sbuf_tensor("blocksums", [128, NEC], F32))
            with ExitStack() as sD1:
                k_tok2 = sD1.enter_context(nc.sbuf_tensor(p.uname("k_tok2"), [128, NTT * D], BF16))
                v_tok2 = sD1.enter_context(nc.sbuf_tensor(p.uname("v_tok2"), [128, NTT * D], BF16))
                _emit_proj_b(p, k, io, "l1_wq_b", stream, None, "copy", dst=qT1)
                _emit_qkv(p, k, io, stream,
                          outs={"k": k_tok2, "v": v_tok2},
                          wnames={"k": "l1_wk_a", "v": "l1_wv_a"},
                          biases={}, rope=set())
                _emit_kv_transpose(p, k, k_tok2, v_tok2, kvT)
            kvt_loc = E(nc.sbuf_tensor("kvt_loc", [128, NEC * TPC], BF16))
            _emit_scan(p, k, kvT, kvt_loc, blocksums)
            _emit_cc(p, "AllGather", groups, cc2_in.ap(), cc2_out.ap(),
                     [(cc2_in.ap(), blocksums[:, :])], "cc2")
            vhat = E(nc.sbuf_tensor("vhat", [128, NEC * TPC], BF16))
            _emit_l1b_vhat(p, k, {"bs_all": cc2_out.ap().rearrange("(r p) c -> r p c", p=128),
                                  "mask_pp": io["mask_pp"]}, qT1, kvt_loc, vhat)
            sn = _ln_scratch(p, sD)
            _emit_proj_b(p, k, io, "l1_wo_b", vhat, stream, "resid_ln", sn=sn)
        with ExitStack() as sF:
            fuseT2 = sF.enter_context(nc.sbuf_tensor(p.uname("fuseT"), [128, NIC * 512], BF16))
            xTf32 = sF.enter_context(nc.sbuf_tensor(p.uname("xTf32"), [128, NEC * 512], F32))
            _emit_ffn(p, k, io, stream, "f1_wg", "f1_wu", "f1_wd",
                      "f1_sg", "f1_su", "f1_sd", fuseT2, out_f32=xTf32, o_y=o_y)
    return p.nc


def _fused_inmap(sh, pc):
    m = {"ident": sh["ident"], "x": pc["x"], "mask_pp": pc["mask_pp"],
         "sel_pp": pc["sel_pp"],
         "ropecq": pc["ropecq"], "ropesq": pc["ropesq"],
         "ropeck": pc["ropeck"], "ropesk": pc["ropesk"]}
    for n in ("l0_wq_a", "l0_wk_a", "l0_wv_a", "l0_wo_b", "l0_bq_r", "l0_bk_r",
              "l0_bv_r", "l0_bo_pp", "f0_wg", "f0_wu", "f0_wd", "f0_sg", "f0_su",
              "f0_sd", "l1_wq_b", "l1_wk_a", "l1_wv_a", "l1_wo_b", "f1_wg",
              "f1_wu", "f1_wd", "f1_sg", "f1_su", "f1_sd"):
        m[n] = sh[n]
    return m


def kernel_fused(**inputs):
    sh, percore = _stage(inputs)
    cores = list(range(NCORE))
    res = run_bass_kernel_spmd(build_fused(), [_fused_inmap(sh, percore[c]) for c in cores],
                               core_ids=cores)
    out = np.zeros((2, SEQ, D), dtype=np.float32)
    for c in cores:
        b, sl = divmod(c, NRANK)
        out[b, sl * TPC:(sl + 1) * TPC, :] = np.asarray(res.results[c]["o_y"])
    return out


def kernel_unfused(**inputs):
    sh, percore = _stage(inputs)
    cores = list(range(NCORE))
    run = lambda nc, maps: run_bass_kernel_spmd(nc, maps, core_ids=cores)
    col = lambda res, key: [np.asarray(r[key]) for r in res.results]

    res = run(build_l0a(), [_l0a_inmap(sh, percore[c]) for c in cores])
    streams = col(res, "o_stream")
    attns = col(res, "o_attn")

    maps = []
    for c in cores:
        b, sl = divmod(c, NRANK)
        maps.append({"ident": sh["ident"], "a2a": _host_a2a(attns, b, sl),
                     "sel_pp": percore[c]["sel_pp"],
                     "stream_in": streams[c], "l0_wo_b": sh["l0_wo_b"],
                     "l0_bo_pp": sh["l0_bo_pp"]})
    streams = col(run(build_l0b(), maps), "o_stream")

    maps = [{"ident": sh["ident"], "stream_in": streams[c],
             "wg": sh["f0_wg"], "wu": sh["f0_wu"], "wd": sh["f0_wd"],
             "sg": sh["f0_sg"], "su": sh["f0_su"], "sd": sh["f0_sd"]} for c in cores]
    streams = col(run(build_ffn(0), maps), "o_stream")

    maps = [{"ident": sh["ident"], "stream_in": streams[c],
             "l1_wq_b": sh["l1_wq_b"], "l1_wk_a": sh["l1_wk_a"],
             "l1_wv_a": sh["l1_wv_a"]} for c in cores]
    res = run(build_l1a(), maps)
    qs, kvts, bss = col(res, "o_q"), col(res, "o_kvt"), col(res, "o_bs")

    maps = []
    for c in cores:
        b, sl = divmod(c, NRANK)
        bs_all = np.ascontiguousarray(np.stack([bss[b * NRANK + r] for r in range(NRANK)]))
        maps.append({"ident": sh["ident"], "stream_in": streams[c], "q_in": qs[c],
                     "kvt_in": kvts[c], "bs_all": bs_all,
                     "mask_pp": percore[c]["mask_pp"], "l1_wo_b": sh["l1_wo_b"]})
    streams = col(run(build_l1b(), maps), "o_stream")

    maps = [{"ident": sh["ident"], "stream_in": streams[c],
             "wg": sh["f1_wg"], "wu": sh["f1_wu"], "wd": sh["f1_wd"],
             "sg": sh["f1_sg"], "su": sh["f1_su"], "sd": sh["f1_sd"]} for c in cores]
    ys = col(run(build_ffn(1, final=True), maps), "o_y")

    out = np.zeros((2, SEQ, D), dtype=np.float32)
    for c in cores:
        b, sl = divmod(c, NRANK)
        out[b, sl * TPC:(sl + 1) * TPC, :] = ys[c]
    return out


def kernel(**inputs):
    return kernel_fused(**inputs)


# version 14 (final)
# speedup vs baseline: 1.0888x; 1.0089x over previous
import numpy as np
import ml_dtypes
from contextlib import ExitStack
import concourse.bass as bass
import concourse.mybir as mybir
from concourse.bass_utils import run_bass_kernel_spmd
from concourse.aot_env import allow_ici_all_to_all

F32 = mybir.dt.float32
BF16 = mybir.dt.bfloat16
I8 = mybir.dt.int8
AF = mybir.ActivationFunctionType
ALU = mybir.AluOpType
AX = mybir.AxisListType
BFNP = ml_dtypes.bfloat16

NCORE = 8
TPC = 1024
NTT = TPC // 128
D = 2048
NEC = D // 128
NEG = 4
INTER = 8192
NIC = INTER // 128
NHEAD = 16
HD = 128
SEQ = 4096
NRANK = 4
INV_SQRT_HD = 1.0 / float(np.sqrt(np.float32(HD)))
L1_DENOM = float(np.float32(np.sqrt(np.float32(D))) / np.float32(SEQ))
LN_EPS = 1e-5



def _tile_b(w):
    K, M = w.shape
    return np.ascontiguousarray(w.reshape(K // 128, 128, M // 128, 128).transpose(2, 0, 1, 3))


def _tile_a(w):
    K, M = w.shape
    return np.ascontiguousarray(w.reshape(K // 128, 128, M // 512, 512).transpose(0, 2, 1, 3))


def _pp(vec):
    n = vec.shape[0] // 128
    return np.ascontiguousarray(vec.reshape(n, 128).T)


def _rope_tables(sl, scale):
    inv = (1.0 / (10000.0 ** (np.arange(0, HD, 2, dtype=np.float32) / np.float32(HD)))).astype(np.float32)
    pos = np.arange(sl * TPC, (sl + 1) * TPC, dtype=np.float32)
    fr = np.outer(pos, inv).astype(np.float32)
    cos = (np.cos(fr) * scale).astype(np.float32)
    sin = (np.sin(fr) * scale).astype(np.float32)
    rep = lambda t: np.ascontiguousarray(np.tile(t, (1, NHEAD)))
    return rep(cos), rep(sin)


def _stage(inputs):
    sh = {}
    sh["ident"] = np.eye(128, dtype=np.float32)
    T = lambda a: np.ascontiguousarray(np.asarray(a).T)
    sh["l0_wq_a"] = _tile_a(T(inputs["L0_attn_Wq"]))
    sh["l0_wk_a"] = _tile_a(T(inputs["L0_attn_Wk"]))
    sh["l0_wv_a"] = _tile_a(T(inputs["L0_attn_Wv"]))
    sh["l0_wo_b"] = _tile_b(T(inputs["L0_attn_Wo"]))
    sh["l0_bq_r"] = np.ascontiguousarray(np.asarray(inputs["L0_attn_bq"]).reshape(1, D))
    sh["l0_bk_r"] = np.ascontiguousarray(np.asarray(inputs["L0_attn_bk"]).reshape(1, D))
    sh["l0_bv_r"] = np.ascontiguousarray(np.asarray(inputs["L0_attn_bv"]).reshape(1, D))
    sh["l0_bo_pp"] = _pp(np.asarray(inputs["L0_attn_bo"]))
    for l in (0, 1):
        sh[f"f{l}_wg"] = np.ascontiguousarray(
            _tile_b(np.asarray(inputs[f"L{l}_ffn_w_gate"])).reshape(NIC // 2, 2 * NEC, 128, 128))
        sh[f"f{l}_wu"] = np.ascontiguousarray(
            _tile_b(np.asarray(inputs[f"L{l}_ffn_w_up"])).reshape(NIC // 2, 2 * NEC, 128, 128))
        sh[f"f{l}_wd"] = _tile_b(np.asarray(inputs[f"L{l}_ffn_w_down"]))
        sh[f"f{l}_sg"] = _pp(np.asarray(inputs[f"L{l}_ffn_s_gate"]))
        sh[f"f{l}_su"] = _pp(np.asarray(inputs[f"L{l}_ffn_s_up"]))
        sh[f"f{l}_sd"] = _pp(np.asarray(inputs[f"L{l}_ffn_s_down"]))
    sh["l1_wq_b"] = _tile_b(T(inputs["L1_attn_Wq"]))
    sh["l1_wk_a"] = _tile_a(T(inputs["L1_attn_Wk"]))
    sh["l1_wv_a"] = _tile_a(T(inputs["L1_attn_Wv"]))
    sh["l1_wo_b"] = _tile_b(T(inputs["L1_attn_Wo"]))

    x = np.asarray(inputs["L0_attn_x"])
    percore = []
    for c in range(NCORE):
        b, sl = divmod(c, NRANK)
        pc = {"x": np.ascontiguousarray(x[b, sl * TPC:(sl + 1) * TPC, :])}
        pc["ropecq"], pc["ropesq"] = _rope_tables(sl, INV_SQRT_HD)
        pc["ropeck"], pc["ropesk"] = _rope_tables(sl, 1.0)
        pc["mask_pp"] = np.ascontiguousarray(
            np.broadcast_to((np.arange(NRANK) < sl).astype(np.float32)[None, :], (128, NRANK)).copy())
        sel = np.array([1.0, 0.0] if b == 0 else [0.0, 1.0], dtype=np.float32)
        pc["sel_pp"] = np.ascontiguousarray(np.broadcast_to(sel[None, :], (128, 2)).copy())
        percore.append(pc)
    return sh, percore



class P:

    def __init__(self):
        self.nc = bass.Bass("TRN2", target_bir_lowering=False, debug=False)
        self.ctx = ExitStack()
        self.h = {}
        self.c = {}
        for n in ("pe", "vc", "sc", "pl", "ld", "lwa", "lwb", "lwc", "lwd", "sd",
                  "cc1", "cc2", "fsa", "fsb", "fsc", "fsd"):
            self.h[n] = self.ctx.enter_context(self.nc.semaphore(n))
            self.c[n] = 0

    _n = 0

    def uname(self, base):
        P._n += 1
        return f"{base}_{P._n}"

    def sb(self, name, shape, dtype):
        return self.ctx.enter_context(self.nc.sbuf_tensor(name, shape, dtype))

    def dram_in(self, name, shape, dtype):
        return self.nc.dram_tensor(name, list(shape), dtype, kind="ExternalInput").ap()

    def dram_out(self, name, shape, dtype):
        return self.nc.dram_tensor(name, list(shape), dtype, kind="ExternalOutput").ap()


def _consts(p):
    k = {
        "ident_b": p.sb("ident_b", [128, 128], BF16),
        "ident_f": p.sb("ident_f", [128, 128], F32),
        "ones_row_b": p.sb("ones_row_b", [1, 128], BF16),
        "ones_row_f": p.sb("ones_row_f", [1, 128], F32),
        "ones_col_b": p.sb("ones_col_b", [128, 1], BF16),
        "eps11": p.sb("eps11", [1, 1], F32),
    }
    return k


def _emit_init(p, k, io):
    ld0 = p.c["ld"]
    with p.nc.Block() as blk:
        @blk.gpsimd
        def _(g):
            g.dma_start(out=k["ident_b"][:, :], in_=io["ident"]).then_inc(p.h["ld"], 16)
            g.dma_start(out=k["ident_f"][:, :], in_=io["ident"]).then_inc(p.h["ld"], 16)
            g.memset(k["ones_row_b"][:, :], 1.0)
            g.memset(k["ones_row_f"][:, :], 1.0)
            g.memset(k["eps11"][:, :], LN_EPS)
            g.memset(k["ones_col_b"][:, :], 1.0).then_inc(p.h["pl"], 1)
            g.wait_ge(p.h["ld"], ld0 + 32)
    p.c["ld"] = ld0 + 32
    p.c["pl"] += 1


def _emit_load_stream(p, k, x_ap, stream):
    nc = p.nc
    with ExitStack() as ctx:
        E = ctx.enter_context
        xr = E(nc.sbuf_tensor(p.uname("ls_xr"), [128, D], BF16))
        trp = [E(nc.psum_tensor(p.uname(f"ls_tr{i}"), [128, 1024], BF16)) for i in range(4)]
        for tc in range(NTT):
            ld0, pe0, vc0 = p.c["ld"], p.c["pe"], p.c["vc"]
            with nc.Block() as blk:
                @blk.gpsimd
                def _(g, tc=tc):
                    g.dma_start(out=xr[:, :], in_=x_ap[tc * 128:(tc + 1) * 128, :]).then_inc(p.h["ld"], 16)

                @blk.tensor
                def _(t, tc=tc, ld0=ld0, vc0=vc0):
                    t.wait_ge(p.h["ld"], ld0 + 16)
                    t.wait_ge(p.h["pl"], 1)
                    for ec in range(NEC):
                        if ec >= 4:
                            t.wait_ge(p.h["vc"], vc0 + ec - 3)
                        t.transpose(trp[ec % 4][:, 0:128], xr[:, ec * 128:(ec + 1) * 128],
                                    k["ident_b"][:, :]).then_inc(p.h["pe"], 1)

                @blk.vector
                def _(v, tc=tc, pe0=pe0):
                    for ec in range(NEC):
                        v.wait_ge(p.h["pe"], pe0 + ec + 1)
                        v.tensor_copy(stream[:, ec * TPC + tc * 128: ec * TPC + (tc + 1) * 128],
                                      trp[ec % 4][:, 0:128]).then_inc(p.h["vc"], 1)
            p.c["ld"] = ld0 + 16
            p.c["pe"] = pe0 + NEC
            p.c["vc"] = vc0 + NEC


def _emit_qkv(p, k, io, stream, outs, wnames, biases, rope, l1=False):
    nc = p.nc
    names = list(outs.keys())
    with ExitStack() as ctx:
        E = ctx.enter_context
        wrhs = [E(nc.sbuf_tensor(p.uname(f"qk_w{i}"), [128, NEC * 512], BF16)) for i in range(2)]
        bias_r = E(nc.sbuf_tensor(p.uname("qk_bias"), [1, D], BF16))
        if rope:
            tab_c = E(nc.sbuf_tensor(p.uname("qk_tc"), [128, NTT * 256], F32))
            tab_s = E(nc.sbuf_tensor(p.uname("qk_ts"), [128, NTT * 256], F32))
            qtmp = [E(nc.sbuf_tensor(p.uname(f"qk_t{i}"), [128, 512], F32)) for i in range(4)]
            m1 = E(nc.sbuf_tensor(p.uname("qk_m1"), [128, 256], F32))
            m2 = E(nc.sbuf_tensor(p.uname("qk_m2"), [128, 256], F32))
            m3 = E(nc.sbuf_tensor(p.uname("qk_m3"), [128, 256], F32))
            m4 = E(nc.sbuf_tensor(p.uname("qk_m4"), [128, 256], F32))
        acc = [E(nc.psum_tensor(p.uname(f"qk_ps{i}"), [128, 512], F32)) for i in range(4)]

        lwsem = ("lwa", "lwb")
        with nc.Block() as blk:
            @blk.gpsimd
            def _(g):
                g.dma_start(out=wrhs[0][:, :].rearrange("p (d e) -> p d e", d=NEC),
                            in_=io[wnames[names[0]]][:, 0, :, :].rearrange("d p e -> p d e")).then_inc(p.h["lwa"], 16)
        p.c["lwa"] += 16

        for ti, tname in enumerate(names):
            do_rope = tname in rope
            has_bias = biases.get(tname) is not None
            dst = outs[tname]
            if do_rope:
                ctab = io["ropecq"] if tname == "q" else io["ropeck"]
                stab = io["ropesq"] if tname == "q" else io["ropesk"]
            for eg in range(NEG):
                wslot = (ti * NEG + eg) % 2
                ld0, pe0, vc0, pl0 = (p.c["ld"], p.c["pe"], p.c["vc"], p.c["pl"])
                wsem = lwsem[wslot]
                lw_need = p.c[wsem]
                n_tab = 2 if do_rope else 0
                has_pref = not (ti == len(names) - 1 and eg == NEG - 1)
                sd0 = p.c["sd"]
                load_bias = has_bias and eg == 0
                with nc.Block() as blk:
                    @blk.gpsimd
                    def _(g, ti=ti, eg=eg, wslot=wslot, has_bias=has_bias,
                          tname=tname, has_pref=has_pref, load_bias=load_bias):
                        if load_bias:
                            g.dma_start(out=bias_r[:, :], in_=io[biases[tname]]).then_inc(p.h["ld"], 16)
                        if has_pref:
                            nt, ne = (ti, eg + 1) if eg + 1 < NEG else (ti + 1, 0)
                            g.dma_start(
                                out=wrhs[1 - wslot][:, :].rearrange("p (d e) -> p d e", d=NEC),
                                in_=io[wnames[names[nt]]][:, ne, :, :].rearrange("d p e -> p d e")).then_inc(p.h[lwsem[1 - wslot]], 16)

                    if do_rope:
                        @blk.sync
                        def _(sy, ti=ti, eg=eg, tname=tname):
                            sy.dma_start(
                                out=tab_c[:, :].rearrange("p (t e) -> p t e", t=NTT),
                                in_=(io["ropecq"] if tname == "q" else io["ropeck"])
                                .rearrange("(t p) e -> p t e", p=128)[:, :, eg * 256:(eg + 1) * 256],
                            ).then_inc(p.h["sd"], 16)
                            sy.dma_start(
                                out=tab_s[:, :].rearrange("p (t e) -> p t e", t=NTT),
                                in_=(io["ropesq"] if tname == "q" else io["ropesk"])
                                .rearrange("(t p) e -> p t e", p=128)[:, :, eg * 256:(eg + 1) * 256],
                            ).then_inc(p.h["sd"], 16)
                    n_dma = 1 if load_bias else 0
                    ld_data = ld0 + 16 * n_dma
                    sd_data = sd0 + (32 if do_rope else 0)

                    v_rope = 3 if do_rope else 0
                    v_per_sweep = 4 + 4 * v_rope if do_rope else 4
                    v_total = 2 * v_per_sweep

                    @blk.tensor
                    def _(t, eg=eg, wslot=wslot, vc0=vc0, ld_data=ld_data, lw_need=lw_need,
                          has_bias=has_bias, wsem=wsem):
                        t.wait_ge(p.h["ld"], ld_data)
                        t.wait_ge(p.h[wsem], lw_need)
                        for sw in range(2):
                            if sw == 1:
                                t.wait_ge(p.h["vc"], vc0 + 4)
                            for tt in range(4):
                                tc = sw * 4 + tt
                                for dc in range(NEC):
                                    mm = t.matmul(acc[tt][:, :],
                                                  stream[:, dc * TPC + tc * 128:dc * TPC + (tc + 1) * 128],
                                                  wrhs[wslot][:, dc * 512:(dc + 1) * 512],
                                                  start=(dc == 0),
                                                  stop=(not has_bias) and dc == NEC - 1)
                                    if (not has_bias) and dc == NEC - 1:
                                        mm.then_inc(p.h["pe"], 1)
                                if has_bias:
                                    t.matmul(acc[tt][:, :], k["ones_row_b"][:, :],
                                             bias_r[:, eg * 512:(eg + 1) * 512],
                                             start=False, stop=True).then_inc(p.h["pe"], 1)

                    if do_rope:
                        @blk.vector
                        def _(v, eg=eg, pe0=pe0, pl0=pl0, vc0=vc0, ld_data=ld_data, dst=dst,
                              sd_data=sd_data):
                            v.wait_ge(p.h["ld"], ld_data)
                            v.wait_ge(p.h["sd"], sd_data)
                            vcc = vc0
                            for sw in range(2):
                                for tt in range(4):
                                    v.wait_ge(p.h["pe"], pe0 + sw * 4 + tt + 1)
                                    if sw == 1:
                                        v.wait_ge(p.h["pl"], pl0 + tt * 3 + 2)
                                        v.wait_ge(p.h["vc"], vcc)
                                    v.tensor_copy(qtmp[tt][:, :], acc[tt][:, :]).then_inc(p.h["vc"], 1)
                                    vcc += 1
                                for tt in range(4):
                                    tc = sw * 4 + tt
                                    ts = qtmp[tt]
                                    x0 = ts[:, 0:512:2]
                                    x1 = ts[:, 1:512:2]
                                    tcs = tab_c[:, tc * 256:(tc + 1) * 256]
                                    plb = pl0 + sw * 12 + tt * 3
                                    dste = dst[:, tc * D + eg * 512 + 0: tc * D + (eg + 1) * 512: 2]
                                    v.wait_ge(p.h["vc"], vcc)
                                    v.tensor_tensor(m1[:, :], x0, tcs, op=ALU.mult).then_inc(p.h["vc"], 1)
                                    vcc += 1
                                    v.wait_ge(p.h["pl"], plb + 1)
                                    v.wait_ge(p.h["vc"], vcc)
                                    v.tensor_tensor(dste, m1[:, :], m2[:, :],
                                                    op=ALU.subtract).then_inc(p.h["vc"], 1)
                                    vcc += 1
                                    v.tensor_tensor(m4[:, :], x1, tcs, op=ALU.mult).then_inc(p.h["vc"], 1)
                                    vcc += 1

                        @blk.gpsimd
                        def _(g, eg=eg, pl0=pl0, vc0=vc0, ld_data=ld_data, dst=dst,
                              v_per_sweep=v_per_sweep, sd_data=sd_data):
                            g.wait_ge(p.h["ld"], ld_data)
                            g.wait_ge(p.h["sd"], sd_data)
                            plc = pl0
                            for sw in range(2):
                                for tt in range(4):
                                    tc = sw * 4 + tt
                                    ts = qtmp[tt]
                                    x0 = ts[:, 0:512:2]
                                    x1 = ts[:, 1:512:2]
                                    tss = tab_s[:, tc * 256:(tc + 1) * 256]
                                    vb = vc0 + sw * v_per_sweep + 4 + tt * 3
                                    evb = vc0 + sw * v_per_sweep + tt + 1
                                    dsto = dst[:, tc * D + eg * 512 + 1: tc * D + (eg + 1) * 512: 2]
                                    g.wait_ge(p.h["vc"], evb)
                                    g.wait_ge(p.h["pl"], plc)
                                    g.tensor_tensor(m2[:, :], x1, tss, op=ALU.mult).then_inc(p.h["pl"], 1)
                                    plc += 1
                                    g.tensor_tensor(m3[:, :], x0, tss, op=ALU.mult).then_inc(p.h["pl"], 1)
                                    plc += 1
                                    g.wait_ge(p.h["vc"], vb + 3)
                                    g.wait_ge(p.h["pl"], plc)
                                    g.tensor_tensor(dsto, m3[:, :], m4[:, :],
                                                    op=ALU.add).then_inc(p.h["pl"], 1)
                                    plc += 1
                    else:
                        @blk.vector
                        def _(v, eg=eg, pe0=pe0, dst=dst):
                            for sw in range(2):
                                for tt in range(4):
                                    tc = sw * 4 + tt
                                    v.wait_ge(p.h["pe"], pe0 + sw * 4 + tt + 1)
                                    v.tensor_copy(dst[:, tc * D + eg * 512:tc * D + (eg + 1) * 512],
                                                  acc[tt][:, :]).then_inc(p.h["vc"], 1)

                p.c["ld"] = ld0 + 16 * n_dma
                p.c["sd"] = sd0 + (32 if do_rope else 0)
                if has_pref:
                    p.c[lwsem[1 - wslot]] += 16
                p.c["pe"] = pe0 + 8
                p.c["vc"] = vc0 + v_total
                p.c["pl"] = pl0 + (24 if do_rope else 0)


def _emit_l0_attn(p, k, io, bufs, cc=None):
    nc = p.nc
    q_tok, k_tok, v_tok, attn_fm = bufs["q_tok"], bufs["k_tok"], bufs["v_tok"], bufs["attn_fm"]
    NPV = 16
    with ExitStack() as ctx:
        E = ctx.enter_context
        prodS = [E(nc.sbuf_tensor(p.uname(f"at_ps{i}"), [128, 2048], BF16)) for i in range(2)]
        NPP = 6 if NPV == NHEAD else 2
        prodP = [E(nc.sbuf_tensor(p.uname(f"at_pp{i}"), [128, 2048], BF16)) for i in range(NPP)]
        prodV = E(nc.sbuf_tensor(p.uname("at_pv"), [128, 4 * 2048], BF16)) if NPV < NHEAD else None
        sc = [E(nc.sbuf_tensor(p.uname(f"at_sc{i}"), [128, 256], F32)) for i in range(2)]
        ex = [E(nc.sbuf_tensor(p.uname(f"at_ex{i}"), [128, 256], F32)) for i in range(2)]
        ssum = [E(nc.sbuf_tensor(p.uname(f"at_ss{i}"), [128, 16], F32)) for i in range(2)]
        rcp = [E(nc.sbuf_tensor(p.uname(f"at_rc{i}"), [128, 16], F32)) for i in range(2)]
        pn = [E(nc.sbuf_tensor(p.uname(f"at_pn{i}"), [128, 256], F32)) for i in range(2)]
        otok = [E(nc.sbuf_tensor(p.uname(f"at_ot{i}"), [128, 2048], BF16)) for i in range(2)]
        trp = [E(nc.psum_tensor(p.uname(f"at_tr{i}"), [128, 1024], BF16)) for i in range(2)]

        pe0, vc0, pl0, sc0, ld0, cc0 = (p.c["pe"], p.c["vc"], p.c["pl"], p.c["sc"],
                                        p.c["ld"], p.c["cc1"])
        vbase = [vc0]
        for tc in range(NTT):
            vbase.append(vbase[-1] + 35 + (16 if tc >= 1 else 0) + (NHEAD - NPV))
        vfinal = vbase[NTT] + 16
        sbase = [sc0]
        for tc in range(NTT):
            sbase.append(sbase[-1] + 1 + (16 if tc >= 1 else 0))
        sfinal = sbase[NTT] + 16

        def pncnt(tc):
            return vbase[tc] + 35

        def redcnt(tcp, h):
            if tcp < NTT - 1:
                return vbase[tcp + 1] + 35 + h + 1
            return vbase[NTT] + h + 1

        def expcnt(tc):
            return sbase[tc] + 1

        def cpycnt(tcp, h):
            if tcp < NTT - 1:
                return sbase[tcp + 1] + 1 + h + 1
            return sbase[NTT] + h + 1

        with nc.Block() as blk:
            @blk.vector
            def _(v):
                vcc = vc0

                def emit_pvreds(tcp):
                    nonlocal vcc
                    ot = otok[tcp % 2]
                    if tcp >= 2:
                        v.wait_ge(p.h["pe"], pe0 + (tcp - 2) * 16 + 16)
                    for h in range(NHEAD):
                        if h < NPV:
                            v.wait_ge(p.h["pl"], pl0 + tcp * NPV + h + 1)
                            pr = prodP[(tcp * NPV + h) % NPP][:, :]
                        else:
                            v.wait_ge(p.h["vc"], vcc)
                            pr = prodV[:, (h - NPV) * 2048:(h - NPV + 1) * 2048]
                        with p.nc.allow_low_precision(reason="DVE reduce accumulates fp32; bf16 store"):
                            v.tensor_reduce(ot[:, h * 128:(h + 1) * 128],
                                            pr.rearrange("p (d h) -> p d h", d=128),
                                            axis=AX.X, op=ALU.add).then_inc(p.h["vc"], 1)
                        vcc += 1

                for tc in range(NTT):
                    s_ = sc[tc % 2]
                    e_ = ex[tc % 2]
                    if tc >= 2:
                        v.wait_ge(p.h["sc"], expcnt(tc - 2))
                    for h in range(NHEAD):
                        qv = bass.AP(q_tok, tc * D + h * 128, [[NTT * D, 128], [0, 16], [1, 128]])
                        kv = bass.AP(k_tok, tc * D, [[NTT * D, 128], [128, 16], [1, 128]])
                        v.wait_ge(p.h["vc"], vcc)
                        v.tensor_tensor(prodS[h % 2][:, :].rearrange("p (h d) -> p h d", h=16),
                                        qv, kv, op=ALU.mult).then_inc(p.h["vc"], 1)
                        vcc += 1
                        v.wait_ge(p.h["vc"], vcc)
                        v.tensor_reduce(s_[:, h * 16:(h + 1) * 16],
                                        prodS[h % 2][:, :].rearrange("p (h d) -> p h d", h=16),
                                        axis=AX.X, op=ALU.add).then_inc(p.h["vc"], 1)
                        vcc += 1
                    v.wait_ge(p.h["sc"], expcnt(tc))
                    v.tensor_reduce(ssum[tc % 2][:, :], e_[:, :].rearrange("p (h j) -> p h j", h=16),
                                    axis=AX.X, op=ALU.add).then_inc(p.h["vc"], 1)
                    vcc += 1
                    v.wait_ge(p.h["vc"], vcc)
                    v.reciprocal(rcp[tc % 2][:, :], ssum[tc % 2][:, :]).then_inc(p.h["vc"], 1)
                    vcc += 1
                    rcv = bass.AP(rcp[tc % 2], 0, [[16, 128], [1, 16], [0, 16]])
                    v.wait_ge(p.h["vc"], vcc)
                    if tc >= 2:
                        v.wait_ge(p.h["pl"], pl0 + (tc - 1) * NPV)
                    v.tensor_tensor(pn[tc % 2][:, :].rearrange("p (h j) -> p h j", h=16),
                                    e_[:, :].rearrange("p (h j) -> p h j", h=16),
                                    rcv, op=ALU.mult).then_inc(p.h["vc"], 1)
                    vcc += 1
                    if tc >= 1:
                        emit_pvreds(tc - 1)
                    for i in range(NHEAD - NPV):
                        h = NPV + i
                        pv = bass.AP(pn[tc % 2], h * 16, [[256, 128], [0, 128], [1, 16]])
                        vv = bass.AP(v_tok, tc * D, [[NTT * D, 128], [1, 128], [128, 16]])
                        v.wait_ge(p.h["vc"], vcc)
                        v.tensor_tensor(prodV[:, i * 2048:(i + 1) * 2048]
                                        .rearrange("p (d h) -> p d h", d=128),
                                        pv, vv, op=ALU.mult).then_inc(p.h["vc"], 1)
                        vcc += 1
                emit_pvreds(NTT - 1)

            @blk.scalar
            def _(s):
                def emit_copies(tcp):
                    for h in range(NHEAD):
                        s.wait_ge(p.h["pe"], pe0 + tcp * 16 + h + 1)
                        s.copy(attn_fm[:, h * TPC + tcp * 128:h * TPC + (tcp + 1) * 128],
                               trp[(tcp * 16 + h) % 2][:, 0:128]).then_inc(p.h["sc"], 1)
                for tc in range(NTT):
                    s.wait_ge(p.h["vc"], vbase[tc] + 32)
                    s.activation(ex[tc % 2][:, :], sc[tc % 2][:, :], AF.Exp).then_inc(p.h["sc"], 1)
                    if tc >= 1:
                        emit_copies(tc - 1)
                emit_copies(NTT - 1)

            @blk.gpsimd
            def _(g):
                plc = pl0
                ldc = ld0

                def emit_cc(tcp):
                    nonlocal ldc
                    cc1i, cc1o, groups = cc
                    g.wait_ge(p.h["sc"], cpycnt(tcp, 15))
                    for j in range(2 * NRANK):
                        g.dma_start(
                            out=cc1i[tcp].ap()[j * 128:(j + 1) * 128, :]
                            .rearrange("p (h s) -> p h s", h=4),
                            in_=bass.AP(attn_fm, 4 * (j % NRANK) * TPC + tcp * 128,
                                        [[NHEAD * TPC, 128], [TPC, 4], [1, 128]]),
                        ).then_inc(p.h["ld"], 16)
                    ldc += 16 * 2 * NRANK
                    g.wait_ge(p.h["ld"], ldc)
                    with allow_ici_all_to_all():
                        g.collective_compute(
                            "AllToAll", ALU.bypass, replica_groups=groups,
                            ins=[cc1i[tcp].ap()], outs=[cc1o[tcp].ap()]).then_inc(p.h["cc1"])

                for tc in range(NTT):
                    if cc is not None and tc >= 1:
                        emit_cc(tc - 1)
                    for h in range(NPV):
                        j = tc * NPV + h
                        w = pncnt(tc)
                        if j >= NPP:
                            tcp, hp = divmod(j - NPP, NPV)
                            w = max(w, redcnt(tcp, hp))
                        g.wait_ge(p.h["vc"], w)
                        g.wait_ge(p.h["pl"], plc)
                        pv = bass.AP(pn[tc % 2], h * 16, [[256, 128], [0, 128], [1, 16]])
                        vv = bass.AP(v_tok, tc * D, [[NTT * D, 128], [1, 128], [128, 16]])
                        g.tensor_tensor(prodP[j % NPP][:, :].rearrange("p (d h) -> p d h", d=128),
                                        pv, vv, op=ALU.mult).then_inc(p.h["pl"], 1)
                        plc += 1
                if cc is not None:
                    emit_cc(NTT - 1)

            @blk.tensor
            def _(t):
                for tc in range(NTT):
                    ot = otok[tc % 2]
                    for h in range(NHEAD):
                        i = tc * 16 + h
                        t.wait_ge(p.h["vc"], redcnt(tc, h))
                        if i >= 2:
                            tcp, hp = divmod(i - 2, 16)
                            t.wait_ge(p.h["sc"], cpycnt(tcp, hp))
                        t.transpose(trp[i % 2][:, 0:128], ot[:, h * 128:(h + 1) * 128],
                                    k["ident_b"][:, :]).then_inc(p.h["pe"], 1)

        p.c["pe"] = pe0 + 16 * NTT
        p.c["vc"] = vfinal
        p.c["pl"] = pl0 + NPV * NTT
        p.c["sc"] = sfinal
        if cc is not None:
            p.c["ld"] = ld0 + 16 * 2 * NRANK * NTT
            p.c["cc1"] = cc0 + NTT


def _emit_store(p, aps):
    sd0 = p.c["sd"]
    n = len(aps)
    with p.nc.Block() as blk:
        @blk.sync
        def _(sy):
            for dram, sb in aps:
                sy.dma_start(out=dram, in_=sb).then_inc(p.h["sd"], 16)
            sy.wait_ge(p.h["sd"], sd0 + 16 * n)
    p.c["sd"] = sd0 + 16 * n



def _emit_scramble8(p, io, a2abuf8, out_scrT, sel_sb, cc1o=None):
    nc = p.nc
    ld0, vc0 = p.c["ld"], p.c["vc"]
    per_tc = cc1o is not None
    with nc.Block() as blk:
        @blk.gpsimd
        def _(g):
            if per_tc:
                for tc in range(NTT):
                    g.wait_ge(p.h["cc1"], tc + 1)
                    for gg in range(2 * NRANK):
                        g.dma_start(
                            out=bass.AP(a2abuf8, gg * 4096 + tc * 128,
                                        [[32 * TPC, 128], [TPC, 4], [1, 128]]),
                            in_=cc1o[tc].ap()[gg * 128:(gg + 1) * 128, :]
                            .rearrange("p (h c) -> p h c", h=4),
                        ).then_inc(p.h["ld"], 16)
            else:
                g.dma_start(
                    out=a2abuf8[:, :].rearrange("p (r f) -> p r f", r=2 * NRANK),
                    in_=io["a2a"].rearrange("(r p) f -> p r f", p=128),
                ).then_inc(p.h["ld"], 16)

        @blk.vector
        def _(v):
            vcc = vc0
            v.wait_ge(p.h["ld"], ld0 + 16 * (NTT * 2 * NRANK if per_tc else 1))
            for r in range(NRANK):
                for hl in range(4):
                    va = bass.AP(a2abuf8, (r * 4 + hl) * TPC,
                                 [[32 * TPC, 128], [1, 16], [16, 64]])
                    vb = bass.AP(a2abuf8, ((r + NRANK) * 4 + hl) * TPC,
                                 [[32 * TPC, 128], [1, 16], [16, 64]])
                    dstv = bass.AP(out_scrT, hl * 256 + r * 64,
                                   [[16 * TPC, 128], [1024, 16], [1, 64]])
                    v.wait_ge(p.h["vc"], vcc)
                    v.tensor_scalar(dstv, va, sel_sb[:, 0:1], None, op0=ALU.mult).then_inc(p.h["vc"], 1)
                    vcc += 1
                    v.wait_ge(p.h["vc"], vcc)
                    v.scalar_tensor_tensor(dstv, vb, sel_sb[:, 1:2], dstv,
                                           op0=ALU.mult, op1=ALU.add).then_inc(p.h["vc"], 1)
                    vcc += 1
    p.c["ld"] = ld0 + 16 * (NTT * 2 * NRANK if per_tc else 1)
    p.c["vc"] = vc0 + 32


def _ln_scratch(p, ctx, mbc=None, rbc=None, pool_split=True):
    E = ctx.enter_context
    nc = p.nc
    return {
        "mean": E(nc.sbuf_tensor(p.uname("ln_mean"), [1, 512], F32)),
        "msq": E(nc.sbuf_tensor(p.uname("ln_msq"), [1, 512], F32)),
        "var": E(nc.sbuf_tensor(p.uname("ln_var"), [1, 512], F32)),
        "sqt": E(nc.sbuf_tensor(p.uname("ln_sqt"), [1, 512], F32)),
        "rstd": E(nc.sbuf_tensor(p.uname("ln_rstd"), [1, 512], F32)),
        "tmp": E(nc.sbuf_tensor(p.uname("ln_tmp"), [128, 512], F32)),
        "tmp2": E(nc.sbuf_tensor(p.uname("ln_tmp2"), [128, 512], F32)) if pool_split else None,
        "mbs": E(nc.sbuf_tensor(p.uname("ln_mbs"), [128, 512], F32)) if pool_split else None,
        "rbs": E(nc.sbuf_tensor(p.uname("ln_rbs"), [128, 512], F32)) if pool_split else None,
        "mbc": mbc if mbc is not None else E(nc.psum_tensor(p.uname("ln_mbc"), [128, 512], F32)),
        "rbc": rbc if rbc is not None else E(nc.psum_tensor(p.uname("ln_rbc"), [128, 512], F32)),
    }


def _ln_finalize(p, k, stream, half, ps_s, ps_sq, sn, out_f32=None):
    pool_split = sn["mbs"] is not None
    """Finalize post-LN for a 512-token half from stats psums [1,512].

    Chunk normalization split between VectorE (ec 0-7, psum operands) and
    GpSimd (ec 8-15, from sbuf copies of the broadcasts).
    """
    nc = p.nc
    pe0, vc0, sc0, pl0 = p.c["pe"], p.c["vc"], p.c["sc"], p.c["pl"]
    with nc.Block() as blk:
        @blk.vector
        def _(v, half=half):
            vcc = vc0
            v.tensor_scalar(sn["mean"][:, :], ps_s[0:1, :], 1.0 / D, None, op0=ALU.mult).then_inc(p.h["vc"], 1)
            vcc += 1
            v.wait_ge(p.h["vc"], vcc)
            v.tensor_tensor(sn["msq"][:, :], sn["mean"][:, :], sn["mean"][:, :], op=ALU.mult).then_inc(p.h["vc"], 1)
            vcc += 1
            v.wait_ge(p.h["vc"], vcc)
            v.scalar_tensor_tensor(sn["var"][:, :], ps_sq[0:1, :], 1.0 / D, sn["msq"][:, :],
                                   op0=ALU.mult, op1=ALU.subtract).then_inc(p.h["vc"], 1)
            vcc += 1
            v.wait_ge(p.h["sc"], sc0 + 1)
            v.reciprocal(sn["rstd"][:, :], sn["sqt"][:, :]).then_inc(p.h["vc"], 1)
            vcc += 1
            v.wait_ge(p.h["pe"], pe0 + 2)
            if pool_split:
                v.wait_ge(p.h["sc"], sc0 + 3)
            for ec in range(8 if pool_split else NEC):
                sl_ = stream[:, ec * TPC + half * 512: ec * TPC + half * 512 + 512]
                v.wait_ge(p.h["vc"], vcc)
                v.tensor_tensor(sn["tmp"][:, :], sl_, sn["mbc"][:, :], op=ALU.subtract).then_inc(p.h["vc"], 1)
                vcc += 1
                v.wait_ge(p.h["vc"], vcc)
                if out_f32 is not None:
                    v.tensor_tensor(out_f32[:, ec * 512:(ec + 1) * 512],
                                    sn["tmp"][:, :], sn["rbc"][:, :], op=ALU.mult).then_inc(p.h["vc"], 1)
                    vcc += 1
                    v.wait_ge(p.h["vc"], vcc)
                v.tensor_tensor(sl_, sn["tmp"][:, :], sn["rbc"][:, :], op=ALU.mult).then_inc(p.h["vc"], 1)
                vcc += 1

        @blk.scalar
        def _(s):
            s.wait_ge(p.h["vc"], vc0 + 3)
            s.activation(sn["sqt"][:, :], sn["var"][:, :], AF.Sqrt, bias=k["eps11"][:, :]).then_inc(p.h["sc"], 1)
            if pool_split:
                s.wait_ge(p.h["pe"], pe0 + 2)
                s.copy(sn["mbs"][:, :], sn["mbc"][:, :]).then_inc(p.h["sc"], 1)
                s.copy(sn["rbs"][:, :], sn["rbc"][:, :]).then_inc(p.h["sc"], 1)

        @blk.gpsimd
        def _(g, half=half):
            if not pool_split:
                return
            plc = pl0
            g.wait_ge(p.h["sc"], sc0 + 3)
            for ec in range(8, NEC):
                sl_ = stream[:, ec * TPC + half * 512: ec * TPC + half * 512 + 512]
                g.wait_ge(p.h["pl"], plc)
                g.tensor_tensor(sn["tmp2"][:, :], sl_, sn["mbs"][:, :], op=ALU.subtract).then_inc(p.h["pl"], 1)
                plc += 1
                g.wait_ge(p.h["pl"], plc)
                if out_f32 is not None:
                    g.tensor_tensor(out_f32[:, ec * 512:(ec + 1) * 512],
                                    sn["tmp2"][:, :], sn["rbs"][:, :], op=ALU.mult).then_inc(p.h["pl"], 1)
                    plc += 1
                    g.wait_ge(p.h["pl"], plc)
                g.tensor_tensor(sl_, sn["tmp2"][:, :], sn["rbs"][:, :], op=ALU.mult).then_inc(p.h["pl"], 1)
                plc += 1

        @blk.tensor
        def _(t):
            t.wait_ge(p.h["vc"], vc0 + 1)
            t.matmul(sn["mbc"][:, :], k["ones_row_f"][:, :], sn["mean"][:, :],
                     start=True, stop=True).then_inc(p.h["pe"], 1)
            t.wait_ge(p.h["vc"], vc0 + 4)
            t.matmul(sn["rbc"][:, :], k["ones_row_f"][:, :], sn["rstd"][:, :],
                     start=True, stop=True).then_inc(p.h["pe"], 1)
    nf = 3 if out_f32 is not None else 2
    p.c["pe"] = pe0 + 2
    if pool_split:
        p.c["vc"] = vc0 + 4 + 8 * nf
        p.c["sc"] = sc0 + 3
        p.c["pl"] = pl0 + 8 * nf
    else:
        p.c["vc"] = vc0 + 4 + NEC * nf
        p.c["sc"] = sc0 + 1


def _emit_proj_b(p, k, io, wkey, rhs_buf, stream, mode, bias_pp=None, dst=None,
                 sn=None, out_f32=None, n_out=NEC, n_k=NEC):
    nc = p.nc
    with ExitStack() as ctx:
        E = ctx.enter_context
        wb = [E(nc.sbuf_tensor(p.uname(f"pb_w{i}"), [128, n_k * 128], BF16)) for i in range(2)]
        acc = [E(nc.psum_tensor(p.uname(f"pb_ps{i}"), [128, 512], F32)) for i in range(2)]
        wsems = ("lwa", "lwb")
        if mode == "resid_ln":
            ps_s = [E(nc.psum_tensor(p.uname(f"pb_s{i}"), [128, 512], F32)) for i in range(2)]
            ps_sq = [E(nc.psum_tensor(p.uname(f"pb_q{i}"), [128, 512], F32)) for i in range(2)]
            ysq = [E(nc.sbuf_tensor(p.uname(f"pb_y{i}"), [128, 512], BF16)) for i in range(2)]
            bias_sb = E(nc.sbuf_tensor(p.uname("pb_bias"), [128, NEC], F32)) if bias_pp else None
            if bias_pp:
                ld0 = p.c["ld"]
                with nc.Block() as blk:
                    @blk.gpsimd
                    def _(g):
                        g.dma_start(out=bias_sb[:, :], in_=io[bias_pp]).then_inc(p.h["ld"], 16)
                        g.wait_ge(p.h["ld"], ld0 + 16)
                p.c["ld"] = ld0 + 16
        with nc.Block() as blk:
            @blk.gpsimd
            def _(g):
                g.dma_start(out=wb[0][:, :].rearrange("p (c m) -> p c m", c=n_k),
                            in_=io[wkey][0].rearrange("c p m -> p c m")).then_inc(p.h["lwa"], 16)
        p.c["lwa"] += 16

        pe0, vc0 = p.c["pe"], p.c["vc"]
        resid = mode == "resid_ln"
        pe_pos = {}
        cnt = 0
        order = []
        for oc in range(n_out):
            order += [("c", oc, 0), ("c", oc, 1)]
            if resid and oc >= 1:
                for half in range(2):
                    order += [("s", oc - 1, half), ("q", oc - 1, half)]
        if resid:
            for half in range(2):
                order += [("s", n_out - 1, half), ("q", n_out - 1, half)]
        for it in order:
            cnt += 1
            pe_pos[it] = cnt
        vper = 4 if resid else 2
        cur = {w: p.c[w] for w in wsems}
        lw_cnt = {0: cur[wsems[0]]}
        for oc in range(1, n_out):
            cur[wsems[oc % 2]] += 16
            lw_cnt[oc] = cur[wsems[oc % 2]]

        with nc.Block() as blk:
            @blk.gpsimd
            def _(g):
                for oc in range(1, n_out):
                    if oc >= 2:
                        g.wait_ge(p.h["pe"], pe0 + pe_pos[("c", oc - 2, 1)])
                    g.dma_start(out=wb[oc % 2][:, :].rearrange("p (c m) -> p c m", c=n_k),
                                in_=io[wkey][oc].rearrange("c p m -> p c m")
                                ).then_inc(p.h[wsems[oc % 2]], 16)

            @blk.tensor
            def _(t):
                def stats(j):
                    for half in range(2):
                        t.wait_ge(p.h["vc"], vc0 + vper * j + 2 * half + 2)
                        t.matmul(ps_s[half][0:1, :], k["ones_col_b"][:, :],
                                 stream[:, j * TPC + half * 512: j * TPC + half * 512 + 512],
                                 start=(j == 0), stop=(j == n_out - 1)).then_inc(p.h["pe"], 1)
                        t.matmul(ps_sq[half][0:1, :], k["ones_col_b"][:, :],
                                 ysq[half][:, :],
                                 start=(j == 0), stop=(j == n_out - 1)).then_inc(p.h["pe"], 1)
                for oc in range(n_out):
                    t.wait_ge(p.h[wsems[oc % 2]], lw_cnt[oc])
                    for half in range(2):
                        if oc >= 1:
                            t.wait_ge(p.h["vc"], vc0 + vper * (oc - 1) + (2 * half if resid else half) + 1)
                        for kc in range(n_k):
                            mm = t.matmul(acc[half][:, :],
                                          wb[oc % 2][:, kc * 128:(kc + 1) * 128],
                                          rhs_buf[:, kc * TPC + half * 512:kc * TPC + half * 512 + 512],
                                          start=(kc == 0), stop=(kc == n_k - 1))
                            if kc == n_k - 1:
                                mm.then_inc(p.h["pe"], 1)
                    if resid and oc >= 1:
                        stats(oc - 1)
                if resid:
                    stats(n_out - 1)

            @blk.vector
            def _(v):
                vcc = vc0
                for oc in range(n_out):
                    for half in range(2):
                        v.wait_ge(p.h["pe"], pe0 + pe_pos[("c", oc, half)])
                        if resid:
                            sl_ = stream[:, oc * TPC + half * 512: oc * TPC + half * 512 + 512]
                            if bias_pp:
                                v.scalar_tensor_tensor(sl_, acc[half][:, :],
                                                       bias_sb[:, oc:oc + 1], sl_,
                                                       op0=ALU.add, op1=ALU.add).then_inc(p.h["vc"], 1)
                            else:
                                v.tensor_tensor(sl_, acc[half][:, :], sl_,
                                                op=ALU.add).then_inc(p.h["vc"], 1)
                            vcc += 1
                            v.wait_ge(p.h["vc"], vcc)
                            if oc >= 1:
                                v.wait_ge(p.h["pe"], pe0 + pe_pos[("q", oc - 1, half)])
                            v.tensor_tensor(ysq[half][:, :], sl_, sl_, op=ALU.mult).then_inc(p.h["vc"], 1)
                            vcc += 1
                        else:
                            v.tensor_copy(dst[:, oc * TPC + half * 512: oc * TPC + half * 512 + 512],
                                          acc[half][:, :]).then_inc(p.h["vc"], 1)
                            vcc += 1
        p.c["pe"] = pe0 + len(order)
        p.c["vc"] = vc0 + vper * n_out
        for w in wsems:
            p.c[w] = cur[w]
        if mode == "resid_ln":
            for half in range(2):
                _ln_finalize(p, k, stream, half, ps_s[half], ps_sq[half], sn,
                             out_f32=out_f32)


def _emit_ffn(p, k, io, stream, wg_key, wu_key, wd_key, sg_key, su_key, sd_key,
              fuseT, out_f32=None, o_y=None):
    nc = p.nc
    with ExitStack() as ctx:
        E = ctx.enter_context
        wgb = [E(nc.sbuf_tensor(p.uname(f"ff_wg{i}"), [128, 2 * NEC * 128], BF16)) for i in range(2)]
        wub = [E(nc.sbuf_tensor(p.uname(f"ff_wu{i}"), [128, 2 * NEC * 128], BF16)) for i in range(2)]
        sg_sb = E(nc.sbuf_tensor(p.uname("ff_sg"), [128, NIC], F32))
        su_sb = E(nc.sbuf_tensor(p.uname("ff_su"), [128, NIC], F32))
        sd_sb = E(nc.sbuf_tensor(p.uname("ff_sd"), [128, NEC], F32))
        silu_sb = [E(nc.sbuf_tensor(p.uname(f"ff_si{i}"), [128, 512], BF16)) for i in range(2)]
        ysq = [E(nc.sbuf_tensor(p.uname(f"ff_ysq{i}"), [128, 512], BF16)) for i in range(2)]
        pg = [E(nc.psum_tensor(p.uname(f"ff_pg{i}"), [128, 512], F32)) for i in range(2)]
        pu = [E(nc.psum_tensor(p.uname(f"ff_pu{i}"), [128, 512], F32)) for i in range(2)]
        py = [E(nc.psum_tensor(p.uname(f"ff_py{i}"), [128, 512], F32)) for i in range(2)]
        ps_s = E(nc.psum_tensor(p.uname("ff_s"), [128, 512], F32))
        ps_sq = E(nc.psum_tensor(p.uname("ff_q"), [128, 512], F32))

        ld0 = p.c["ld"]
        with nc.Block() as blk:
            @blk.gpsimd
            def _(g):
                g.dma_start(out=sg_sb[:, :], in_=io[sg_key]).then_inc(p.h["ld"], 16)
                g.dma_start(out=su_sb[:, :], in_=io[su_key]).then_inc(p.h["ld"], 16)
                g.dma_start(out=sd_sb[:, :], in_=io[sd_key]).then_inc(p.h["ld"], 16)
                g.wait_ge(p.h["ld"], ld0 + 48)
        p.c["ld"] = ld0 + 48

        for half in range(2):
            with nc.Block() as blk:
                @blk.gpsimd
                def _(g):
                    g.dma_start(out=wgb[0][:, :].rearrange("p (c m) -> p c m", c=2 * NEC),
                                in_=io[wg_key][0].rearrange("c p m -> p c m")).then_inc(p.h["lwa"], 16)
                    g.dma_start(out=wub[0][:, :].rearrange("p (c m) -> p c m", c=2 * NEC),
                                in_=io[wu_key][0].rearrange("c p m -> p c m")).then_inc(p.h["lwc"], 16)
            p.c["lwa"] += 16
            p.c["lwc"] += 16
            pe0, vc0, sc0 = p.c["pe"], p.c["vc"], p.c["sc"]
            lwg = [p.c["lwa"], p.c["lwb"]]
            lwu = [p.c["lwc"], p.c["lwd"]]
            need_g = []
            need_u = []
            for q in range(NIC // 2):
                need_g.append(lwg[q % 2])
                need_u.append(lwu[q % 2])
                if q + 1 < NIC // 2:
                    lwg[(q + 1) % 2] += 16
                    lwu[(q + 1) % 2] += 16
            gsem = ("lwa", "lwb")
            usem = ("lwc", "lwd")
            with nc.Block() as blk:
                @blk.gpsimd
                def _(g, half=half):
                    for q in range(NIC // 2):
                        if q + 1 < NIC // 2:
                            if q >= 1:
                                g.wait_ge(p.h["pe"], pe0 + 2 * (2 * q - 1) + 2)
                            g.dma_start(out=wgb[(q + 1) % 2][:, :].rearrange("p (c m) -> p c m", c=2 * NEC),
                                        in_=io[wg_key][q + 1].rearrange("c p m -> p c m")
                                        ).then_inc(p.h[gsem[(q + 1) % 2]], 16)
                            g.dma_start(out=wub[(q + 1) % 2][:, :].rearrange("p (c m) -> p c m", c=2 * NEC),
                                        in_=io[wu_key][q + 1].rearrange("c p m -> p c m")
                                        ).then_inc(p.h[usem[(q + 1) % 2]], 16)

                @blk.tensor
                def _(t, half=half):
                    for ic in range(NIC):
                        q, qi = divmod(ic, 2)
                        if qi == 0:
                            t.wait_ge(p.h[gsem[q % 2]], need_g[q])
                            t.wait_ge(p.h[usem[q % 2]], need_u[q])
                        if ic >= 2:
                            t.wait_ge(p.h["sc"], sc0 + (ic - 2) + 1)
                            t.wait_ge(p.h["vc"], vc0 + (ic - 2) + 1)
                        for dc in range(NEC):
                            t.matmul(pg[ic % 2][:, :], wgb[q % 2][:, (qi * NEC + dc) * 128:(qi * NEC + dc + 1) * 128],
                                     stream[:, dc * TPC + half * 512:dc * TPC + half * 512 + 512],
                                     start=(dc == 0), stop=(dc == NEC - 1)
                                     ).then_inc(p.h["pe"], 1) if dc == NEC - 1 else t.matmul(
                                pg[ic % 2][:, :], wgb[q % 2][:, (qi * NEC + dc) * 128:(qi * NEC + dc + 1) * 128],
                                stream[:, dc * TPC + half * 512:dc * TPC + half * 512 + 512],
                                start=(dc == 0), stop=False)
                        for dc in range(NEC):
                            t.matmul(pu[ic % 2][:, :], wub[q % 2][:, (qi * NEC + dc) * 128:(qi * NEC + dc + 1) * 128],
                                     stream[:, dc * TPC + half * 512:dc * TPC + half * 512 + 512],
                                     start=(dc == 0), stop=(dc == NEC - 1)
                                     ).then_inc(p.h["pe"], 1) if dc == NEC - 1 else t.matmul(
                                pu[ic % 2][:, :], wub[q % 2][:, (qi * NEC + dc) * 128:(qi * NEC + dc + 1) * 128],
                                stream[:, dc * TPC + half * 512:dc * TPC + half * 512 + 512],
                                start=(dc == 0), stop=False)

                @blk.scalar
                def _(s, half=half):
                    s.wait_ge(p.h["ld"], ld0 + 48)
                    for ic in range(NIC):
                        s.wait_ge(p.h["pe"], pe0 + 2 * ic + 1)
                        if ic >= 2:
                            s.wait_ge(p.h["vc"], vc0 + (ic - 2) + 1)
                        s.activation(silu_sb[ic % 2][:, :], pg[ic % 2][:, :], AF.Silu,
                                     scale=sg_sb[:, ic:ic + 1]).then_inc(p.h["sc"], 1)

                @blk.vector
                def _(v, half=half):
                    for ic in range(NIC):
                        v.wait_ge(p.h["pe"], pe0 + 2 * ic + 2)
                        v.wait_ge(p.h["sc"], sc0 + ic + 1)
                        v.scalar_tensor_tensor(fuseT[:, ic * 512:(ic + 1) * 512],
                                               pu[ic % 2][:, :], su_sb[:, ic:ic + 1],
                                               silu_sb[ic % 2][:, :],
                                               op0=ALU.mult, op1=ALU.mult).then_inc(p.h["vc"], 1)
            p.c["pe"] = pe0 + 2 * NIC
            p.c["sc"] = sc0 + NIC
            p.c["vc"] = vc0 + NIC
            p.c["lwa"], p.c["lwb"] = lwg[0], lwg[1]
            p.c["lwc"], p.c["lwd"] = lwu[0], lwu[1]

            ctxd = ExitStack()
            wdb = [ctxd.enter_context(nc.sbuf_tensor(p.uname(f"ff_wd{i}"), [128, 64 * 128], BF16))
                   for i in range(2)]
            with nc.Block() as blk:
                @blk.gpsimd
                def _(g):
                    g.dma_start(out=wdb[0][:, :].rearrange("p (c m) -> p c m", c=64),
                                in_=io[wd_key][0].rearrange("c p m -> p c m")).then_inc(p.h["lwa"], 16)
            p.c["lwa"] += 16
            pe0, vc0 = p.c["pe"], p.c["vc"]
            lwd_ = [p.c["lwa"], p.c["lwb"]]
            need_d = []
            for dc in range(NEC):
                need_d.append(lwd_[dc % 2])
                if dc + 1 < NEC:
                    lwd_[(dc + 1) % 2] += 16
            dsem = ("lwa", "lwb")
            with nc.Block() as blk:
                @blk.gpsimd
                def _(g):
                    for dc in range(1, NEC):
                        if dc >= 2:
                            dcp = dc - 2
                            g.wait_ge(p.h["pe"], pe0 + (1 if dcp == 0 else 3 * dcp - 1))
                        g.dma_start(out=wdb[dc % 2][:, :].rearrange("p (c m) -> p c m", c=64),
                                    in_=io[wd_key][dc].rearrange("c p m -> p c m")
                                    ).then_inc(p.h[dsem[dc % 2]], 16)

                @blk.tensor
                def _(t, half=half):
                    def chain_inc(dc):
                        return pe0 + (1 if dc == 0 else 3 * dc - 1)
                    def emit_stats(j):
                        t.wait_ge(p.h["vc"], vc0 + 2 * j + 2)
                        t.matmul(ps_s[0:1, :], k["ones_col_b"][:, :],
                                 stream[:, j * TPC + half * 512:j * TPC + half * 512 + 512],
                                 start=(j == 0), stop=(j == NEC - 1)).then_inc(p.h["pe"], 1)
                        t.matmul(ps_sq[0:1, :], k["ones_col_b"][:, :], ysq[j % 2][:, :],
                                 start=(j == 0), stop=(j == NEC - 1)).then_inc(p.h["pe"], 1)
                    for dc in range(NEC):
                        if dc >= 2:
                            t.wait_ge(p.h["vc"], vc0 + 2 * (dc - 2) + 2)
                        t.wait_ge(p.h[dsem[dc % 2]], need_d[dc])
                        for ic in range(NIC):
                            mm = t.matmul(py[dc % 2][:, :],
                                          wdb[dc % 2][:, ic * 128:(ic + 1) * 128],
                                          fuseT[:, ic * 512:(ic + 1) * 512],
                                          start=(ic == 0), stop=(ic == NIC - 1))
                            if ic == NIC - 1:
                                mm.then_inc(p.h["pe"], 1)
                        if dc >= 1:
                            emit_stats(dc - 1)
                    emit_stats(NEC - 1)

                @blk.vector
                def _(v, half=half):
                    vcc = vc0
                    for dc in range(NEC):
                        v.wait_ge(p.h["pe"], pe0 + (1 if dc == 0 else 3 * dc - 1))
                        if dc >= 2:
                            v.wait_ge(p.h["pe"], pe0 + 3 * (dc - 2) + 4)
                        sl_ = stream[:, dc * TPC + half * 512:dc * TPC + half * 512 + 512]
                        v.scalar_tensor_tensor(sl_, py[dc % 2][:, :], sd_sb[:, dc:dc + 1], sl_,
                                               op0=ALU.mult, op1=ALU.add).then_inc(p.h["vc"], 1)
                        vcc += 1
                        v.wait_ge(p.h["vc"], vcc)
                        v.tensor_tensor(ysq[dc % 2][:, :], sl_, sl_, op=ALU.mult).then_inc(p.h["vc"], 1)
                        vcc += 1
            p.c["pe"] = pe0 + 3 * NEC
            p.c["vc"] = vc0 + 2 * NEC
            p.c["lwa"], p.c["lwb"] = lwd_[0], lwd_[1]
            ctxd.close()

            with ExitStack() as ctx2:
                sn = _ln_scratch(p, ctx2, mbc=pg[0], rbc=pg[1],
                                 pool_split=(out_f32 is None))
                _ln_finalize(p, k, stream, half, ps_s, ps_sq, sn, out_f32=out_f32)
            if out_f32 is not None and o_y is not None:
                _emit_final_store_half(p, k, out_f32, o_y, half, pg)



def _emit_kv_transpose(p, k, k_tok, v_tok, kvT):
    nc = p.nc
    with ExitStack() as ctx:
        E = ctx.enter_context
        kvpc = [E(nc.sbuf_tensor(p.uname(f"kv_pc{i}"), [128, D], BF16)) for i in range(1)]
        trp = [E(nc.psum_tensor(p.uname(f"kv_tr{i}"), [128, 1024], BF16)) for i in range(2)]
        for tc in range(NTT):
            pe0, vc0 = p.c["pe"], p.c["vc"]
            with nc.Block() as blk:
                @blk.vector
                def _(v, tc=tc, pe0=pe0, vc0=vc0):
                    v.tensor_tensor(kvpc[0][:, :], k_tok[:, tc * D:(tc + 1) * D],
                                    v_tok[:, tc * D:(tc + 1) * D], op=ALU.mult).then_inc(p.h["vc"], 1)
                    for ec in range(NEC):
                        v.wait_ge(p.h["pe"], pe0 + ec + 1)
                        v.tensor_copy(kvT[:, ec * TPC + tc * 128:ec * TPC + (tc + 1) * 128],
                                      trp[ec % 2][:, 0:128]).then_inc(p.h["vc"], 1)

                @blk.tensor
                def _(t, tc=tc, pe0=pe0, vc0=vc0):
                    t.wait_ge(p.h["vc"], vc0 + 1)
                    for ec in range(NEC):
                        if ec >= 2:
                            t.wait_ge(p.h["vc"], vc0 + 1 + ec - 1)
                        t.transpose(trp[ec % 2][:, 0:128], kvpc[0][:, ec * 128:(ec + 1) * 128],
                                    k["ident_b"][:, :]).then_inc(p.h["pe"], 1)
            p.c["pe"] = pe0 + NEC
            p.c["vc"] = vc0 + 1 + NEC


def _emit_scan(p, k, kvT, kvt_loc, blocksums):
    nc = p.nc
    with ExitStack() as ctx:
        E = ctx.enter_context
        zeros = E(nc.sbuf_tensor(p.uname("sc_zero"), [128, TPC], F32))
        pl0 = p.c["pl"]
        with nc.Block() as blk:
            @blk.gpsimd
            def _(g):
                g.memset(zeros[:, :], 0.0).then_inc(p.h["pl"], 1)
        p.c["pl"] = pl0 + 1
        vc0 = p.c["vc"]
        with nc.Block() as blk:
            @blk.vector
            def _(v):
                vcc = vc0
                v.wait_ge(p.h["pl"], pl0 + 1)
                for ec in range(NEC):
                    with p.nc.allow_low_precision(reason="scan state fp32; bf16 store"):
                        v.tensor_tensor_scan(kvt_loc[:, ec * TPC:(ec + 1) * TPC],
                                             kvT[:, ec * TPC:(ec + 1) * TPC], zeros[:, :],
                                             0.0, op0=ALU.add, op1=ALU.add).then_inc(p.h["vc"], 1)
                    vcc += 1
                    v.wait_ge(p.h["vc"], vcc)
                    v.tensor_copy(blocksums[:, ec:ec + 1],
                                  kvt_loc[:, (ec + 1) * TPC - 1:(ec + 1) * TPC]).then_inc(p.h["vc"], 1)
                    vcc += 1
        p.c["vc"] = vc0 + 2 * NEC


def _emit_l1b_vhat(p, k, io, qT1, kvt, vhat):
    nc = p.nc
    with ExitStack() as ctx:
        E = ctx.enter_context
        bs4 = E(nc.sbuf_tensor(p.uname("vb_bs4"), [128, 4 * NEC], F32))
        mask = E(nc.sbuf_tensor(p.uname("vb_mask"), [128, NRANK], F32))
        offs = E(nc.sbuf_tensor(p.uname("vb_offs"), [128, NEC], F32))
        kvf = E(nc.sbuf_tensor(p.uname("vb_kvf"), [128, TPC], F32))
        ld0, vc0, pl0 = p.c["ld"], p.c["vc"], p.c["pl"]
        with nc.Block() as blk:
            @blk.gpsimd
            def _(g):
                g.dma_start(out=bs4[:, :].rearrange("p (r c) -> p r c", r=4),
                            in_=io["bs_all"].rearrange("r p c -> p r c")).then_inc(p.h["ld"], 16)
                g.dma_start(out=mask[:, :], in_=io["mask_pp"]).then_inc(p.h["ld"], 16)
                g.memset(offs[:, :], 0.0).then_inc(p.h["pl"], 1)

            @blk.vector
            def _(v):
                vcc = vc0
                v.wait_ge(p.h["ld"], ld0 + 32)
                v.wait_ge(p.h["pl"], pl0 + 1)
                for r in range(NRANK):
                    v.wait_ge(p.h["vc"], vcc)
                    v.scalar_tensor_tensor(offs[:, :], bs4[:, r * NEC:(r + 1) * NEC],
                                           mask[:, r:r + 1], offs[:, :],
                                           op0=ALU.mult, op1=ALU.add).then_inc(p.h["vc"], 1)
                    vcc += 1
        p.c["ld"] = ld0 + 32
        p.c["pl"] = pl0 + 1
        p.c["vc"] = vc0 + NRANK
        vc0 = p.c["vc"]
        with nc.Block() as blk:
            @blk.vector
            def _(v):
                vcc = vc0
                for ec in range(NEC):
                    v.wait_ge(p.h["vc"], vcc)
                    v.tensor_scalar(kvf[:, :], kvt[:, ec * TPC:(ec + 1) * TPC],
                                    offs[:, ec:ec + 1], None, op0=ALU.add).then_inc(p.h["vc"], 1)
                    vcc += 1
                    v.wait_ge(p.h["vc"], vcc)
                    v.scalar_tensor_tensor(vhat[:, ec * TPC:(ec + 1) * TPC],
                                           qT1[:, ec * TPC:(ec + 1) * TPC], L1_DENOM, kvf[:, :],
                                           op0=ALU.mult, op1=ALU.mult).then_inc(p.h["vc"], 1)
                    vcc += 1
        p.c["vc"] = vc0 + 2 * NEC


def build_l1a():
    p = P()
    io = {
        "ident": p.dram_in("ident", [128, 128], F32),
        "stream_in": p.dram_in("stream_in", [128, NEC * TPC], BF16),
        "l1_wq_b": p.dram_in("l1_wq_b", [NEC, NEC, 128, 128], F32),
        "l1_wk_a": p.dram_in("l1_wk_a", [NEC, NEG, 128, 512], F32),
        "l1_wv_a": p.dram_in("l1_wv_a", [NEC, NEG, 128, 512], F32),
    }
    o_q = p.dram_out("o_q", [128, NEC * TPC], BF16)
    o_kvt = p.dram_out("o_kvt", [128, NEC * TPC], BF16)
    o_bs = p.dram_out("o_bs", [128, NEC], F32)
    with p.ctx:
        k = _consts(p)
        stream = p.sb("stream", [128, NEC * TPC], BF16)
        qT1 = p.sb("qT1", [128, NEC * TPC], BF16)
        kvT = p.sb("kvT", [128, NEC * TPC], BF16)
        blocksums = p.sb("blocksums", [128, NEC], F32)
        _emit_init(p, k, io)
        _emit_load_sb(p, [(stream[:, :], io["stream_in"])])
        _emit_proj_b(p, k, io, "l1_wq_b", stream, None, "copy", dst=qT1)
        with ExitStack() as ctx2:
            k_tok = ctx2.enter_context(p.nc.sbuf_tensor(p.uname("k_tok"), [128, NTT * D], BF16))
            v_tok = ctx2.enter_context(p.nc.sbuf_tensor(p.uname("v_tok"), [128, NTT * D], BF16))
            _emit_qkv(p, k, io, stream,
                      outs={"k": k_tok, "v": v_tok},
                      wnames={"k": "l1_wk_a", "v": "l1_wv_a"},
                      biases={}, rope=set())
            _emit_kv_transpose(p, k, k_tok, v_tok, kvT)
        kvt_loc = p.sb("kvt_loc", [128, NEC * TPC], BF16)
        _emit_scan(p, k, kvT, kvt_loc, blocksums)
        _emit_store(p, [(o_q, qT1[:, :]), (o_kvt, kvt_loc[:, :]), (o_bs, blocksums[:, :])])
    return p.nc


def build_l1b():
    p = P()
    io = {
        "ident": p.dram_in("ident", [128, 128], F32),
        "stream_in": p.dram_in("stream_in", [128, NEC * TPC], BF16),
        "q_in": p.dram_in("q_in", [128, NEC * TPC], BF16),
        "kvt_in": p.dram_in("kvt_in", [128, NEC * TPC], BF16),
        "bs_all": p.dram_in("bs_all", [NRANK, 128, NEC], F32),
        "mask_pp": p.dram_in("mask_pp", [128, NRANK], F32),
        "l1_wo_b": p.dram_in("l1_wo_b", [NEC, NEC, 128, 128], F32),
    }
    o_stream = p.dram_out("o_stream", [128, NEC * TPC], BF16)
    with p.ctx:
        k = _consts(p)
        stream = p.sb("stream", [128, NEC * TPC], BF16)
        qT1 = p.sb("qT1", [128, NEC * TPC], BF16)
        kvt = p.sb("kvt", [128, NEC * TPC], BF16)
        vhat = p.sb("vhat", [128, NEC * TPC], BF16)
        _emit_init(p, k, io)
        _emit_load_sb(p, [(stream[:, :], io["stream_in"]),
                          (qT1[:, :], io["q_in"]), (kvt[:, :], io["kvt_in"])])
        _emit_l1b_vhat(p, k, io, qT1, kvt, vhat)
        with ExitStack() as ctx2:
            sn = _ln_scratch(p, ctx2)
            _emit_proj_b(p, k, io, "l1_wo_b", vhat, stream, "resid_ln", sn=sn)
        _emit_store(p, [(o_stream, stream[:, :])])
    return p.nc


def build_l0a(debug=False):
    p = P()
    io = {
        "ident": p.dram_in("ident", [128, 128], F32),
        "x": p.dram_in("x", [TPC, D], F32),
        "l0_wk_a": p.dram_in("l0_wk_a", [NEC, NEG, 128, 512], F32),
        "l0_wv_a": p.dram_in("l0_wv_a", [NEC, NEG, 128, 512], F32),
        "l0_wq_a": p.dram_in("l0_wq_a", [NEC, NEG, 128, 512], F32),
        "l0_bq_r": p.dram_in("l0_bq_r", [1, D], F32),
        "l0_bk_r": p.dram_in("l0_bk_r", [1, D], F32),
        "l0_bv_r": p.dram_in("l0_bv_r", [1, D], F32),
        "ropecq": p.dram_in("ropecq", [TPC, TPC], F32),
        "ropesq": p.dram_in("ropesq", [TPC, TPC], F32),
        "ropeck": p.dram_in("ropeck", [TPC, TPC], F32),
        "ropesk": p.dram_in("ropesk", [TPC, TPC], F32),
    }
    o_stream = p.dram_out("o_stream", [128, NEC * TPC], BF16)
    o_attn = p.dram_out("o_attn", [128, NHEAD * TPC], BF16)
    if debug:
        o_q = p.dram_out("o_q", [128, NTT * D], BF16)
        o_k = p.dram_out("o_k", [128, NTT * D], BF16)
        o_v = p.dram_out("o_v", [128, NTT * D], BF16)

    with p.ctx:
        k = _consts(p)
        stream = p.sb("stream", [128, NEC * TPC], BF16)
        q_tok = p.sb("q_tok", [128, NTT * D], BF16)
        k_tok = p.sb("k_tok", [128, NTT * D], BF16)
        v_tok = p.sb("v_tok", [128, NTT * D], BF16)
        _emit_init(p, k, io)
        _emit_load_stream(p, k, io["x"], stream)
        _emit_qkv(p, k, io, stream,
                  outs={"k": k_tok, "v": v_tok, "q": q_tok},
                  wnames={"k": "l0_wk_a", "v": "l0_wv_a", "q": "l0_wq_a"},
                  biases={"k": "l0_bk_r", "v": "l0_bv_r", "q": "l0_bq_r"},
                  rope={"q", "k"})
        attn_fm = p.sb("attn_fm", [128, NHEAD * TPC], BF16)
        bufs = {"q_tok": q_tok, "k_tok": k_tok, "v_tok": v_tok, "attn_fm": attn_fm}
        _emit_l0_attn(p, k, io, bufs)
        stores = [
            (o_stream, stream[:, :]),
            (o_attn, attn_fm[:, :]),
        ]
        if debug:
            stores += [(o_q, q_tok[:, :]), (o_k, k_tok[:, :]), (o_v, v_tok[:, :])]
        _emit_store(p, stores)
    return p.nc


def _l0a_inmap(sh, pc):
    return {"ident": sh["ident"],
            "l0_wk_a": sh["l0_wk_a"], "l0_wv_a": sh["l0_wv_a"], "l0_wq_a": sh["l0_wq_a"],
            "l0_bq_r": sh["l0_bq_r"], "l0_bk_r": sh["l0_bk_r"], "l0_bv_r": sh["l0_bv_r"],
            "x": pc["x"],
            "ropecq": pc["ropecq"], "ropesq": pc["ropesq"],
            "ropeck": pc["ropeck"], "ropesk": pc["ropesk"]}


def _emit_load_sb(p, pairs):
    sd0 = p.c["sd"]
    n = len(pairs)
    with p.nc.Block() as blk:
        @blk.sync
        def _(sy):
            for sb, dram in pairs:
                sy.dma_start(out=sb, in_=dram).then_inc(p.h["sd"], 16)
            sy.wait_ge(p.h["sd"], sd0 + 16 * n)
    p.c["sd"] = sd0 + 16 * n


def build_l0b():
    p = P()
    io = {
        "ident": p.dram_in("ident", [128, 128], F32),
        "a2a": p.dram_in("a2a", [2 * NRANK * 128, 4 * TPC], BF16),
        "sel_pp": p.dram_in("sel_pp", [128, 2], F32),
        "stream_in": p.dram_in("stream_in", [128, NEC * TPC], BF16),
        "l0_wo_b": p.dram_in("l0_wo_b", [NEC, NEC, 128, 128], F32),
        "l0_bo_pp": p.dram_in("l0_bo_pp", [128, NEC], F32),
    }
    o_stream = p.dram_out("o_stream", [128, NEC * TPC], BF16)
    with p.ctx:
        k = _consts(p)
        stream = p.sb("stream", [128, NEC * TPC], BF16)
        a2abuf = p.sb("a2abuf", [128, 2 * NEC * TPC], BF16)
        out_scrT = p.sb("out_scrT", [128, NEC * TPC], BF16)
        sel_sb = p.sb("sel_sb", [128, 2], F32)
        _emit_init(p, k, io)
        _emit_load_sb(p, [(stream[:, :], io["stream_in"]), (sel_sb[:, :], io["sel_pp"])])
        _emit_scramble8(p, io, a2abuf, out_scrT, sel_sb)
        with ExitStack() as ctx2:
            sn = _ln_scratch(p, ctx2)
            _emit_proj_b(p, k, io, "l0_wo_b", out_scrT, stream, "resid_ln",
                         bias_pp="l0_bo_pp", sn=sn)
        _emit_store(p, [(o_stream, stream[:, :])])
    return p.nc


def _host_a2a(attn_fms, b, sl):
    blocks = [np.asarray(attn_fms[g])[:, 4 * sl * TPC:(4 * sl + 4) * TPC]
              for g in range(2 * NRANK)]
    return np.ascontiguousarray(np.concatenate(blocks, axis=0))


def build_ffn(l, final=False):
    p = P()
    io = {
        "ident": p.dram_in("ident", [128, 128], F32),
        "stream_in": p.dram_in("stream_in", [128, NEC * TPC], BF16),
        "wg": p.dram_in("wg", [NIC // 2, 2 * NEC, 128, 128], I8),
        "wu": p.dram_in("wu", [NIC // 2, 2 * NEC, 128, 128], I8),
        "wd": p.dram_in("wd", [NEC, NIC, 128, 128], I8),
        "sg": p.dram_in("sg", [128, NIC], F32),
        "su": p.dram_in("su", [128, NIC], F32),
        "sd": p.dram_in("sd", [128, NEC], F32),
    }
    o_stream = p.dram_out("o_stream", [128, NEC * TPC], BF16)
    o_y = p.dram_out("o_y", [TPC, D], F32) if final else None
    with p.ctx:
        k = _consts(p)
        stream = p.sb("stream", [128, NEC * TPC], BF16)
        _emit_init(p, k, io)
        _emit_load_sb(p, [(stream[:, :], io["stream_in"])])
        with ExitStack() as ctx2:
            fuseT = ctx2.enter_context(p.nc.sbuf_tensor(p.uname("fuseT"), [128, NIC * 512], BF16))
            xTf32 = (ctx2.enter_context(p.nc.sbuf_tensor(p.uname("xTf32"), [128, NEC * 512], F32))
                     if final else None)
            _emit_ffn(p, k, io, stream, "wg", "wu", "wd", "sg", "su", "sd",
                      fuseT, out_f32=xTf32, o_y=o_y)
        if not final:
            _emit_store(p, [(o_stream, stream[:, :])])
        else:
            _emit_store(p, [(o_stream, stream[:, :])])
    return p.nc


def _emit_final_store_half(p, k, xTf32, o_y, half, trp):
    nc = p.nc
    with ExitStack() as ctx:
        E = ctx.enter_context
        stg = [E(nc.sbuf_tensor(p.uname(f"fs_st{i}"), [128, 128], F32)) for i in range(4)]
        pe0, sc0 = p.c["pe"], p.c["sc"]
        fs = ("fsa", "fsb", "fsc", "fsd")
        fs0 = [p.c[n] for n in fs]
        NB = NEC * 4
        with nc.Block() as blk:
            @blk.tensor
            def _(t):
                for i in range(NB):
                    ec, tc = divmod(i, 4)
                    if i >= 2:
                        t.wait_ge(p.h["sc"], sc0 + i - 1)
                    t.transpose(trp[i % 2][:, 0:128],
                                xTf32[:, ec * 512 + tc * 128:ec * 512 + (tc + 1) * 128],
                                k["ident_f"][:, :]).then_inc(p.h["pe"], 1)

            @blk.scalar
            def _(s):
                for i in range(NB):
                    s.wait_ge(p.h["pe"], pe0 + i + 1)
                    if i >= 4:
                        s.wait_ge(p.h[fs[i % 4]], fs0[i % 4] + 16 * (i // 4))
                    s.copy(stg[i % 4][:, :], trp[i % 2][:, 0:128]).then_inc(p.h["sc"], 1)

            @blk.sync
            def _(sy, half=half):
                for i in range(NB):
                    ec, tc = divmod(i, 4)
                    sy.wait_ge(p.h["sc"], sc0 + i + 1)
                    sy.dma_start(
                        out=o_y[half * 512 + tc * 128:half * 512 + (tc + 1) * 128,
                                ec * 128:(ec + 1) * 128],
                        in_=stg[i % 4][:, :]).then_inc(p.h[fs[i % 4]], 16)
                for j in range(4):
                    sy.wait_ge(p.h[fs[j]], fs0[j] + 16 * (NB // 4))
        p.c["pe"] = pe0 + NB
        p.c["sc"] = sc0 + NB
        for j in range(4):
            p.c[fs[j]] += 16 * (NB // 4)


def _emit_cc(p, kind, groups, in_ap, out_ap, src_pairs, ccname, ici=False):
    ld0 = p.c["ld"]
    n = len(src_pairs)
    with p.nc.Block() as blk:
        @blk.gpsimd
        def _(g):
            for dst, sb in src_pairs:
                g.dma_start(out=dst, in_=sb).then_inc(p.h["ld"], 16)
            g.wait_ge(p.h["ld"], ld0 + 16 * n)
            if ici:
                with allow_ici_all_to_all():
                    g.collective_compute(kind, ALU.bypass, replica_groups=groups,
                                         ins=[in_ap], outs=[out_ap]).then_inc(p.h[ccname])
            else:
                g.collective_compute(kind, ALU.bypass, replica_groups=groups,
                                     ins=[in_ap], outs=[out_ap]).then_inc(p.h[ccname])
            g.wait_ge(p.h[ccname], 1)
    p.c["ld"] = ld0 + 16 * n
    p.c[ccname] = 1


def build_fused():
    p = P()
    nc = p.nc
    io = {
        "ident": p.dram_in("ident", [128, 128], F32),
        "x": p.dram_in("x", [TPC, D], F32),
        "l0_wk_a": p.dram_in("l0_wk_a", [NEC, NEG, 128, 512], F32),
        "l0_wv_a": p.dram_in("l0_wv_a", [NEC, NEG, 128, 512], F32),
        "l0_wq_a": p.dram_in("l0_wq_a", [NEC, NEG, 128, 512], F32),
        "l0_bq_r": p.dram_in("l0_bq_r", [1, D], F32),
        "l0_bk_r": p.dram_in("l0_bk_r", [1, D], F32),
        "l0_bv_r": p.dram_in("l0_bv_r", [1, D], F32),
        "ropecq": p.dram_in("ropecq", [TPC, TPC], F32),
        "ropesq": p.dram_in("ropesq", [TPC, TPC], F32),
        "ropeck": p.dram_in("ropeck", [TPC, TPC], F32),
        "ropesk": p.dram_in("ropesk", [TPC, TPC], F32),
        "l0_wo_b": p.dram_in("l0_wo_b", [NEC, NEC, 128, 128], F32),
        "l0_bo_pp": p.dram_in("l0_bo_pp", [128, NEC], F32),
        "f0_wg": p.dram_in("f0_wg", [NIC // 2, 2 * NEC, 128, 128], I8),
        "f0_wu": p.dram_in("f0_wu", [NIC // 2, 2 * NEC, 128, 128], I8),
        "f0_wd": p.dram_in("f0_wd", [NEC, NIC, 128, 128], I8),
        "f0_sg": p.dram_in("f0_sg", [128, NIC], F32),
        "f0_su": p.dram_in("f0_su", [128, NIC], F32),
        "f0_sd": p.dram_in("f0_sd", [128, NEC], F32),
        "l1_wq_b": p.dram_in("l1_wq_b", [NEC, NEC, 128, 128], F32),
        "l1_wk_a": p.dram_in("l1_wk_a", [NEC, NEG, 128, 512], F32),
        "l1_wv_a": p.dram_in("l1_wv_a", [NEC, NEG, 128, 512], F32),
        "l1_wo_b": p.dram_in("l1_wo_b", [NEC, NEC, 128, 128], F32),
        "mask_pp": p.dram_in("mask_pp", [128, NRANK], F32),
        "sel_pp": p.dram_in("sel_pp", [128, 2], F32),
        "f1_wg": p.dram_in("f1_wg", [NIC // 2, 2 * NEC, 128, 128], I8),
        "f1_wu": p.dram_in("f1_wu", [NIC // 2, 2 * NEC, 128, 128], I8),
        "f1_wd": p.dram_in("f1_wd", [NEC, NIC, 128, 128], I8),
        "f1_sg": p.dram_in("f1_sg", [128, NIC], F32),
        "f1_su": p.dram_in("f1_su", [128, NIC], F32),
        "f1_sd": p.dram_in("f1_sd", [128, NEC], F32),
    }
    o_y = p.dram_out("o_y", [TPC, D], F32)
    cc1i = [nc.dram_tensor(f"cc1i{t}", [2 * NRANK * 128, 512], BF16) for t in range(NTT)]
    cc1o = [nc.dram_tensor(f"cc1o{t}", [2 * NRANK * 128, 512], BF16) for t in range(NTT)]
    cc2_in = nc.dram_tensor("cc2_in", [128, NEC], F32)
    cc2_out = nc.dram_tensor("cc2_out", [NRANK * 128, NEC], F32)
    groups = [[0, 1, 2, 3], [4, 5, 6, 7]]

    with p.ctx:
        k = _consts(p)
        stream = p.sb("stream", [128, NEC * TPC], BF16)
        _emit_init(p, k, io)
        _emit_load_stream(p, k, io["x"], stream)
        with ExitStack() as sA:
            E = sA.enter_context
            q_tok = E(nc.sbuf_tensor("q_tok", [128, NTT * D], BF16))
            k_tok = E(nc.sbuf_tensor("k_tok", [128, NTT * D], BF16))
            v_tok = E(nc.sbuf_tensor("v_tok", [128, NTT * D], BF16))
            _emit_qkv(p, k, io, stream,
                      outs={"k": k_tok, "v": v_tok, "q": q_tok},
                      wnames={"k": "l0_wk_a", "v": "l0_wv_a", "q": "l0_wq_a"},
                      biases={"k": "l0_bk_r", "v": "l0_bv_r", "q": "l0_bq_r"},
                      rope={"q", "k"})
            attn_fm = E(nc.sbuf_tensor("attn_fm", [128, NHEAD * TPC], BF16))
            _emit_l0_attn(p, k, io, {"q_tok": q_tok, "k_tok": k_tok,
                                     "v_tok": v_tok, "attn_fm": attn_fm},
                          cc=(cc1i, cc1o, [list(range(2 * NRANK))]))
        with ExitStack() as sB:
            E = sB.enter_context
            a2abuf = E(nc.sbuf_tensor("a2abuf", [128, 2 * NEC * TPC], BF16))
            out_scrT = E(nc.sbuf_tensor("out_scrT", [128, NEC * TPC], BF16))
            sel_sb = E(nc.sbuf_tensor("sel_sb", [128, 2], F32))
            _emit_load_sb(p, [(sel_sb[:, :], io["sel_pp"])])
            _emit_scramble8(p, {}, a2abuf, out_scrT, sel_sb, cc1o=cc1o)
            sn = _ln_scratch(p, sB)
            _emit_proj_b(p, k, io, "l0_wo_b", out_scrT, stream, "resid_ln",
                         bias_pp="l0_bo_pp", sn=sn)
        with ExitStack() as sC:
            fuseT = sC.enter_context(nc.sbuf_tensor(p.uname("fuseT"), [128, NIC * 512], BF16))
            _emit_ffn(p, k, io, stream, "f0_wg", "f0_wu", "f0_wd",
                      "f0_sg", "f0_su", "f0_sd", fuseT)
        with ExitStack() as sD:
            E = sD.enter_context
            qT1 = E(nc.sbuf_tensor("qT1", [128, NEC * TPC], BF16))
            kvT = E(nc.sbuf_tensor("kvT", [128, NEC * TPC], BF16))
            blocksums = E(nc.sbuf_tensor("blocksums", [128, NEC], F32))
            with ExitStack() as sD1:
                k_tok2 = sD1.enter_context(nc.sbuf_tensor(p.uname("k_tok2"), [128, NTT * D], BF16))
                v_tok2 = sD1.enter_context(nc.sbuf_tensor(p.uname("v_tok2"), [128, NTT * D], BF16))
                _emit_proj_b(p, k, io, "l1_wq_b", stream, None, "copy", dst=qT1)
                _emit_qkv(p, k, io, stream,
                          outs={"k": k_tok2, "v": v_tok2},
                          wnames={"k": "l1_wk_a", "v": "l1_wv_a"},
                          biases={}, rope=set())
                _emit_kv_transpose(p, k, k_tok2, v_tok2, kvT)
            kvt_loc = E(nc.sbuf_tensor("kvt_loc", [128, NEC * TPC], BF16))
            _emit_scan(p, k, kvT, kvt_loc, blocksums)
            _emit_cc(p, "AllGather", groups, cc2_in.ap(), cc2_out.ap(),
                     [(cc2_in.ap(), blocksums[:, :])], "cc2")
            vhat = E(nc.sbuf_tensor("vhat", [128, NEC * TPC], BF16))
            _emit_l1b_vhat(p, k, {"bs_all": cc2_out.ap().rearrange("(r p) c -> r p c", p=128),
                                  "mask_pp": io["mask_pp"]}, qT1, kvt_loc, vhat)
            sn = _ln_scratch(p, sD)
            _emit_proj_b(p, k, io, "l1_wo_b", vhat, stream, "resid_ln", sn=sn)
        with ExitStack() as sF:
            fuseT2 = sF.enter_context(nc.sbuf_tensor(p.uname("fuseT"), [128, NIC * 512], BF16))
            xTf32 = sF.enter_context(nc.sbuf_tensor(p.uname("xTf32"), [128, NEC * 512], F32))
            _emit_ffn(p, k, io, stream, "f1_wg", "f1_wu", "f1_wd",
                      "f1_sg", "f1_su", "f1_sd", fuseT2, out_f32=xTf32, o_y=o_y)
    return p.nc


def _fused_inmap(sh, pc):
    m = {"ident": sh["ident"], "x": pc["x"], "mask_pp": pc["mask_pp"],
         "sel_pp": pc["sel_pp"],
         "ropecq": pc["ropecq"], "ropesq": pc["ropesq"],
         "ropeck": pc["ropeck"], "ropesk": pc["ropesk"]}
    for n in ("l0_wq_a", "l0_wk_a", "l0_wv_a", "l0_wo_b", "l0_bq_r", "l0_bk_r",
              "l0_bv_r", "l0_bo_pp", "f0_wg", "f0_wu", "f0_wd", "f0_sg", "f0_su",
              "f0_sd", "l1_wq_b", "l1_wk_a", "l1_wv_a", "l1_wo_b", "f1_wg",
              "f1_wu", "f1_wd", "f1_sg", "f1_su", "f1_sd"):
        m[n] = sh[n]
    return m


def kernel_fused(**inputs):
    sh, percore = _stage(inputs)
    cores = list(range(NCORE))
    res = run_bass_kernel_spmd(build_fused(), [_fused_inmap(sh, percore[c]) for c in cores],
                               core_ids=cores)
    out = np.zeros((2, SEQ, D), dtype=np.float32)
    for c in cores:
        b, sl = divmod(c, NRANK)
        out[b, sl * TPC:(sl + 1) * TPC, :] = np.asarray(res.results[c]["o_y"])
    return out


def kernel_unfused(**inputs):
    sh, percore = _stage(inputs)
    cores = list(range(NCORE))
    run = lambda nc, maps: run_bass_kernel_spmd(nc, maps, core_ids=cores)
    col = lambda res, key: [np.asarray(r[key]) for r in res.results]

    res = run(build_l0a(), [_l0a_inmap(sh, percore[c]) for c in cores])
    streams = col(res, "o_stream")
    attns = col(res, "o_attn")

    maps = []
    for c in cores:
        b, sl = divmod(c, NRANK)
        maps.append({"ident": sh["ident"], "a2a": _host_a2a(attns, b, sl),
                     "sel_pp": percore[c]["sel_pp"],
                     "stream_in": streams[c], "l0_wo_b": sh["l0_wo_b"],
                     "l0_bo_pp": sh["l0_bo_pp"]})
    streams = col(run(build_l0b(), maps), "o_stream")

    maps = [{"ident": sh["ident"], "stream_in": streams[c],
             "wg": sh["f0_wg"], "wu": sh["f0_wu"], "wd": sh["f0_wd"],
             "sg": sh["f0_sg"], "su": sh["f0_su"], "sd": sh["f0_sd"]} for c in cores]
    streams = col(run(build_ffn(0), maps), "o_stream")

    maps = [{"ident": sh["ident"], "stream_in": streams[c],
             "l1_wq_b": sh["l1_wq_b"], "l1_wk_a": sh["l1_wk_a"],
             "l1_wv_a": sh["l1_wv_a"]} for c in cores]
    res = run(build_l1a(), maps)
    qs, kvts, bss = col(res, "o_q"), col(res, "o_kvt"), col(res, "o_bs")

    maps = []
    for c in cores:
        b, sl = divmod(c, NRANK)
        bs_all = np.ascontiguousarray(np.stack([bss[b * NRANK + r] for r in range(NRANK)]))
        maps.append({"ident": sh["ident"], "stream_in": streams[c], "q_in": qs[c],
                     "kvt_in": kvts[c], "bs_all": bs_all,
                     "mask_pp": percore[c]["mask_pp"], "l1_wo_b": sh["l1_wo_b"]})
    streams = col(run(build_l1b(), maps), "o_stream")

    maps = [{"ident": sh["ident"], "stream_in": streams[c],
             "wg": sh["f1_wg"], "wu": sh["f1_wu"], "wd": sh["f1_wd"],
             "sg": sh["f1_sg"], "su": sh["f1_su"], "sd": sh["f1_sd"]} for c in cores]
    ys = col(run(build_ffn(1, final=True), maps), "o_y")

    out = np.zeros((2, SEQ, D), dtype=np.float32)
    for c in cores:
        b, sl = divmod(c, NRANK)
        out[b, sl * TPC:(sl + 1) * TPC, :] = ys[c]
    return out


def kernel(**inputs):
    return kernel_fused(**inputs)
